# version 17
# baseline (speedup 1.0000x reference)
import contextlib
import numpy as np
import ml_dtypes
import concourse.bass as bass
import concourse.mybir as mybir
from concourse.bass_utils import run_bass_kernel_spmd

F32 = mybir.dt.float32
BF16 = mybir.dt.bfloat16
FP16 = mybir.dt.float16
AF = mybir.ActivationFunctionType
ALU = mybir.AluOpType
AX = mybir.AxisListType

ENGINES = ("sync", "tensor", "vector", "scalar", "gpsimd")

NSEQ = 2
T = 2048
NTOK = NSEQ * T
D = 1024
DC = D // 128
INTER = 4096
EPS = 1e-5
MAGIC = float(np.float32(1.5 * 2 ** 23))


class K:
    def __init__(self):
        self.nc = bass.Bass("TRN2", target_bir_lowering=False)
        self.prog = {e: [] for e in ENGINES}
        self.cnt = {}
        self.last_wait = {}
        self.sem_names = ["T", "V", "S", "G"]
        self.dma_sems = set()
        self._dirty = {e: [] for e in ENGINES}
        self._dirty_r = {e: [] for e in ENGINES}
        self.phase_bounds = []
        self.es = contextlib.ExitStack()
        self._consts = {}

    def phase_mark(self, name):
        self.phase_bounds.append((name, {e: sum(1 for it in self.prog[e] if it[0] == "op")
                                         for e in ENGINES}))

    _WRITE_KEYS = ("out", "ap", "accum_out", "out_max", "out_indices")

    @staticmethod
    def _ap_name(v):
        t = getattr(v, "tensor", None)
        return getattr(t, "name", None) if t is not None else None

    NODRAIN = ("junk", "qjunk", "ojnk")

    def _is_scratch(self, nm):
        return any(s in nm for s in self.NODRAIN)

    @staticmethod
    def _fpr(v):
        t = getattr(v, "tensor", None)
        nm = getattr(t, "name", None) if t is not None else None
        if nm is None:
            return None
        try:
            dims = v.ap
            bp = v.base_partition()
            np_ = dims[0][1] if dims else 1
            pstride = dims[0][0] if dims else 0
            lo = v.offset - bp * pstride
            hi = lo + 1
            for st, sz in dims[1:]:
                d = (sz - 1) * st
                if d >= 0:
                    hi += d
                else:
                    lo += d
            return (nm, bp, bp + np_, lo, hi)
        except Exception:
            return (nm, 0, 1 << 20, 0, 1 << 40)

    @staticmethod
    def _ovl(a, b):
        return (a[0] == b[0] and a[1] < b[2] and b[1] < a[2]
                and a[3] < b[4] and b[3] < a[4])

    def emit(self, eng, meth, **kw):
        if meth == "drain":
            self._dirty[eng] = []
            self._dirty_r[eng] = []
            self.prog[eng].append(("op", meth, kw))
            return
        if meth != "dma_start" and eng in ("vector", "scalar", "gpsimd"):
            writes = []
            reads = []
            for key, v in kw.items():
                fp = self._fpr(v)
                if fp is None or (key in self._WRITE_KEYS and self._is_scratch(fp[0])):
                    continue
                (writes if key in self._WRITE_KEYS else reads).append(fp)
            pw, pr = self._dirty[eng], self._dirty_r[eng]
            hazard = any(self._ovl(x, p) for x in writes + reads for p in pw) or \
                     any(self._ovl(w, p) for w in writes for p in pr)
            if hazard:
                self.prog[eng].append(("op", "drain", {}))
                self._dirty[eng] = pw = []
                self._dirty_r[eng] = pr = []
            pw.extend(writes)
            pr.extend(reads)
        self.prog[eng].append(("op", meth, kw))

    def wait(self, eng, tok):
        if tok is None:
            return
        sem, val = tok
        if val <= 0:
            return
        key = (eng, sem)
        if self.last_wait.get(key, -1) >= val:
            return
        self.last_wait[key] = val
        self.prog[eng].append(("wait", sem, val))

    def mark(self, eng, sem, inc=1):
        assert self.prog[eng] and self.prog[eng][-1][0] == "op", f"mark needs op on {eng}"
        self.cnt[sem] = self.cnt.get(sem, 0) + inc
        self.prog[eng].append(("inc", sem, inc))
        return (sem, self.cnt[sem])

    def tok(self, sem):
        return (sem, self.cnt.get(sem, 0))

    def set_warm(self, psum_ap, opnd_ap=None):
        self._warm = None if psum_ap is None else (psum_ap, opnd_ap)

    def warm(self):
        w = getattr(self, "_warm", None)
        if w is None:
            return
        self.emit("tensor", "matmul", out=w[0], lhsT=w[1], rhs=w[1],
                  start=True, stop=True)

    def stage_join(self):
        for e in ("sync", "gpsimd"):
            for sem in ("T", "V", "S"):
                self.wait(e, self.tok(sem))
        for e, own in (("vector", "V"), ("scalar", "S"), ("tensor", "T")):
            for sem in ("T", "V", "S"):
                if sem != own:
                    self.wait(e, self.tok(sem))
        for e in ("sync", "gpsimd", "vector", "scalar", "tensor"):
            for s in sorted(self.dma_sems):
                self.wait(e, self.tok(s))

    def dma(self, eng, out, in_, buf, **kw):
        sem = "d_" + buf
        self.dma_sems.add(sem)
        self.emit(eng, "dma_start", out=out, in_=in_, **kw)
        return self.mark(eng, sem, 16)

    def sb(self, name, shape, dtype):
        return self.es.enter_context(self.nc.sbuf_tensor(name, shape, dtype))

    def ps(self, name, shape, dtype=F32):
        return self.es.enter_context(self.nc.psum_tensor(name, shape, dtype))

    def const(self, name, arr):
        if name not in self._consts:
            self._consts[name] = self.nc.inline_tensor(np.ascontiguousarray(arr), name)
        return self._consts[name]

    def const_sb(self, name, arr):
        key = "sb_" + name
        if key not in self._consts:
            dram = self.nc.inline_tensor(np.ascontiguousarray(arr), name)
            dtm = {np.dtype(ml_dtypes.bfloat16): BF16, np.dtype(np.float32): F32,
                   np.dtype(np.float16): FP16}
            tile = self.sb(key, list(arr.shape), dtm[np.dtype(arr.dtype)])
            tok = self.dma("sync", out=tile[:], in_=dram.ap(), buf="c_" + name)
            self._consts[key] = (tile, tok)
        return self._consts[key]

    def build_block(self):
        for s in sorted(self.dma_sems):
            self.wait("sync", (s, self.cnt.get(s, 0)))
        names = self.sem_names + sorted(self.dma_sems)
        sems = {n: self.es.enter_context(self.nc.semaphore(n)) for n in names}

        def runner(E, stream):
            last = None
            for item in stream:
                kind = item[0]
                if kind == "op":
                    last = getattr(E, item[1])(**item[2])
                elif kind == "wait":
                    E.wait_ge(sems[item[1]], item[2])
                else:
                    last.then_inc(sems[item[1]], item[2])

        block = self.es.enter_context(self.nc.Block())
        for e in ENGINES:
            getattr(block, e)(lambda E, s=tuple(self.prog[e]): runner(E, s))


def ident_bf16(k):
    return k.const_sb("identity128", np.eye(128, dtype=ml_dtypes.bfloat16))


def ones_col128(k):
    return k.const_sb("ones_col128", np.ones((1, 128), dtype=ml_dtypes.bfloat16))


class LnState:
    def __init__(self, k, name, G, alloc=None, junk_from=None):
        self.k = k
        self.G = G
        alloc = alloc or k.sb
        self.ssq = alloc(name + "_ssq", [128, G], F32)
        self.sm = alloc(name + "_sm", [128, G], F32)
        self.rs = alloc(name + "_rs", [128, G], F32)
        self.nmr = alloc(name + "_nmr", [128, G], F32)
        self.tmp = alloc(name + "_tmp", [128, G], F32)
        self.junk = junk_from.junk if junk_from is not None else alloc(
            name + "_junk", [128, 1024], F32)
        if junk_from is not None:
            self.last_statS = getattr(junk_from, "last_statS", None)
        self.fin_tok = None

    def stats_tile(self, x_ap, g, n=D):
        k = self.k
        k.wait("scalar", self.fin_tok)
        k.wait("scalar", getattr(self, "last_statS", None))
        k.emit("scalar", "activation", out=self.junk[:, 0:n], in_=x_ap,
               func=AF.Square, accum_out=self.ssq[:, g:g + 1])
        tokS = k.mark("scalar", "S")
        self.last_statS = tokS
        k.emit("vector", "reduce_sum", out=self.sm[:, g:g + 1], in_=x_ap, axis=AX.X)
        tokV = k.mark("vector", "V")
        return tokS, tokV

    def finalize(self, n=D, eps=EPS, mean_zero=False):
        k = self.k
        c = 1.0 / n

        def dr():
            k.emit("vector", "drain")

        k.wait("vector", getattr(self, "last_statS", None))
        dr()
        if mean_zero:
            k.emit("vector", "tensor_scalar", out=self.nmr[:], in0=self.ssq[:],
                   scalar1=c, scalar2=eps, op0=ALU.mult, op1=ALU.add)
        else:
            k.emit("vector", "tensor_scalar", out=self.tmp[:], in0=self.sm[:],
                   scalar1=c, scalar2=None, op0=ALU.mult)
            dr()
            k.emit("vector", "tensor_mul", out=self.rs[:], in0=self.tmp[:], in1=self.tmp[:])
            dr()
            k.emit("vector", "scalar_tensor_tensor", out=self.rs[:], in0=self.ssq[:],
                   scalar=c, in1=self.rs[:], op0=ALU.mult, op1=ALU.subtract)
            dr()
            k.emit("vector", "tensor_scalar", out=self.nmr[:], in0=self.rs[:],
                   scalar1=eps, scalar2=None, op0=ALU.add)
        tokV = k.mark("vector", "V")
        k.wait("scalar", tokV)
        k.emit("scalar", "activation", out=self.rs[:], in_=self.nmr[:], func=AF.Sqrt)
        tokS = k.mark("scalar", "S")
        k.wait("vector", tokS)
        k.emit("vector", "reciprocal", out=self.rs[:], in_=self.rs[:])
        dr()
        k.emit("vector", "tensor_mul", out=self.ssq[:], in0=self.rs[:], in1=self.rs[:])
        dr()
        k.emit("vector", "tensor_mul", out=self.ssq[:], in0=self.ssq[:], in1=self.nmr[:])
        dr()
        k.emit("vector", "tensor_scalar", out=self.ssq[:], in0=self.ssq[:],
               scalar1=-0.5, scalar2=1.5, op0=ALU.mult, op1=ALU.add)
        dr()
        k.emit("vector", "tensor_mul", out=self.rs[:], in0=self.rs[:], in1=self.ssq[:])
        dr()
        if mean_zero:
            k.emit("vector", "memset", ap=self.nmr[:], constant=0.0)
        else:
            k.emit("vector", "scalar_tensor_tensor", out=self.nmr[:], in0=self.tmp[:],
                   scalar=-1.0, in1=self.rs[:], op0=ALU.mult, op1=ALU.mult)
        dr()
        self.fin_tok = k.mark("vector", "V")
        return self.fin_tok

    def apply(self, out_ap, x_ap, g, eng="scalar"):
        k = self.k
        if eng == "scalar":
            k.emit("scalar", "activation", out=out_ap, in_=x_ap, func=AF.Identity,
                   scale=self.rs[:, g:g + 1], bias=self.nmr[:, g:g + 1])
            return k.mark("scalar", "S")
        k.emit("vector", "tensor_scalar", out=out_ap, in0=x_ap,
               scalar1=self.rs[:, g:g + 1], scalar2=self.nmr[:, g:g + 1],
               op0=ALU.mult, op1=ALU.add)
        return k.mark("vector", "V")


def tm_to_fm(k, dst_ap, src_ap_fn, nchunks, psum, tok_src, evac_eng="vector", dt16=None):
    if dt16 is FP16:
        ident, itok = k.const_sb("identity128_f16", np.eye(128, dtype=np.float16))
    else:
        ident, itok = ident_bf16(k)
    k.wait("tensor", itok)
    k.wait("tensor", tok_src)
    for c in range(nchunks):
        k.emit("tensor", "transpose", out=psum[:, c * 128:(c + 1) * 128],
               in_=src_ap_fn(c), identity=ident[:])
    tokT = k.mark("tensor", "T")
    k.wait(evac_eng, tokT)
    k.emit(evac_eng, "tensor_copy", out=dst_ap,
           in_=psum[:, 0:nchunks * 128].rearrange("p (c t) -> p c t", t=128))
    return k.mark(evac_eng, "V" if evac_eng == "vector" else "S")


def phase_p3(k, src, dst, W):
    nc = k.nc
    TT = 512
    NT = NTOK // TT
    with contextlib.ExitStack() as ph:
        sb = lambda n, s, d: ph.enter_context(nc.sbuf_tensor(n, s, d))
        ps = lambda n, s, d=F32: ph.enter_context(nc.psum_tensor(n, s, d))
        Wsb = [sb(f"p3_w{i}", [128, DC, D], BF16) for i in range(3)]
        bias_sb = sb("p3_bias", [1, D], BF16)
        xa = [sb(f"p3_x{i}", [128, 4, D], F32) for i in range(2)]
        ubf = sb("p3_ubf", [128, 4, D], BF16)
        uT = sb("p3_uT", [128, DC, TT], BF16)
        t1 = sb("p3_t1", [128, DC, TT], BF16)
        t2 = sb("p3_t2", [128, DC, TT], BF16)
        z2 = sb("p3_z2", [128, 4, D], F32)
        outb = [sb(f"p3_out{i}", [128, 4, D], F32) for i in range(2)]
        ptr = ps("p3_ptr", [128, 1024], BF16)
        pmm = [ps(f"p3_pm{i}", [128, 512], F32) for i in range(2)]
        pt3 = [ps(f"p3_pt3_{i}", [128, 1024], F32) for i in range(2)]
        pwm3 = ph.enter_context(nc.psum_tensor("p3_pwm", [1, 1], F32))
        och3, ochtok3 = k.const_sb("ones_col128_f16", np.ones((128, 1), dtype=np.float16))
        k.wait("tensor", ochtok3)
        k.set_warm(pwm3[:], och3[:])
        ln_u = LnState(k, "p3lnu", 4, alloc=sb)
        ln_z = LnState(k, "p3lnz", 4, alloc=sb)

        for i, nm in enumerate(("weight", "aux_weight1", "aux_weight2")):
            for kc in range(DC):
                k.dma("gpsimd", out=Wsb[i][:, kc, :], in_=W[nm][kc * 128:(kc + 1) * 128, :], buf="p3w")
        k.dma("gpsimd", out=bias_sb[:], in_=W["bias"].rearrange("(o d) -> o d", o=1), buf="p3w")
        tok_w = k.tok("d_p3w")

        store_tok = {0: None, 1: None}
        z2_free = None
        HS = {"ubf_free": None, "apm": None, "x": None, "atoks": None}
        pmm_ev = [None, None]
        pt3_ev = [None, None]

        def emit_head(t):
            buf = t % 2
            x = xa[buf]
            k.wait("sync", store_tok[buf])
            ld = k.dma("sync", out=x[:, :, :],
                       in_=src[t * TT:(t + 1) * TT, :].rearrange("(a p) d -> p a d", p=128),
                       buf=f"p3x{buf}")
            k.wait("scalar", ld)
            k.wait("vector", ld)
            k.warm()
            for st in range(4):
                ln_u.stats_tile(x[:, st, :], st)
            k.wait("vector", HS["apm"])
            fin = ln_u.finalize()
            k.wait("scalar", fin)
            k.wait("scalar", HS["ubf_free"])
            atoks = [ln_u.apply(ubf[:, st, :], x[:, st, :], st) for st in range(4)]
            HS["apm"] = atoks[-1]
            HS["x"] = x
            HS["atoks"] = atoks

        emit_head(0)
        for t in range(NT):
            buf = t % 2
            x = HS["x"]
            atoks = HS["atoks"]
            ob = outb[buf]
            tr_ev = None
            for st in range(4):
                tr_ev = tm_to_fm(
                    k, uT[:, :, st * 128:(st + 1) * 128],
                    lambda c, st=st: ubf[:, st, c * 128:(c + 1) * 128],
                    DC, ptr, atoks[st], evac_eng="vector")
                k.wait("tensor", tr_ev)
            HS["ubf_free"] = tr_ev
            k.wait("tensor", tok_w)
            src_fm = uT
            prev_ev = tr_ev
            for li, dst_fm in ((0, t1), (1, t2)):
                k.wait("tensor", prev_ev)
                for nch in range(DC):
                    pb = pmm[nch % 2]
                    k.wait("tensor", pmm_ev[nch % 2])
                    for kc in range(DC):
                        k.emit("tensor", "matmul", out=pb[:],
                               lhsT=Wsb[li][:, kc, nch * 128:(nch + 1) * 128],
                               rhs=src_fm[:, kc, :],
                               start=(kc == 0), stop=(kc == DC - 1))
                    tokm = k.mark("tensor", "T")
                    k.wait("vector", tokm)
                    k.emit("vector", "tensor_copy", out=dst_fm[:, nch, :], in_=pb[:])
                    pmm_ev[nch % 2] = k.mark("vector", "V")
                prev_ev = pmm_ev[(DC - 1) % 2]
                src_fm = t1
            if t + 1 < NT:
                emit_head(t + 1)
            for st in range(4):
                pb = pt3[st % 2]
                k.wait("tensor", prev_ev)
                k.wait("tensor", pt3_ev[st % 2])
                for kc in range(DC):
                    for hh in range(2):
                        k.emit("tensor", "matmul", out=pb[:, hh * 512:(hh + 1) * 512],
                               lhsT=t2[:, kc, st * 128:(st + 1) * 128],
                               rhs=Wsb[2][:, kc, hh * 512:(hh + 1) * 512],
                               start=(kc == 0), stop=False)
                onescol, octok = ones_col128(k)
                k.wait("tensor", octok)
                for hh in range(2):
                    k.emit("tensor", "matmul", out=pb[:, hh * 512:(hh + 1) * 512],
                           lhsT=onescol[:],
                           rhs=bias_sb[0:1, hh * 512:(hh + 1) * 512],
                           start=False, stop=True)
                tokm = k.mark("tensor", "T")
                k.wait("scalar", tokm)
                k.wait("scalar", z2_free)
                k.emit("scalar", "activation", out=z2[:, st, :], in_=pb[:], func=AF.Tanh)
                tokz = k.mark("scalar", "S")
                pt3_ev[st % 2] = tokz
                k.wait("vector", tokz)
                ln_z.stats_tile(z2[:, st, :], st)
            fin = ln_z.finalize()
            k.wait("scalar", fin)
            k.wait("vector", store_tok[buf])
            for st in range(4):
                ap_tok = ln_z.apply(z2[:, st, :], z2[:, st, :], st)
                k.wait("vector", ap_tok)
                k.emit("vector", "tensor_add", out=ob[:, st, :], in0=z2[:, st, :], in1=x[:, st, :])
            z2_free = k.mark("vector", "V")
            k.wait("sync", z2_free)
            store_tok[buf] = k.dma(
                "sync", out=dst[t * TT:(t + 1) * TT, :].rearrange("(a p) d -> p a d", p=128),
                in_=ob[:, :, :], buf=f"p3o{buf}")
        k.set_warm(None)
        k.wait("sync", k.tok("d_p3o0"))
        k.wait("sync", k.tok("d_p3o1"))


def bcast_scalar(k, dst_sb, src_11, psum1, ident1_f32):
    onesf, otok = k.const_sb("ones_f32_row", np.ones((1, 128), dtype=np.float32))
    k.wait("tensor", otok)
    k.emit("tensor", "matmul", out=psum1[:], lhsT=onesf[:], rhs=src_11, start=True, stop=True)
    tk = k.mark("tensor", "T")
    k.wait("vector", tk)
    k.emit("vector", "tensor_copy", out=dst_sb, in_=psum1[:])
    return k.mark("vector", "V")


def part_sum_col(k, col_ap, out11_psum, guard=None):
    onesf, otok = k.const_sb("ones_col128_f32", np.ones((128, 1), dtype=np.float32))
    k.wait("tensor", otok)
    k.wait("tensor", guard)
    k.emit("tensor", "matmul", out=out11_psum, lhsT=onesf[:], rhs=col_ap,
           start=True, stop=True)
    return k.mark("tensor", "T")


def phase_p1(k, src, dst, W, scratch):
    nc = k.nc
    TT = 512
    NT = NTOK // TT
    T2 = 256
    N2 = NTOK // T2
    HC = INTER // 128
    B256 = [b for b in range(1, 16) if b % 8 != 0]
    NE = 2 * len(B256)
    with contextlib.ExitStack() as ph:
        sb = lambda n, s, d: ph.enter_context(nc.sbuf_tensor(n, s, d))
        ps = lambda n, s, d=F32: ph.enter_context(nc.psum_tensor(n, s, d))
        w1qT = sb("p1_w1qT", [128, DC, INTER], FP16)
        w2qT = sb("p1_w2qT", [128, HC, D], FP16)
        cw = sb("p1_cw", [128, HC, 3], F32)
        cb = sb("p1_cb", [128, HC], F32)
        sc = sb("p1_sc", [128, 8], F32)
        tiny = sb("p1_tiny", [1, 8], F32)
        uTedge = sb("p1_uTedge", [128, DC, NE], FP16)
        h_edge = sb("p1_hedge", [128, HC, NE], FP16)
        ptr = ps("p1_ptr", [128, 1024], FP16)
        pwm = ps("p1_pwm", [1, 1])
        ident1f, id1tok = k.const_sb("ident1_f32", np.ones((1, 1), dtype=np.float32))
        och, ochtok = k.const_sb("ones_col128_f16", np.ones((128, 1), dtype=np.float16))
        k.wait("tensor", ochtok)
        k.set_warm(pwm[:], och[:])
        bc_guard = [None]

        def bcast(dst_sb, src_11):
            onesf, otok = k.const_sb("ones_f32_row", np.ones((1, 128), dtype=np.float32))
            k.wait("tensor", otok)
            k.wait("tensor", bc_guard[0])
            k.emit("tensor", "matmul", out=ps1[:, 0:1], lhsT=onesf[:], rhs=src_11,
                   start=True, stop=True)
            tk = k.mark("tensor", "T")
            k.wait("vector", tk)
            k.emit("vector", "tensor_copy", out=dst_sb, in_=ps1[:, 0:1])
            bc_guard[0] = k.mark("vector", "V")
            return bc_guard[0]

        with contextlib.ExitStack() as wp:
            sbw = lambda n, s, d: wp.enter_context(nc.sbuf_tensor(n, s, d))
            psw = lambda n, s, d=F32: wp.enter_context(nc.psum_tensor(n, s, d))
            ps1 = psw("p1_ps1", [128, 4])
            pss = psw("p1_pss", [1, 4])
            wbuf = [sbw(f"p1_wbuf{i}", [128, 4, 1024], F32) for i in range(2)]
            qsc = wp.enter_context(contextlib.ExitStack())
            wq = [qsc.enter_context(nc.sbuf_tensor(f"p1_wq{i}", [128, 4096], FP16))
                  for i in range(2)]
            wqf = qsc.enter_context(nc.sbuf_tensor("p1_wqf", [128, 4096], F32))
            acc8 = qsc.enter_context(nc.sbuf_tensor("p1_acc8", [128, 9], F32))

            def wstats(wdram, nrows, ncols, col_out, gate, gate2=None):
                rpg = 4096 * 128 // ncols
                ngrp = nrows // rpg
                toks = [gate, gate]
                k.wait("scalar", gate2)
                prevS = None
                for g in range(ngrp):
                    b = g % 2
                    k.wait("sync", toks[b])
                    seg = wdram[g * rpg:(g + 1) * rpg, :]
                    if ncols == 1024:
                        seg = seg.rearrange("(a p) d -> p a d", p=128)
                    else:
                        seg = seg.rearrange("p (a d) -> p a d", d=1024)
                    ld = k.dma("sync", out=wbuf[b][:, :, :], in_=seg, buf=f"p1wb{b}")
                    k.wait("vector", ld)
                    k.warm()
                    k.emit("vector", "reduce_sum", out=acc8[:, g:g + 1],
                           in_=wbuf[b][:, :, :], axis=AX.XY, apply_absolute_value=True)
                    toks[b] = k.mark("vector", "V")
                    prevS = toks[b]
                k.emit("vector", "drain")
                k.emit("vector", "reduce_sum", out=acc8[:, 8:9], in_=acc8[:, 0:ngrp], axis=AX.X)
                vt = k.mark("vector", "V")
                ttk = part_sum_col(k, acc8[:, 8:9], pss[0:1, 0:1], guard=vt)
                k.wait("vector", ttk)
                k.emit("vector", "tensor_copy", out=tiny[0:1, col_out:col_out + 1],
                       in_=pss[0:1, 0:1])
                return k.mark("vector", "V")

            def wscales(nelem, col_in, cols, gate):
                k.wait("vector", gate)
                k.emit("vector", "tensor_scalar", out=tiny[0:1, col_in:col_in + 1],
                       in0=tiny[0:1, col_in:col_in + 1], scalar1=1.0 / nelem,
                       scalar2=1e-5, op0=ALU.mult, op1=ALU.max)
                k.emit("vector", "drain")
                k.emit("vector", "reciprocal", out=tiny[0:1, 7:8], in_=tiny[0:1, col_in:col_in + 1])
                k.emit("vector", "drain")
                vt = k.mark("vector", "V")
                k.wait("tensor", vt)
                bcast(sc[:, cols[0]:cols[0] + 1], tiny[0:1, col_in:col_in + 1])
                return bcast(sc[:, cols[1]:cols[1] + 1], tiny[0:1, 7:8])

            def wquant(wdram, nrows, ncols, sw_col, tr_groups, gate, sc_tok):
                nchunk = nrows // 128
                toks = [gate, gate]
                k.wait("scalar", sc_tok)
                evtok = None
                vtoks = {}
                tts = {}
                for r in range(nchunk):
                    b = r % 2
                    k.wait("sync", toks[b])
                    ld = k.dma("sync", out=wbuf[b][:, 0:ncols // 1024, :],
                               in_=wdram[r * 128:(r + 1) * 128, :].rearrange(
                                   "p (a d) -> p a d", d=1024),
                               buf=f"p1wb{b}")
                    k.wait("scalar", ld)
                    k.wait("scalar", vtoks.get(r - 1))
                    k.emit("scalar", "activation", out=wqf[:, 0:ncols],
                           in_=wbuf[b][:, 0:ncols // 1024, :],
                           func=AF.Identity, scale=sc[:, sw_col:sw_col + 1], bias=np.float32(MAGIC))
                    stok = k.mark("scalar", "S")
                    toks[b] = stok
                    k.wait("vector", stok)
                    k.wait("vector", tts.get(r - 2))
                    k.emit("vector", "tensor_scalar", out=wqf[:, 0:ncols], in0=wqf[:, 0:ncols],
                           scalar1=-MAGIC, scalar2=1.0, op0=ALU.add, op1=ALU.min)
                    k.emit("vector", "tensor_scalar", out=wq[b][:, 0:ncols], in0=wqf[:, 0:ncols],
                           scalar1=-1.0, scalar2=None, op0=ALU.max)
                    vtoks[r] = k.mark("vector", "V")
                    for g, dst_ap in tr_groups(r):
                        evtok = tm_to_fm(k, dst_ap,
                                         lambda c, g=g, b=b: wq[b][:, (g * 8 + c) * 128:(g * 8 + c + 1) * 128],
                                         8, ptr, vtoks[r], evac_eng="vector", dt16=FP16)
                        k.wait("tensor", evtok)
                    tts[r] = k.tok("T")
                return evtok, toks, vtoks[nchunk - 1]

            k.phase_mark("p1_wq")
            s1 = wstats(W["w1"], INTER, 1024, 0, None)
            sct1 = wscales(INTER * D, 0, (0, 1), s1)

            def w1_groups(r):
                yield 0, w1qT[:, :, r * 128:(r + 1) * 128]
            w1ev, w1casts, w1clamp = wquant(W["w1"], INTER, 1024, 1, w1_groups, s1, sct1)

            s2 = wstats(W["w2"], D, 4096, 1,
                        max((t for t in w1casts if t), key=lambda t: t[1]),
                        gate2=w1clamp)
            sct2 = wscales(INTER * D, 1, (2, 3), s2)

            def w2_groups(r):
                for g in range(4):
                    yield g, w2qT[:, g * 8:(g + 1) * 8, r * 128:(r + 1) * 128]
            w2ev, _, _ = wquant(W["w2"], D, 4096, 3, w2_groups, s2, sct2)

            qsc.close()
            k.dma("sync", out=cw[:, :, :], in_=W["conv_w"].rearrange("(c p) t -> p c t", p=128), buf="p1cw")
            k.dma("sync", out=cb[:, :], in_=W["conv_b"].rearrange("(c p) -> p c", p=128), buf="p1cw", allow_slow_non_contiguous=True)
            tok_cw = k.tok("d_p1cw")
            k.wait("scalar", tok_cw)
            k.wait("scalar", sct1)
            k.emit("scalar", "activation",
                   out=cw[:, :, 1:2].rearrange("p c o -> p (c o)"),
                   in_=cw[:, :, 1:2].rearrange("p c o -> p (c o)"),
                   func=AF.Identity, scale=sc[:, 0:1])
            tok_cw = k.mark("scalar", "S")

            k.phase_mark("p1_l1")
            qjunk = sbw("p1_qjunk", [128, D], F32)
            uqd = sbw("p1_uqd", [128, 4, D], FP16)
            mx = sbw("p1_mx", [128, 4], F32)
            mn = sbw("p1_mn", [128, 4], F32)
            rsu = sbw("p1_rsu", [128, 4], F32)
            c1 = sbw("p1_c1", [128, 4], F32)
            c0 = sbw("p1_c0", [128, 4], F32)
            ln_u = LnState(k, "p1lnu", 4, alloc=sbw)

            x_free = {0: w2ev, 1: w2ev}
            uqd_store = None
            for t in range(NT):
                b = t % 2
                x = wbuf[b]
                k.wait("sync", x_free[b])
                ld = k.dma("sync", out=x[:, :, :],
                           in_=src[t * TT:(t + 1) * TT, :].rearrange("(a p) d -> p a d", p=128),
                           buf=f"p1wb{b}")
                k.wait("scalar", ld)
                k.wait("vector", ld)
                k.warm()
                for st in range(4):
                    ln_u.stats_tile(x[:, st, :], st)
                    k.emit("vector", "reduce_max", out=mx[:, st:st + 1], in_=x[:, st, :],
                           axis=AX.X)
                    k.emit("vector", "tensor_reduce", out=mn[:, st:st + 1], in_=x[:, st, :],
                           axis=AX.X, op=ALU.min)
                    k.mark("vector", "V")
                fin = ln_u.finalize()
                k.emit("vector", "drain")
                k.emit("vector", "scalar_tensor_tensor", out=mx[:], in0=mx[:],
                       scalar=1.0, in1=ln_u.rs[:], op0=ALU.mult, op1=ALU.mult)
                k.emit("vector", "scalar_tensor_tensor", out=mn[:], in0=mn[:],
                       scalar=1.0, in1=ln_u.rs[:], op0=ALU.mult, op1=ALU.mult)
                k.emit("vector", "drain")
                k.emit("vector", "tensor_add", out=mx[:], in0=mx[:], in1=ln_u.nmr[:])
                k.emit("vector", "tensor_add", out=mn[:], in0=mn[:], in1=ln_u.nmr[:])
                k.emit("vector", "drain")
                k.emit("vector", "scalar_tensor_tensor", out=mx[:], in0=mn[:],
                       scalar=-1.0, in1=mx[:], op0=ALU.mult, op1=ALU.max)
                k.emit("vector", "drain")
                k.emit("vector", "tensor_scalar", out=rsu[:], in0=mx[:],
                       scalar1=1e-5, scalar2=1.0 / 127.0, op0=ALU.max, op1=ALU.mult)
                k.emit("vector", "drain")
                k.emit("vector", "reciprocal", out=mn[:], in_=rsu[:])
                k.emit("vector", "drain")
                k.emit("vector", "tensor_mul", out=c1[:], in0=ln_u.rs[:], in1=mn[:])
                k.emit("vector", "tensor_mul", out=c0[:], in0=ln_u.nmr[:], in1=mn[:])
                k.emit("vector", "drain")
                k.emit("vector", "tensor_scalar", out=c0[:], in0=c0[:],
                       scalar1=MAGIC, scalar2=None, op0=ALU.add)
                k.emit("vector", "drain")
                qtok = k.mark("vector", "V")
                k.wait("scalar", qtok)
                k.wait("vector", uqd_store)
                k.wait("vector", k.tok("d_p1ue"))
                k.warm()
                for st in range(4):
                    k.emit("scalar", "activation", out=qjunk[:], in_=x[:, st, :],
                           func=AF.Identity, scale=c1[:, st:st + 1], bias=c0[:, st:st + 1])
                    stk = k.mark("scalar", "S")
                    k.wait("vector", stk)
                    k.emit("vector", "tensor_scalar", out=uqd[:, st, :], in0=qjunk[:],
                           scalar1=-MAGIC, scalar2=rsu[:, st:st + 1], op0=ALU.add, op1=ALU.mult)
                    vst = k.mark("vector", "V")
                    k.wait("scalar", vst)
                vtok = k.tok("V")
                k.wait("sync", vtok)
                uqd_store = k.dma("sync",
                                  out=scratch[t * TT:(t + 1) * TT, :].rearrange("(a p) d -> p a d", p=128),
                                  in_=uqd[:, :, :], buf="p1uq")
                x_free[b] = uqd_store
                for i, bnd in enumerate(B256):
                    for tk_, slot in ((256 * bnd - 1, 2 * i), (256 * bnd, 2 * i + 1)):
                        if t * TT <= tk_ < (t + 1) * TT:
                            k.wait("gpsimd", uqd_store)
                            k.dma("gpsimd",
                                  out=uTedge[:, :, slot:slot + 1].rearrange("p c o -> p (c o)"),
                                  in_=scratch[tk_:tk_ + 1, :].rearrange("o (c p) -> (o p) c", p=128),
                                  buf="p1ue", allow_slow_non_contiguous=True)
            tok_uqd_all = k.tok("d_p1uq")
            tok_edge = k.tok("d_p1ue")

            pse = psw("p1_pse", [128, NE])
            k.wait("tensor", tok_edge)
            k.wait("tensor", w1ev)
            pe_ev = None
            for ic in range(HC):
                k.wait("tensor", pe_ev)
                for dc in range(DC):
                    k.emit("tensor", "matmul", out=pse[:],
                           lhsT=w1qT[:, dc, ic * 128:(ic + 1) * 128],
                           rhs=uTedge[:, dc, :],
                           start=(dc == 0), stop=(dc == DC - 1))
                mtk = k.mark("tensor", "T")
                k.wait("scalar", mtk)
                k.wait("scalar", sct1)
                k.emit("scalar", "activation", out=h_edge[:, ic, :], in_=pse[:],
                       func=AF.Identity, scale=sc[:, 0:1])
                pe_ev = k.mark("scalar", "S")
            h_edge_tok = pe_ev
        k.stage_join()

        k.phase_mark("p1_l2")
        uqdl = [sb(f"p1_uqdl{i}", [128, 2, D], FP16) for i in range(2)]
        x2 = sb("p1_x2", [128, 2, D], F32)
        uqdT = sb("p1_uqdT", [128, DC, T2], FP16)
        ht4 = sb("p1_ht4", [128, 4, T2], FP16)
        cvt4 = sb("p1_cvt4", [128, 4, T2], FP16)
        h2 = [sb(f"p1_h2{i}", [128, HC, T2], FP16) for i in range(2)]
        am4 = [sb(f"p1_am4{i}", [128, 4, T2], FP16) for i in range(2)]
        am2b = sb("p1_am2b", [128, T2], FP16)
        am2 = sb("p1_am2", [128, T2], F32)
        su2tm = sb("p1_su2tm", [128, 2], F32)
        su2bf = sb("p1_su2bf", [1, T2], F32)
        su2b = sb("p1_su2b", [128, T2], F32)
        uq2m3 = sb("p1_uq2m3", [128, 3, T2], F32)
        rsu2T = sb("p1_rsu2T", [128, 2], F32)
        rsud = sb("p1_rsud", [128, 2], F32)
        z = sb("p1_z", [128, 2, D], BF16)
        ln_z = LnState(k, "p1lnz", 2, alloc=sb)
        psh = [ps(f"p1_psh{i}", [128, T2]) for i in range(3)]
        psd = ps("p1_psd", [128, 1024])
        pst = ps("p1_pst", [128, T2])
        onesf32, onesf32tok = k.const_sb("ones_f32_row", np.ones((1, 128), dtype=np.float32))

        identf, iftok = k.const_sb("identity128_f32", np.eye(128, dtype=np.float32))
        k.wait("tensor", iftok)
        k.wait("vector", h_edge_tok)
        k.wait("sync", tok_uqd_all)
        ld_free = {0: None, 1: None}
        G = {"out_free": None, "x2_free": None, "pst_free": None, "z_free": None,
             "psd_ev": None, "psb_ev": None, "last_rne": None, "gen": None}
        h2_free_p = {0: None, 1: None}
        am_free = {0: None, 1: None}
        Am = {}
        B2m = {}
        Cm = {}
        mtkd = {}
        trtoks = {}

        def back_half(t):
            b = t % 2
            h2t = h2[b]
            am4t = am4[b]
            k.emit("vector", "scalar_tensor_tensor", out=am2b[:], in0=am4t[:, 0, :],
                   scalar=1.0, in1=am4t[:, 1, :], op0=ALU.mult, op1=ALU.max)
            k.emit("vector", "scalar_tensor_tensor", out=am2[:], in0=am4t[:, 2, :],
                   scalar=1.0, in1=am4t[:, 3, :], op0=ALU.mult, op1=ALU.max)
            k.emit("vector", "scalar_tensor_tensor", out=am2[:], in0=am2b[:],
                   scalar=1.0, in1=am2[:], op0=ALU.mult, op1=ALU.max)
            am2tok = k.mark("vector", "V")
            yield
            k.wait("tensor", am2tok)
            k.wait("tensor", iftok)
            k.wait("tensor", G["pst_free"])
            for j in range(2):
                k.emit("tensor", "transpose", out=pst[:, j * 128:(j + 1) * 128],
                       in_=am2[:, j * 128:(j + 1) * 128], identity=identf[:])
            ttk = k.mark("tensor", "T")
            am_free[b] = ttk
            yield
            k.wait("vector", ttk)
            for j in range(2):
                k.emit("vector", "reduce_max", out=rsu2T[:, j:j + 1],
                       in_=pst[:, j * 128:(j + 1) * 128], axis=AX.X)
            k.emit("vector", "tensor_scalar", out=rsu2T[:], in0=rsu2T[:],
                   scalar1=1e-5, scalar2=1.0 / 127.0, op0=ALU.max, op1=ALU.mult)
            k.emit("vector", "tensor_scalar", out=rsud[:], in0=rsu2T[:],
                   scalar1=sc[:, 2:3], scalar2=None, op0=ALU.mult)
            k.emit("vector", "reciprocal", out=su2tm[:], in_=rsu2T[:])
            psd_ev0 = k.mark("vector", "V")
            yield
            k.wait("tensor", psd_ev0)
            for j in range(2):
                k.emit("tensor", "transpose", out=pst[0:1, j * 128:(j + 1) * 128],
                       in_=su2tm[:, j:j + 1], identity=identf[:])
            ttk2 = k.mark("tensor", "T")
            k.wait("vector", ttk2)
            for j in range(2):
                k.emit("vector", "tensor_copy", out=su2bf[0:1, j * 128:(j + 1) * 128],
                       in_=pst[0:1, j * 128:(j + 1) * 128])
            sutok = k.mark("vector", "V")
            G["pst_free"] = sutok
            yield
            k.wait("tensor", sutok)
            k.wait("tensor", onesf32tok)
            k.wait("tensor", G["psb_ev"])
            k.wait("tensor", G["psd_ev"])
            k.emit("tensor", "matmul", out=psd[:, 0:T2], lhsT=onesf32[:],
                   rhs=su2bf[:], start=True, stop=True)
            btk = k.mark("tensor", "T")
            k.wait("vector", btk)
            k.emit("vector", "tensor_copy", out=su2b[:], in_=psd[:, 0:T2])
            G["psb_ev"] = k.mark("vector", "V")
            yield
            k.wait("vector", G["last_rne"])
            stk1 = {}
            stk2 = {}
            mlt = {}
            for jj in range(HC + 2):
                if jj < HC:
                    k.wait("vector", G["psb_ev"])
                    k.wait("vector", stk2.get(jj - 3))
                    k.emit("vector", "tensor_mul", out=uq2m3[:, jj % 3, :],
                           in0=h2t[:, jj, :], in1=su2b[:])
                    mlt[jj] = k.mark("vector", "V")
                j1 = jj - 1
                if 0 <= j1 < HC:
                    k.wait("scalar", mlt[j1])
                    k.emit("scalar", "activation", out=uq2m3[:, j1 % 3, :],
                           in_=uq2m3[:, j1 % 3, :], func=AF.Identity,
                           bias=np.float32(MAGIC))
                    stk1[j1] = k.mark("scalar", "S")
                j2 = jj - 2
                if 0 <= j2 < HC:
                    k.emit("scalar", "activation", out=h2t[:, j2, :],
                           in_=uq2m3[:, j2 % 3, :], func=AF.Identity,
                           bias=np.float32(-MAGIC))
                    stk2[j2] = k.mark("scalar", "S")
                yield
            h2_rne = stk2[HC - 1]
            G["last_rne"] = k.tok("S")
            k.wait("sync", G["x2_free"])
            xl = k.dma("sync", out=x2[:, :, :],
                       in_=src[t * T2:(t + 1) * T2, :].rearrange("(a p) d -> p a d", p=128),
                       buf="p1y")
            k.wait("tensor", h2_rne)
            dmtk = None
            for s4 in range(2):
                k.wait("tensor", G["psd_ev"])
                for hcg in range(0, HC, 8):
                    if hcg:
                        yield
                    for hc in range(hcg, hcg + 8):
                        for hh in range(2):
                            k.emit("tensor", "matmul",
                                   out=psd[:, hh * 512:(hh + 1) * 512],
                                   lhsT=h2t[:, hc, s4 * 128:(s4 + 1) * 128],
                                   rhs=w2qT[:, hc, hh * 512:(hh + 1) * 512],
                                   start=(hc == 0), stop=(hc == HC - 1))
                dmtk = k.mark("tensor", "T")
                k.wait("scalar", dmtk)
                k.wait("scalar", G["z_free"])
                k.emit("scalar", "activation", out=z[:, s4, :], in_=psd[:],
                       func=AF.Identity, scale=rsud[:, s4:s4 + 1])
                ztk = k.mark("scalar", "S")
                G["psd_ev"] = ztk
                k.wait("vector", ztk)
                ln_z.stats_tile(z[:, s4, :], s4)
                yield
            h2_free_p[b] = dmtk
            finz = ln_z.finalize()
            k.wait("scalar", finz)
            k.wait("scalar", xl)
            k.wait("vector", xl)
            k.wait("vector", G["out_free"])
            yield
            for s4 in range(2):
                atk = ln_z.apply(z[:, s4, :], z[:, s4, :], s4)
                k.wait("vector", atk)
                k.emit("vector", "tensor_add", out=x2[:, s4, :], in0=z[:, s4, :],
                       in1=x2[:, s4, :])
            ot = k.mark("vector", "V")
            G["z_free"] = ot
            k.wait("sync", ot)
            G["out_free"] = k.dma("sync",
                                  out=dst[t * T2:(t + 1) * T2, :].rearrange("(a p) d -> p a d", p=128),
                                  in_=x2[:, :, :], buf="p1o")
            G["x2_free"] = G["out_free"]

        def pump(n):
            g = G["gen"]
            if g is None:
                return
            for _ in range(n):
                try:
                    next(g)
                except StopIteration:
                    G["gen"] = None
                    return

        for t in range(N2):
            b = t % 2
            uq_t = uqdl[b]
            k.wait("sync", ld_free[b])
            ld = k.dma("sync", out=uq_t[:, :, :],
                       in_=scratch[t * T2:(t + 1) * T2, :].rearrange("(a p) d -> p a d", p=128),
                       buf=f"p1ul{b}")
            trtok = None
            for st in range(2):
                trtok = tm_to_fm(k, uqdT[:, :, st * 128:(st + 1) * 128],
                                 lambda c, st=st: uq_t[:, st, c * 128:(c + 1) * 128],
                                 DC, ptr, ld, evac_eng="vector", dt16=FP16)
                k.wait("tensor", trtok)
            ld_free[b] = trtok
            seqpos = t % 8
            g0 = t * HC
            k.wait("vector", h2_free_p[b])
            k.wait("scalar", h2_free_p[b])
            k.wait("scalar", tok_cw)
            k.wait("vector", am_free[b])
            k.emit("vector", "memset",
                   ap=am4[b][:, :, :].rearrange("p a t -> p (a t)"), constant=0.0)
            k.mark("vector", "V")
            if t >= 1:
                G["gen"] = back_half(t - 1)
            for ii in range(HC + 4):
                ic = ii
                gic = g0 + ic
                if ic < HC:
                    k.wait("tensor", trtok)
                    k.wait("tensor", Am.get(gic - 3))
                    pb = psh[gic % 3]
                    for dc in range(DC):
                        k.emit("tensor", "matmul", out=pb[:],
                               lhsT=w1qT[:, dc, ic * 128:(ic + 1) * 128],
                               rhs=uqdT[:, dc, :],
                               start=(dc == 0), stop=(dc == DC - 1))
                    mtkd[gic] = k.mark("tensor", "T")
                    k.wait("scalar", mtkd[gic])
                    k.wait("scalar", B2m.get(gic - 4))
                    k.emit("scalar", "activation", out=ht4[:, gic % 4, :],
                           in_=pb[:], func=AF.Identity, scale=sc[:, 0:1])
                    k.wait("scalar", Cm.get(gic - 4))
                    k.emit("scalar", "activation", out=cvt4[:, gic % 4, :],
                           in_=pb[:], func=AF.Identity, scale=cw[:, ic, 1:2],
                           bias=cb[:, ic:ic + 1])
                    Am[gic] = k.mark("scalar", "S")
                jb1 = ii - 2
                gb1 = g0 + jb1
                if 0 <= jb1 < HC:
                    k.wait("vector", Am[gb1])
                    k.emit("vector", "scalar_tensor_tensor",
                           out=cvt4[:, gb1 % 4, 1:T2],
                           in0=ht4[:, gb1 % 4, 0:T2 - 1], scalar=cw[:, jb1, 0:1],
                           in1=cvt4[:, gb1 % 4, 1:T2], op0=ALU.mult, op1=ALU.add)
                    if seqpos > 0:
                        e0 = 2 * B256.index(t)
                        k.emit("vector", "scalar_tensor_tensor",
                               out=cvt4[:, gb1 % 4, 0:1],
                               in0=h_edge[:, jb1, e0:e0 + 1], scalar=cw[:, jb1, 0:1],
                               in1=cvt4[:, gb1 % 4, 0:1], op0=ALU.mult, op1=ALU.add)
                    k.mark("vector", "V")
                jb2 = ii - 3
                gb2 = g0 + jb2
                if 0 <= jb2 < HC:
                    k.emit("vector", "scalar_tensor_tensor",
                           out=cvt4[:, gb2 % 4, 0:T2 - 1],
                           in0=ht4[:, gb2 % 4, 1:T2], scalar=cw[:, jb2, 2:3],
                           in1=cvt4[:, gb2 % 4, 0:T2 - 1], op0=ALU.mult, op1=ALU.add)
                    if seqpos < 7:
                        e1 = 2 * B256.index(t + 1) + 1
                        k.emit("vector", "scalar_tensor_tensor",
                               out=cvt4[:, gb2 % 4, T2 - 1:T2],
                               in0=h_edge[:, jb2, e1:e1 + 1], scalar=cw[:, jb2, 2:3],
                               in1=cvt4[:, gb2 % 4, T2 - 1:T2], op0=ALU.mult, op1=ALU.add)
                    B2m[gb2] = k.mark("vector", "V")
                    k.wait("scalar", B2m[gb2])
                    k.emit("scalar", "activation", out=h2[b][:, jb2, :],
                           in_=cvt4[:, gb2 % 4, :], func=AF.Silu)
                    Cm[gb2] = k.mark("scalar", "S")
                jd = ii - 4
                gd = g0 + jd
                if 0 <= jd < HC:
                    k.wait("vector", Cm[gd])
                    k.emit("vector", "scalar_tensor_tensor", out=am4[b][:, jd % 2, :],
                           in0=h2[b][:, jd, :], scalar=1.0, in1=am4[b][:, jd % 2, :],
                           op0=ALU.mult, op1=ALU.max)
                    k.emit("vector", "scalar_tensor_tensor", out=am4[b][:, 2 + jd % 2, :],
                           in0=h2[b][:, jd, :], scalar=-1.0, in1=am4[b][:, 2 + jd % 2, :],
                           op0=ALU.mult, op1=ALU.max)
                    k.mark("vector", "V")
                pump(2)
            pump(1 << 20)
        for _ in back_half(N2 - 1):
            pass
        k.set_warm(None)
        k.wait("sync", k.tok("d_p1o"))


def phase_p0(k, src, ydram, dst, W):
    nc = k.nc
    NCH = T // 128
    H0 = 8
    identf, iftok = k.const_sb("identity128_f32", np.eye(128, dtype=np.float32))
    identh, ihtok = k.const_sb("identity128_f16", np.eye(128, dtype=np.float16))
    mT = np.triu(np.ones((128, 128), dtype=np.float16))
    maskT, mttok = k.const_sb("p0_maskT", mT)

    with contextlib.ExitStack() as ph:
        sb = lambda n, s, d: ph.enter_context(nc.sbuf_tensor(n, s, d))
        ps = lambda n, s, d=F32: ph.enter_context(nc.psum_tensor(n, s, d))
        och0, ochtok0 = k.const_sb("ones_col128_f16", np.ones((128, 1), dtype=np.float16))
        k.wait("tensor", ochtok0)
        anw = sb("p0_anw", [128, DC], F32)
        onw = sb("p0_onw", [128, 1], F32)
        mlpw = sb("p0_mlpw", [128, DC], F32)
        k.dma("sync", out=anw[:], in_=W["attn_norm_w"].rearrange("(c p) -> p c", p=128),
              buf="p0nw", allow_slow_non_contiguous=True)
        k.dma("sync", out=onw[:], in_=W["out_norm_w"].rearrange("(o p) -> p o", p=128),
              buf="p0nw")
        k.dma("sync", out=mlpw[:], in_=W["mlp_norm_w"].rearrange("(c p) -> p c", p=128),
              buf="p0nw", allow_slow_non_contiguous=True)
        tok_nw = k.tok("d_p0nw")

        for s in range(NSEQ):
            k.stage_join()
            tok0 = s * T
            with contextlib.ExitStack() as sq:
                abc = sq.enter_context(contextlib.ExitStack())
                sbs = lambda n, s2, d: abc.enter_context(nc.sbuf_tensor(n, s2, d))
                lnu = LnState(k, f"p0lnu{s}", NCH, alloc=sbs)
                ch1 = sbs(f"p0_ch1{s}", [128, NCH], F32)
                ch0 = sbs(f"p0_ch0{s}", [128, NCH], F32)
                hT_scope = contextlib.ExitStack()
                hT = hT_scope.enter_context(
                    nc.sbuf_tensor(f"p0_hT{s}", [128, DC, T], FP16, side="right"))
                q16 = sbs(f"p0_q{s}", [128, DC, T], FP16)
                k16 = sbs(f"p0_k{s}", [128, DC, T], FP16)
                v16 = sbs(f"p0_v{s}", [128, NCH, D], FP16)

                k.phase_mark(f"p0A{s}")
                with contextlib.ExitStack() as sa:
                    sba = lambda n, s2, d: sa.enter_context(nc.sbuf_tensor(n, s2, d))
                    xa = [sba(f"p0a_x{i}_{s}", [128, D], F32) for i in range(2)]
                    htm = sba(f"p0a_htm_{s}", [128, D], FP16)
                    ptr = sa.enter_context(nc.psum_tensor(f"p0a_ptr{s}", [128, 1024], FP16))
                    pwma = sa.enter_context(nc.psum_tensor(f"p0a_pwm{s}", [1, 1], F32))
                    k.set_warm(pwma[:], och0[:])
                    xfree = {0: None, 1: None}
                    for ct in range(NCH):
                        b = ct % 2
                        k.wait("sync", xfree[b])
                        ld = k.dma("sync", out=xa[b][:],
                                   in_=src[tok0 + ct * 128: tok0 + (ct + 1) * 128, :],
                                   buf=f"p0ax{b}")
                        k.wait("scalar", ld)
                        k.wait("vector", ld)
                        k.warm()
                        ts_, tv_ = lnu.stats_tile(xa[b][:], ct)
                        k.wait("sync", ts_)
                        xfree[b] = tv_
                    fin = lnu.finalize()
                    k.emit("vector", "tensor_mul", out=ch1[:], in0=lnu.rs[:], in1=lnu.rs[:])
                    k.emit("vector", "tensor_scalar", out=ch0[:], in0=ch1[:],
                           scalar1=-EPS, scalar2=1.0 + EPS, op0=ALU.mult, op1=ALU.add)
                    tv = k.mark("vector", "V")
                    k.wait("scalar", tv)
                    k.emit("scalar", "activation", out=ch1[:], in_=ch0[:], func=AF.Sqrt)
                    ts_ = k.mark("scalar", "S")
                    k.wait("vector", ts_)
                    k.emit("vector", "reciprocal", out=ch1[:], in_=ch1[:])
                    k.emit("vector", "tensor_mul", out=lnu.tmp[:], in0=ch1[:], in1=ch1[:])
                    k.emit("vector", "tensor_mul", out=lnu.tmp[:], in0=lnu.tmp[:], in1=ch0[:])
                    k.emit("vector", "tensor_scalar", out=lnu.tmp[:], in0=lnu.tmp[:],
                           scalar1=-0.5, scalar2=1.5, op0=ALU.mult, op1=ALU.add)
                    k.emit("vector", "tensor_mul", out=ch1[:], in0=ch1[:], in1=lnu.tmp[:])
                    k.emit("vector", "tensor_mul", out=ch0[:], in0=lnu.nmr[:], in1=ch1[:])
                    k.emit("vector", "tensor_mul", out=ch1[:], in0=lnu.rs[:], in1=ch1[:])
                    chtok = k.mark("vector", "V")
                    k.wait("scalar", chtok)
                    xfree = {0: chtok, 1: chtok}
                    trtok = None
                    for ct in range(NCH):
                        b = ct % 2
                        k.wait("sync", xfree[b])
                        ld = k.dma("sync", out=xa[b][:],
                                   in_=src[tok0 + ct * 128: tok0 + (ct + 1) * 128, :],
                                   buf=f"p0ax{b}")
                        k.wait("scalar", ld)
                        k.wait("scalar", trtok)
                        k.emit("scalar", "activation", out=htm[:], in_=xa[b][:],
                               func=AF.Identity, scale=ch1[:, ct:ct + 1], bias=ch0[:, ct:ct + 1])
                        atok = k.mark("scalar", "S")
                        xfree[b] = atok
                        trtok = tm_to_fm(k, hT[:, :, ct * 128:(ct + 1) * 128],
                                         lambda c: htm[:, c * 128:(c + 1) * 128],
                                         DC, ptr, atok, evac_eng="vector", dt16=FP16)
                        k.wait("tensor", trtok)

                k.phase_mark(f"p0B{s}")
                k.stage_join()
                k.set_warm(None)
                with contextlib.ExitStack() as sb_:
                    sbb = lambda n, s2, d: sb_.enter_context(nc.sbuf_tensor(n, s2, d))
                    psb_ = lambda n, s2, d=F32: sb_.enter_context(nc.psum_tensor(n, s2, d))
                    wldb = [sbb(f"p0b_wld{i}_{s}", [128, 4, 1024], F32) for i in range(2)]
                    wf = sbb(f"p0b_wf_{s}", [128, DC, 1024], FP16)
                    pqk = [psb_(f"p0b_pqk{i}_{s}", [128, 1024]) for i in range(2)]
                    wld_free = {0: None, 1: None}

                    def load_weight_f16(wdram, scale_col_fn):
                        k.wait("scalar", k.tok("T"))
                        for g in range(2):
                            b = g % 2
                            k.wait("sync", wld_free[b])
                            ld = k.dma("sync", out=wldb[b][:, :, :],
                                       in_=wdram[g * 512:(g + 1) * 512, :].rearrange("(a p) d -> p a d", p=128),
                                       buf=f"p0w{b}")
                            k.wait("scalar", ld)
                            k.wait("scalar", tok_nw)
                            for a in range(4):
                                kc = g * 4 + a
                                k.emit("scalar", "activation", out=wf[:, kc, :],
                                       in_=wldb[b][:, a, :], func=AF.Identity,
                                       scale=scale_col_fn(kc))
                            wld_free[b] = k.mark("scalar", "S")
                        return k.tok("S")

                    pev = [None, None]
                    for dst16, wnm in ((q16, "Wq"), (k16, "Wk")):
                        ctok = load_weight_f16(W[wnm], lambda kc: anw[:, kc:kc + 1])
                        k.wait("tensor", ctok)
                        for nch in range(DC):
                            for th in range(2):
                                pb = pqk[th]
                                k.wait("tensor", pev[th])
                                for ts4 in range(2):
                                    for kc in range(DC):
                                        k.emit("tensor", "matmul",
                                               out=pb[:, ts4 * 512:(ts4 + 1) * 512],
                                               lhsT=wf[:, kc, nch * 128:(nch + 1) * 128],
                                               rhs=hT[:, kc, th * 1024 + ts4 * 512: th * 1024 + (ts4 + 1) * 512],
                                               start=(kc == 0), stop=(kc == DC - 1))
                                mtk = k.mark("tensor", "T")
                                k.wait("vector", mtk)
                                k.emit("vector", "tensor_copy",
                                       out=dst16[:, nch, th * 1024:(th + 1) * 1024], in_=pb[:])
                                pev[th] = k.mark("vector", "V")
                    ctok = load_weight_f16(W["Wv"], lambda kc: anw[:, kc:kc + 1])
                    k.wait("tensor", ctok)
                    for tc in range(NCH):
                        pb = pqk[tc % 2]
                        k.wait("tensor", pev[tc % 2])
                        for kc in range(DC):
                            for hh in range(2):
                                k.emit("tensor", "matmul",
                                       out=pb[:, hh * 512:(hh + 1) * 512],
                                       lhsT=hT[:, kc, tc * 128:(tc + 1) * 128],
                                       rhs=wf[:, kc, hh * 512:(hh + 1) * 512],
                                       start=(kc == 0), stop=(kc == DC - 1))
                        mtk = k.mark("tensor", "T")
                        k.wait("vector", mtk)
                        k.emit("vector", "tensor_copy", out=v16[:, tc, :], in_=pb[:])
                        pev[tc % 2] = k.mark("vector", "V")

                hT_scope.close()
                k.stage_join()
                k.phase_mark(f"p0C{s}")
                with contextlib.ExitStack() as sc_:
                    sbc = lambda n, s2, d: sc_.enter_context(nc.sbuf_tensor(n, s2, d))
                    psc = lambda n, s2, d=F32: sc_.enter_context(nc.psum_tensor(n, s2, d))
                    Wo16 = sbc(f"p0c_wo_{s}", [128, DC, 1024], FP16)
                    S16 = sbc(f"p0c_s16_{s}", [128, H0, 128], FP16)
                    Sfp = sbc(f"p0c_sfp_{s}", [128, H0, 128], F32)
                    at16 = sbc(f"p0c_at16_{s}", [128, H0, 128], FP16)
                    kt16 = sbc(f"p0c_kt16_{s}", [128, H0, 128], FP16)
                    o_n = sbc(f"p0c_on_{s}", [128, H0, 128], FP16)
                    onT = sbc(f"p0c_onT_{s}", [128, H0, 128], FP16)
                    xa = [sbc(f"p0c_x{i}_{s}", [128, D], F32) for i in range(2)]
                    y0t = sbc(f"p0c_y0_{s}", [128, D], F32)
                    y1t = sbc(f"p0c_y1_{s}", [128, D], F32)
                    sso = sbc(f"p0c_sso_{s}", [128, H0], F32)
                    rso = sbc(f"p0c_rso_{s}", [128, H0], F32)
                    ojnk = sbc(f"p0c_ojnk_{s}", [128, H0, 128], F32)
                    pAT = psc(f"p0c_pat_{s}", [128, 512])
                    pKT = psc(f"p0c_pkt_{s}", [128, 1024], FP16)
                    k.set_warm(pAT[0:1, 0:1], och0[:])
                    pO = psc(f"p0c_po_{s}", [128, 1024])
                    pS = psc(f"p0c_ps_{s}", [128, 1024])
                    pW = psc(f"p0c_pw_{s}", [128, 1024])

                    with contextlib.ExitStack() as wo_sc:
                        wldb = [wo_sc.enter_context(nc.sbuf_tensor(
                            f"p0c_wld{i}_{s}", [128, 4, 1024], F32, side="right")) for i in range(2)]
                        for g in range(2):
                            b = g % 2
                            ld = k.dma("sync", out=wldb[b][:, :, :],
                                       in_=W["Wo"][g * 512:(g + 1) * 512, :].rearrange("(a p) d -> p a d", p=128),
                                       buf=f"p0wo{b}")
                            k.wait("scalar", ld)
                            k.wait("scalar", tok_nw)
                            for a in range(4):
                                kc = g * 4 + a
                                k.emit("scalar", "activation", out=Wo16[:, kc, :],
                                       in_=wldb[b][:, a, :], func=AF.Identity, scale=onw[:, 0:1])
                            k.mark("scalar", "S")
                    wo_tok = k.tok("S")
                    k.wait("tensor", wo_tok)
                    k.wait("tensor", mttok)

                    xfree = {0: None, 1: None}
                    y1_store = None
                    sev = None
                    at_ev = None
                    kt_ev = None
                    on_ev = None
                    pW_ev = None
                    pO_ev = None
                    for c in range(NCH):
                        b = c % 2
                        k.wait("sync", xfree[b])
                        ld = k.dma("sync", out=xa[b][:],
                                   in_=src[tok0 + c * 128: tok0 + (c + 1) * 128, :],
                                   buf=f"p0cx{b}")
                        k.wait("scalar", ld)
                        k.wait("scalar", k.tok("V"))
                        k.emit("scalar", "activation", out=y0t[:], in_=xa[b][:],
                               func=AF.Identity, scale=lnu.rs[:, c:c + 1], bias=lnu.nmr[:, c:c + 1])
                        y0tok = k.mark("scalar", "S")
                        xfree[b] = y0tok
                        k.wait("tensor", at_ev)
                        k.wait("tensor", kt_ev)
                        for h in range(H0):
                            k.emit("tensor", "matmul", out=pAT[:, (h % 4) * 128:(h % 4 + 1) * 128],
                                   lhsT=k16[:, h, c * 128:(c + 1) * 128],
                                   rhs=q16[:, h, c * 128:(c + 1) * 128],
                                   start=True, stop=True)
                            if h % 4 == 3:
                                mtk = k.mark("tensor", "T")
                                k.wait("vector", mtk)
                                for hh in range(h - 3, h + 1):
                                    k.emit("vector", "tensor_mul", out=at16[:, hh, :],
                                           in0=pAT[:, (hh % 4) * 128:(hh % 4 + 1) * 128],
                                           in1=maskT[:])
                                at_ev = k.mark("vector", "V")
                                if h == 3:
                                    k.wait("tensor", at_ev)
                        for h in range(H0):
                            k.emit("tensor", "transpose", out=pKT[:, h * 128:(h + 1) * 128],
                                   in_=k16[:, h, c * 128:(c + 1) * 128], identity=identh[:])
                        ktk = k.mark("tensor", "T")
                        k.wait("vector", ktk)
                        k.emit("vector", "tensor_copy", out=kt16[:, :, :],
                               in_=pKT[:, :].rearrange("p (h t) -> p h t", t=128))
                        kt_ev = k.mark("vector", "V")
                        k.wait("tensor", at_ev)
                        k.wait("tensor", pO_ev)
                        k.wait("tensor", sev)
                        k.wait("tensor", kt_ev)
                        for h in range(H0):
                            k.emit("tensor", "matmul", out=pO[:, h * 128:(h + 1) * 128],
                                   lhsT=at16[:, h, :], rhs=v16[:, c, h * 128:(h + 1) * 128],
                                   start=True, stop=(c == 0))
                            if c > 0:
                                k.emit("tensor", "matmul", out=pO[:, h * 128:(h + 1) * 128],
                                       lhsT=q16[:, h, c * 128:(c + 1) * 128], rhs=S16[:, h, :],
                                       start=False, stop=True)
                        for h in range(H0):
                            k.emit("tensor", "matmul", out=pS[:, h * 128:(h + 1) * 128],
                                   lhsT=kt16[:, h, :], rhs=v16[:, c, h * 128:(h + 1) * 128],
                                   start=True, stop=True)
                        smt = k.mark("tensor", "T")
                        if c < NCH - 1:
                            k.wait("vector", smt)
                            if c == 0:
                                k.emit("vector", "tensor_copy", out=Sfp[:, :, :],
                                       in_=pS[:, :].rearrange("p (h t) -> p h t", t=128))
                            else:
                                k.emit("vector", "tensor_add", out=Sfp[:, :, :],
                                       in0=Sfp[:, :, :],
                                       in1=pS[:, :].rearrange("p (h t) -> p h t", t=128))
                            k.emit("vector", "tensor_copy", out=S16[:, :, :], in_=Sfp[:, :, :])
                            sev = k.mark("vector", "V")
                        k.wait("scalar", smt)
                        k.wait("scalar", on_ev)
                        for h in range(H0):
                            k.emit("scalar", "activation", out=ojnk[:, h, :],
                                   in_=pO[:, h * 128:(h + 1) * 128], func=AF.Square,
                                   accum_out=sso[:, h:h + 1])
                        sqt = k.mark("scalar", "S")
                        k.wait("vector", sqt)
                        k.emit("vector", "tensor_scalar", out=rso[:], in0=sso[:],
                               scalar1=1.0 / 128.0, scalar2=EPS, op0=ALU.mult, op1=ALU.add)
                        vt = k.mark("vector", "V")
                        k.wait("scalar", vt)
                        k.emit("scalar", "activation", out=sso[:], in_=rso[:], func=AF.Sqrt)
                        st_ = k.mark("scalar", "S")
                        k.wait("vector", st_)
                        k.emit("vector", "reciprocal", out=sso[:], in_=sso[:])
                        k.emit("vector", "tensor_mul", out=rso[:], in0=rso[:], in1=sso[:])
                        k.emit("vector", "tensor_mul", out=rso[:], in0=rso[:], in1=sso[:])
                        k.emit("vector", "tensor_scalar", out=rso[:], in0=rso[:],
                               scalar1=-0.5, scalar2=1.5, op0=ALU.mult, op1=ALU.add)
                        k.emit("vector", "tensor_mul", out=sso[:], in0=sso[:], in1=rso[:])
                        rsot = k.mark("vector", "V")
                        k.wait("scalar", rsot)
                        for h in range(H0):
                            k.emit("scalar", "activation", out=o_n[:, h, :],
                                   in_=pO[:, h * 128:(h + 1) * 128], func=AF.Identity,
                                   scale=sso[:, h:h + 1])
                        on_ev = k.mark("scalar", "S")
                        pO_ev = on_ev
                        k.wait("tensor", kt_ev)
                        k.warm()
                        k.wait("tensor", sqt)
                        k.warm()
                        k.wait("tensor", rsot)
                        k.warm()
                        trt = tm_to_fm(k, onT[:, :, :],
                                       lambda cc: o_n[:, cc, :],
                                       H0, pKT, on_ev, evac_eng="vector", dt16=FP16)
                        k.wait("tensor", trt)
                        kt_ev = trt
                        k.wait("tensor", pW_ev)
                        for hh in range(2):
                            for oc in range(DC):
                                k.emit("tensor", "matmul", out=pW[:, hh * 512:(hh + 1) * 512],
                                       lhsT=onT[:, oc, :],
                                       rhs=Wo16[:, oc, hh * 512:(hh + 1) * 512],
                                       start=(oc == 0), stop=(oc == DC - 1))
                        wmt = k.mark("tensor", "T")
                        k.wait("vector", wmt)
                        k.wait("vector", y0tok)
                        k.wait("vector", y1_store)
                        k.emit("vector", "tensor_add", out=y1t[:], in0=pW[:], in1=y0t[:])
                        yv = k.mark("vector", "V")
                        pW_ev = yv
                        k.wait("sync", yv)
                        y1_store = k.dma("sync",
                                         out=ydram[tok0 + c * 128: tok0 + (c + 1) * 128, :],
                                         in_=y1t[:], buf="p0y1")

                abc.close()
                k.stage_join()
                k.phase_mark(f"p0D{s}")
                with contextlib.ExitStack() as sd:
                    sbd = lambda n, s2, d: sd.enter_context(nc.sbuf_tensor(n, s2, d))
                    psd_ = lambda n, s2, d=F32: sd.enter_context(nc.psum_tensor(n, s2, d))
                    h2T = sbd(f"p0d_h2T_{s}", [128, DC, T], FP16)
                    bscope = sd.enter_context(contextlib.ExitStack())
                    ptr2 = [bscope.enter_context(nc.psum_tensor(f"p0d_ptr{i}_{s}", [128, 1024], FP16))
                            for i in range(2)]
                    pwmd = bscope.enter_context(nc.psum_tensor(f"p0d_pwm{s}", [1, 1], F32))
                    k.set_warm(pwmd[:], och0[:])
                    lnr = LnState(k, f"p0d_lnr{s}", NCH, alloc=sbd)
                    ya = [sbd(f"p0d_y{i}_{s}", [128, D], F32) for i in range(3)]
                    htm = [sbd(f"p0d_htm{i}_{s}", [128, D], FP16) for i in range(2)]
                    yfree = {i: None for i in range(3)}
                    sfree = {i: None for i in range(3)}
                    ldk = {}
                    k.wait("sync", k.tok("d_p0y1"))
                    for ii in range(NCH + 1):
                        if ii < NCH:
                            b = ii % 3
                            k.wait("sync", yfree[b])
                            k.wait("sync", sfree[b])
                            ldk[ii] = k.dma("sync", out=ya[b][:],
                                            in_=ydram[tok0 + ii * 128: tok0 + (ii + 1) * 128, :],
                                            buf=f"p0dy{b}")
                        j1 = ii - 1
                        if 0 <= j1 < NCH:
                            k.wait("scalar", ldk[j1])
                            k.wait("vector", ldk[j1])
                            k.warm()
                            ts_, tv_ = lnr.stats_tile(ya[j1 % 3][:], j1)
                            yfree[j1 % 3] = tv_
                            sfree[j1 % 3] = ts_
                    finr = lnr.finalize(mean_zero=True)
                    k.wait("scalar", finr)
                    yfree = {i: finr for i in range(3)}
                    trtoks = {}
                    atoks = {}
                    for ii in range(NCH + 1):
                        if ii < NCH:
                            b = ii % 3
                            k.wait("sync", yfree[b])
                            ldk[ii] = k.dma("sync", out=ya[b][:],
                                            in_=ydram[tok0 + ii * 128: tok0 + (ii + 1) * 128, :],
                                            buf=f"p0dy{b}")
                        j1 = ii - 1
                        if 0 <= j1 < NCH:
                            k.wait("scalar", ldk[j1])
                            k.wait("scalar", trtoks.get(j1 - 2))
                            k.emit("scalar", "activation", out=htm[j1 % 2][:],
                                   in_=ya[j1 % 3][:],
                                   func=AF.Identity, scale=lnr.rs[:, j1:j1 + 1])
                            atoks[j1] = k.mark("scalar", "S")
                            yfree[j1 % 3] = atoks[j1]
                            k.wait("tensor", trtoks.get(j1 - 2))
                            trtoks[j1] = tm_to_fm(
                                k, h2T[:, :, j1 * 128:(j1 + 1) * 128],
                                lambda c, j1=j1: htm[j1 % 2][:, c * 128:(c + 1) * 128],
                                DC, ptr2[j1 % 2], atoks[j1], evac_eng="vector", dt16=FP16)
                    k.wait("tensor", trtoks[NCH - 1])

                    bscope.close()
                    k.set_warm(None)
                    m16 = sbd(f"p0d_m16_{s}", [128, 8, T], FP16)
                    z_sb = sbd(f"p0d_z_{s}", [128, NCH, D], F32)
                    wgld = [sbd(f"p0d_wg{i}_{s}", [128, DC, 128], F32) for i in range(2)]
                    wuld = [sbd(f"p0d_wu{i}_{s}", [128, DC, 128], F32) for i in range(2)]
                    wg16 = [sbd(f"p0d_wg16{i}_{s}", [128, DC, 128], FP16) for i in range(2)]
                    wu16 = [sbd(f"p0d_wu16{i}_{s}", [128, DC, 128], FP16) for i in range(2)]
                    wd16 = sbd(f"p0d_wd16_{s}", [128, 8, 1024], FP16)
                    gs = sbd(f"p0d_gs_{s}", [128, 2, 512], FP16)
                    us = sbd(f"p0d_us_{s}", [128, 2, 512], FP16)
                    pg = [psd_(f"p0d_pg{i}_{s}", [128, 512]) for i in range(2)]
                    pu = [psd_(f"p0d_pu{i}_{s}", [128, 512]) for i in range(2)]
                    pz = [psd_(f"p0d_pz{i}_{s}", [128, 1024]) for i in range(2)]

                    wg_free = {0: None, 1: None}
                    wu_free = {0: None, 1: None}
                    mm_tok = {0: None, 1: None}
                    pg_ev = [None, None]
                    pu_ev = [None, None]
                    gs_free = [None, None]
                    pz_ev = [None, None]
                    m_free = None
                    z_done = None
                    def issue_wload(ic):
                        b = ic % 2
                        k.wait("sync", wg_free[b])
                        lg = k.dma("sync", out=wgld[b][:, :, :],
                                   in_=W["Wgate"][:, ic * 128:(ic + 1) * 128].rearrange("(c p) n -> p c n", p=128),
                                   buf=f"p0dg{b}")
                        k.wait("sync", wu_free[b])
                        lu = k.dma("sync", out=wuld[b][:, :, :],
                                   in_=W["Wup"][:, ic * 128:(ic + 1) * 128].rearrange("(c p) n -> p c n", p=128),
                                   buf=f"p0du{b}")
                        return lg, lu

                    def emit_casts(ic, lg, lu):
                        b = ic % 2
                        k.wait("scalar", lg)
                        k.wait("scalar", lu)
                        k.wait("scalar", mm_tok[b])
                        for kc in range(DC):
                            k.emit("scalar", "activation", out=wg16[b][:, kc, :],
                                   in_=wgld[b][:, kc, :], func=AF.Identity,
                                   scale=mlpw[:, kc:kc + 1])
                        wg_free[b] = k.mark("scalar", "S")
                        for kc in range(DC):
                            k.emit("scalar", "activation", out=wu16[b][:, kc, :],
                                   in_=wuld[b][:, kc, :], func=AF.Identity,
                                   scale=mlpw[:, kc:kc + 1])
                        wu_free[b] = k.mark("scalar", "S")

                    lds = {0: issue_wload(0), 1: issue_wload(1)}
                    emit_casts(0, *lds[0])
                    for icb in range(4):
                        k.wait("gpsimd", m_free)
                        for ici in range(8):
                            ic = icb * 8 + ici
                            k.dma("gpsimd", out=wd16[:, ici, :],
                                  in_=W["Wdown"][ic * 128:(ic + 1) * 128, :],
                                  buf="p0dd")
                        for ici in range(8):
                            ic = icb * 8 + ici
                            b = ic % 2
                            if ic + 2 < 32:
                                lds[(ic + 2) % 2] = issue_wload(ic + 2)
                            k.wait("tensor", wg_free[b])
                            k.wait("tensor", wu_free[b])
                            k.wait("vector", m_free)
                            gmts = {}
                            umts = {}

                            def emit_mm(ts4):
                                pp = ts4 % 2
                                k.wait("tensor", pg_ev[pp])
                                for kc in range(DC):
                                    k.emit("tensor", "matmul", out=pg[pp][:],
                                           lhsT=wg16[b][:, kc, :],
                                           rhs=h2T[:, kc, ts4 * 512:(ts4 + 1) * 512],
                                           start=(kc == 0), stop=(kc == DC - 1))
                                gmts[ts4] = k.mark("tensor", "T")
                                k.wait("tensor", pu_ev[pp])
                                for kc in range(DC):
                                    k.emit("tensor", "matmul", out=pu[pp][:],
                                           lhsT=wu16[b][:, kc, :],
                                           rhs=h2T[:, kc, ts4 * 512:(ts4 + 1) * 512],
                                           start=(kc == 0), stop=(kc == DC - 1))
                                umts[ts4] = k.mark("tensor", "T")

                            def emit_act(ts4):
                                pp = ts4 % 2
                                k.wait("scalar", gmts[ts4])
                                k.wait("scalar", gs_free[pp])
                                k.emit("scalar", "activation", out=gs[:, pp, :],
                                       in_=pg[pp][:], func=AF.Silu)
                                pg_ev[pp] = k.mark("scalar", "S")
                                k.wait("vector", umts[ts4])
                                k.emit("vector", "tensor_copy", out=us[:, pp, :],
                                       in_=pu[pp][:])
                                pu_ev[pp] = k.mark("vector", "V")
                                k.wait("vector", pg_ev[pp])
                                k.emit("vector", "tensor_mul",
                                       out=m16[:, ici, ts4 * 512:(ts4 + 1) * 512],
                                       in0=gs[:, pp, :], in1=us[:, pp, :])
                                gs_free[pp] = k.mark("vector", "V")

                            emit_mm(0)
                            emit_mm(1)
                            if ic + 1 < 32:
                                emit_casts(ic + 1, *lds[(ic + 1) % 2])
                            emit_act(0)
                            emit_mm(2)
                            emit_act(1)
                            emit_mm(3)
                            mm_tok[b] = umts[3]
                            emit_act(2)
                            emit_act(3)
                        mtok = k.tok("V")
                        k.wait("tensor", mtok)
                        k.wait("tensor", k.tok("d_p0dd"))
                        for sub in range(NCH):
                            pb = pz[sub % 2]
                            k.wait("tensor", pz_ev[sub % 2])
                            for ici in range(8):
                                for hh in range(2):
                                    k.emit("tensor", "matmul", out=pb[:, hh * 512:(hh + 1) * 512],
                                           lhsT=m16[:, ici, sub * 128:(sub + 1) * 128],
                                           rhs=wd16[:, ici, hh * 512:(hh + 1) * 512],
                                           start=(ici == 0), stop=(ici == 7))
                            dmt = k.mark("tensor", "T")
                            k.wait("vector", dmt)
                            if icb == 0:
                                k.emit("vector", "tensor_copy", out=z_sb[:, sub, :], in_=pb[:])
                            else:
                                k.emit("vector", "tensor_add", out=z_sb[:, sub, :],
                                       in0=z_sb[:, sub, :], in1=pb[:])
                            pz_ev[sub % 2] = k.mark("vector", "V")
                        m_free = k.tok("T")
                        z_done = k.tok("V")

                    lne = LnState(k, f"p0d_lne{s}", NCH, alloc=sbd, junk_from=lnr)
                    k.set_warm(pz[0][0:1, 0:1], och0[:])
                    k.wait("tensor", z_done)
                    k.wait("vector", z_done)
                    yfree = {i: None for i in range(3)}
                    ldk = {}
                    addm = {}
                    for ii in range(NCH + 2):
                        if ii < NCH:
                            b = ii % 3
                            k.wait("sync", yfree[b])
                            ldk[ii] = k.dma("sync", out=ya[b][:],
                                            in_=ydram[tok0 + ii * 128: tok0 + (ii + 1) * 128, :],
                                            buf=f"p0dy{b}")
                        j1 = ii - 1
                        if 0 <= j1 < NCH:
                            k.wait("vector", ldk[j1])
                            k.warm()
                            k.emit("vector", "tensor_add", out=z_sb[:, j1, :],
                                   in0=z_sb[:, j1, :], in1=ya[j1 % 3][:])
                            addm[j1] = k.mark("vector", "V")
                            yfree[j1 % 3] = addm[j1]
                        j2 = ii - 2
                        if 0 <= j2 < NCH:
                            k.wait("scalar", addm[j2])
                            lne.stats_tile(z_sb[:, j2, :], j2)
                    fine = lne.finalize()
                    k.wait("scalar", fine)
                    xfree = {i: fine for i in range(3)}
                    st_free = {0: None, 1: None}
                    ob = [sbd(f"p0d_ob{i}_{s}", [128, D], F32) for i in range(2)]
                    ld2 = {}
                    apm = {}
                    for ii in range(NCH + 2):
                        if ii < NCH:
                            b = ii % 3
                            k.wait("sync", xfree[b])
                            ld2[ii] = k.dma("sync", out=ya[b][:],
                                            in_=src[tok0 + ii * 128: tok0 + (ii + 1) * 128, :],
                                            buf=f"p0dy{b}")
                        j1 = ii - 1
                        if 0 <= j1 < NCH:
                            k.warm()
                            apm[j1] = lne.apply(z_sb[:, j1, :], z_sb[:, j1, :], j1)
                        j2 = ii - 2
                        if 0 <= j2 < NCH:
                            bo = j2 % 2
                            k.wait("vector", apm[j2])
                            k.wait("vector", ld2[j2])
                            k.wait("vector", st_free[bo])
                            k.emit("vector", "tensor_add", out=ob[bo][:],
                                   in0=z_sb[:, j2, :], in1=ya[j2 % 3][:])
                            ot = k.mark("vector", "V")
                            xfree[j2 % 3] = ot
                            k.wait("sync", ot)
                            st_free[bo] = k.dma("sync",
                                                out=dst[tok0 + j2 * 128: tok0 + (j2 + 1) * 128, :],
                                                in_=ob[bo][:], buf=f"p0o{bo}")
        k.set_warm(None)
        k.wait("sync", k.tok("d_p0o0"))
        k.wait("sync", k.tok("d_p0o1"))


def phase_p2(k, src, dst, W):
    nc = k.nc
    H1 = 16
    DK1 = 64
    NCH = T // 128
    identf, iftok = k.const_sb("identity128_f32", np.eye(128, dtype=np.float32))
    identh, ihtok = k.const_sb("identity128_f16", np.eye(128, dtype=np.float16))
    onesf_all, oatok = k.const_sb("ones128x128_f32", np.ones((128, 128), dtype=np.float32))
    onesrow_h, orhtok = k.const_sb("ones_row128_f16", np.ones((1, 128), dtype=np.float16))
    onescol_f, ocf32tok = k.const_sb("ones_col128_f32", np.ones((128, 1), dtype=np.float32))
    _ki = np.arange(128)[:, None] // 64
    _qi = np.arange(128)[None, :] // 64
    maskc, mctok = k.const_sb("p2_maskneg",
                              np.where(_ki > _qi, -50000.0, 0.0).astype(np.float16))

    with contextlib.ExitStack() as ph:
        sb = lambda n, s, d: ph.enter_context(nc.sbuf_tensor(n, s, d))
        pwm2 = ph.enter_context(nc.psum_tensor("p2_pwm", [1, 1], F32))
        och2, ochtok2 = k.const_sb("ones_col128_f16", np.ones((128, 1), dtype=np.float16))
        k.wait("tensor", ochtok2)
        k.set_warm(pwm2[:], och2[:])
        bqs = sb("p2_bq", [128, DC], F32)
        bks = sb("p2_bk", [128, DC], F32)
        bvrow = sb("p2_bv", [1, D], FP16)
        borow = sb("p2_bo", [1, D], FP16)
        D0 = sb("p2_d0", [128, H1, 128], FP16)
        D1 = sb("p2_d1", [128, H1, 128], FP16)
        rbf = sb("p2_rbf", [128, H1], F32)
        dmaxb = sb("p2_dmaxb", [128, 1], F32)
        cb = sb("p2_cb", [128, DC], F32)
        scr1 = sb("p2_scr1", [128, 4], F32)

        k.dma("sync", out=bqs[:], in_=W["bq"].rearrange("(c p) -> p c", p=128),
              buf="p2b", allow_slow_non_contiguous=True)
        k.dma("sync", out=bks[:], in_=W["bk"].rearrange("(c p) -> p c", p=128),
              buf="p2b", allow_slow_non_contiguous=True)
        k.dma("gpsimd", out=bvrow[:], in_=W["bv"].rearrange("(o d) -> o d", o=1), buf="p2b2")
        k.dma("gpsimd", out=borow[:], in_=W["bo"].rearrange("(o d) -> o d", o=1), buf="p2b2")
        k.dma("sync", out=rbf[:], in_=W["rbf"], buf="p2b")
        tok_b = k.tok("d_p2b")
        tok_b2 = k.tok("d_p2b2")

        with contextlib.ExitStack() as ld0:
            gld = [ld0.enter_context(nc.sbuf_tensor(f"p2_gld{i}", [128, H1, 128], F32,
                                                    side="right")) for i in range(2)]
            pdm = ld0.enter_context(nc.psum_tensor("p2_pdm", [128, 128], F32))
            l0 = k.dma("sync", out=gld[0][:, :, :], in_=W["G0"], buf="p2g0")
            l1 = k.dma("sync", out=gld[1][:, :, :], in_=W["G1"], buf="p2g1")
            k.wait("vector", l0)
            k.wait("vector", l1)
            k.wait("vector", tok_b)
            for h in range(H1):
                k.emit("vector", "tensor_scalar", out=D0[:, h, :], in0=gld[0][:, h, :],
                       scalar1=rbf[:, h:h + 1], scalar2=8.0, op0=ALU.subtract, op1=ALU.mult)
                k.emit("vector", "tensor_scalar", out=D1[:, h, :], in0=gld[1][:, h, :],
                       scalar1=rbf[:, h:h + 1], scalar2=8.0, op0=ALU.subtract, op1=ALU.mult)
            k.mark("vector", "V")
            k.emit("vector", "reduce_max", out=scr1[:, 0:1], in_=D0[:, :, :], axis=AX.XY,
                   apply_absolute_value=True)
            k.emit("vector", "reduce_max", out=scr1[:, 1:2], in_=D1[:, :, :], axis=AX.XY,
                   apply_absolute_value=True)
            k.emit("vector", "tensor_max", out=scr1[:, 0:1], in0=scr1[:, 0:1],
                   in1=scr1[:, 1:2])
            vt = k.mark("vector", "V")
            k.wait("tensor", vt)
            k.wait("tensor", iftok)
            k.emit("tensor", "transpose", out=pdm[0:1, :], in_=scr1[:, 0:1],
                   identity=identf[:])
            tt = k.mark("tensor", "T")
            k.wait("vector", tt)
            k.emit("vector", "reduce_max", out=scr1[0:1, 2:3], in_=pdm[0:1, :], axis=AX.X)
            k.emit("vector", "tensor_scalar", out=scr1[0:1, 3:4], in0=scr1[0:1, 2:3],
                   scalar1=0.125, scalar2=None, op0=ALU.mult)
            vt2 = k.mark("vector", "V")
            k.wait("tensor", vt2)
            k.wait("tensor", oatok)
            k.emit("tensor", "matmul", out=pdm[:, 0:1], lhsT=onesf_all[0:1, :],
                   rhs=scr1[0:1, 3:4], start=True, stop=True)
            tt2 = k.mark("tensor", "T")
            k.wait("vector", tt2)
            k.emit("vector", "tensor_copy", out=dmaxb[:], in_=pdm[:, 0:1])
            k.mark("vector", "V")
            k.wait("vector", mctok)
            for h in range(H1):
                k.emit("vector", "tensor_add", out=D0[:, h, :], in0=D0[:, h, :],
                       in1=maskc[:])
            mvt = k.mark("vector", "V")
            k.wait("scalar", mvt)
            k.emit("scalar", "activation", out=D0[:, :, :], in_=D0[:, :, :],
                   func=AF.Exp, scale=np.float32(0.125))
            k.emit("scalar", "activation", out=D1[:, :, :], in_=D1[:, :, :],
                   func=AF.Exp, scale=np.float32(0.125))
            k.mark("scalar", "S")
            k.stage_join()

        for s in range(NSEQ):
            k.stage_join()
            tok0 = s * T
            with contextlib.ExitStack() as sq:
                sbs = lambda n, s2, d: sq.enter_context(nc.sbuf_tensor(n, s2, d))
                o_all = sbs(f"p2_oall_{s}", [128, NCH, H1 * DK1], FP16)
                lne = LnState(k, f"p2_lne_{s}", NCH, alloc=sbs)
                with contextlib.ExitStack() as at_sc:
                    sba = lambda n, s2, d: at_sc.enter_context(nc.sbuf_tensor(n, s2, d))
                    psa = lambda n, s2, d=F32: at_sc.enter_context(nc.psum_tensor(n, s2, d))
                    Q16 = sba(f"p2_q_{s}", [128, DC, T], FP16)
                    K16 = sba(f"p2_k_{s}", [128, DC, T], FP16)
                    vaug = sba(f"p2_va_{s}", [128, NCH, 1040], FP16)

                    k.phase_mark(f"p2bu{s}")
                    with contextlib.ExitStack() as bu:
                        sbb = lambda n, s2, d: bu.enter_context(nc.sbuf_tensor(n, s2, d))
                        psb_ = lambda n, s2, d=F32: bu.enter_context(nc.psum_tensor(n, s2, d))
                        uT = sbb(f"p2_uT_{s}", [128, DC, T], FP16)
                        xa = [sbb(f"p2_x{i}_{s}", [128, D], F32) for i in range(2)]
                        utm = sbb(f"p2_utm_{s}", [128, D], FP16)
                        wf = sbb(f"p2_wf_{s}", [128, DC, 1024], FP16)
                        ptr = psb_(f"p2_ptr_{s}", [128, 1024], FP16)
                        pqk = [psb_(f"p2_pqk{i}_{s}", [128, 1024]) for i in range(2)]

                        xfree = {0: None, 1: None}
                        trtok = None
                        for ct in range(NCH):
                            b = ct % 2
                            k.wait("sync", xfree[b])
                            ld = k.dma("sync", out=xa[b][:],
                                       in_=src[tok0 + ct * 128: tok0 + (ct + 1) * 128, :],
                                       buf=f"p2x{b}")
                            k.wait("scalar", ld)
                            k.wait("vector", ld)
                            k.warm()
                            ts_, tv_ = lne.stats_tile(xa[b][:], ct)
                            k.wait("sync", ts_)
                            xfree[b] = tv_
                        fin = lne.finalize()
                        k.wait("scalar", fin)
                        xfree = {0: fin, 1: fin}
                        for ct in range(NCH):
                            b = ct % 2
                            k.wait("sync", xfree[b])
                            ld = k.dma("sync", out=xa[b][:],
                                       in_=src[tok0 + ct * 128: tok0 + (ct + 1) * 128, :],
                                       buf=f"p2x{b}")
                            k.wait("scalar", ld)
                            k.wait("scalar", trtok)
                            k.warm()
                            atok = lne.apply(utm[:], xa[b][:], ct)
                            xfree[b] = atok
                            trtok = tm_to_fm(k, uT[:, :, ct * 128:(ct + 1) * 128],
                                             lambda c: utm[:, c * 128:(c + 1) * 128],
                                             DC, ptr, atok, evac_eng="vector", dt16=FP16)
                            k.wait("tensor", trtok)

                        def ldw(wnm):
                            k.wait("gpsimd", k.tok("T"))
                            for kc in range(DC):
                                k.dma("gpsimd", out=wf[:, kc, :],
                                      in_=W[wnm][kc * 128:(kc + 1) * 128, :], buf="p2wf")
                            return k.tok("d_p2wf")

                        pev = [None, None]
                        for dst16, wnm, brow in ((Q16, "Wq", bqs), (K16, "Wk", bks)):
                            ctok = ldw(wnm)
                            k.wait("tensor", ctok)
                            k.wait("gpsimd", k.tok("S"))
                            for nch in range(DC):
                                for th in range(2):
                                    pb = pqk[th]
                                    k.wait("tensor", pev[th])
                                    for ts4 in range(2):
                                        for kc in range(DC):
                                            k.emit("tensor", "matmul",
                                                   out=pb[:, ts4 * 512:(ts4 + 1) * 512],
                                                   lhsT=wf[:, kc, nch * 128:(nch + 1) * 128],
                                                   rhs=uT[:, kc, th * 1024 + ts4 * 512: th * 1024 + (ts4 + 1) * 512],
                                                   start=(kc == 0), stop=(kc == DC - 1))
                                    mtk = k.mark("tensor", "T")
                                    if th == 0:
                                        k.wait("scalar", mtk)
                                        k.wait("scalar", tok_b)
                                        k.emit("scalar", "activation",
                                               out=dst16[:, nch, th * 1024:(th + 1) * 1024],
                                               in_=pb[:], func=AF.Identity,
                                               bias=brow[:, nch:nch + 1])
                                        pev[th] = k.mark("scalar", "S")
                                    else:
                                        k.wait("vector", mtk)
                                        k.wait("vector", tok_b)
                                        k.emit("vector", "tensor_scalar",
                                               out=dst16[:, nch, th * 1024:(th + 1) * 1024],
                                               in0=pb[:], scalar1=brow[:, nch:nch + 1],
                                               scalar2=None, op0=ALU.add)
                                        pev[th] = k.mark("vector", "V")
                        ctok = ldw("Wv")
                        k.wait("tensor", ctok)
                        k.wait("tensor", orhtok)
                        k.wait("tensor", tok_b2)
                        k.emit("vector", "memset",
                               ap=vaug[:, :, :].rearrange("p c (h e) -> p c h e", e=65)[:, :, :, 64:65],
                               constant=1.0)
                        onestok = k.mark("vector", "V")
                        for tc in range(NCH):
                            pb = pqk[tc % 2]
                            k.wait("tensor", pev[tc % 2])
                            for kc in range(DC):
                                for hh in range(2):
                                    k.emit("tensor", "matmul",
                                           out=pb[:, hh * 512:(hh + 1) * 512],
                                           lhsT=uT[:, kc, tc * 128:(tc + 1) * 128],
                                           rhs=wf[:, kc, hh * 512:(hh + 1) * 512],
                                           start=(kc == 0), stop=False)
                            for hh in range(2):
                                k.emit("tensor", "matmul",
                                       out=pb[:, hh * 512:(hh + 1) * 512],
                                       lhsT=onesrow_h[:],
                                       rhs=bvrow[0:1, hh * 512:(hh + 1) * 512],
                                       start=False, stop=True)
                            mtk = k.mark("tensor", "T")
                            k.wait("vector", mtk)
                            k.wait("vector", onestok)
                            k.emit("vector", "tensor_copy",
                                   out=vaug[:, tc, :].rearrange("p (h e) -> p h e", e=65)[:, :, 0:64],
                                   in_=pb[:].rearrange("p (h e) -> p h e", e=64))
                            pev[tc % 2] = k.mark("vector", "V")

                    with contextlib.ExitStack() as sh:
                        sbb = lambda n, s2, d: sh.enter_context(nc.sbuf_tensor(n, s2, d))
                        psb_ = lambda n, s2, d=F32: sh.enter_context(nc.psum_tensor(n, s2, d))
                        sqv = sbb(f"p2_sq_{s}", [128, T], F32)
                        qn = sbb(f"p2_qn_{s}", [128, DC], F32)
                        qpad = sbb(f"p2_qpad_{s}", [128, 128], F32)
                        nres = sbb(f"p2_nres_{s}", [128, 2], F32)
                        nsc = sbb(f"p2_nsc_{s}", [128, 16], F32)
                        pn = psb_(f"p2_pn_{s}", [128, 16])
                        pb1 = psb_(f"p2_pb1_{s}", [128, 128])
                        k.stage_join()
                        k.wait("tensor", ocf32tok)
                        pn_free = None
                        pb1_free = None
                        for ti, t16 in ((0, Q16), (1, K16)):
                            k.wait("vector", k.tok("S"))
                            for pc in range(DC):
                                k.warm()
                                k.emit("vector", "tensor_mul", out=sqv[:],
                                       in0=t16[:, pc, :], in1=t16[:, pc, :])
                                sqk = k.mark("vector", "V")
                                k.wait("tensor", sqk)
                                k.wait("tensor", pn_free)
                                for ts16 in range(NCH):
                                    k.emit("tensor", "matmul", out=pn[:, ts16:ts16 + 1],
                                           lhsT=sqv[:, ts16 * 128:(ts16 + 1) * 128],
                                           rhs=onescol_f[:], start=True, stop=True)
                                ntk = k.mark("tensor", "T")
                                k.wait("vector", ntk)
                                k.emit("vector", "reduce_max", out=qn[:, pc:pc + 1],
                                       in_=pn[:, 0:NCH], axis=AX.X)
                                pn_free = k.mark("vector", "V")
                            for g3 in range(3):
                                pcs = [p for p in range(g3 * 3, min(g3 * 3 + 3, DC))]
                                k.emit("vector", "memset", ap=qpad[:], constant=0.0)
                                for j, pc in enumerate(pcs):
                                    k.emit("vector", "tensor_copy",
                                           out=qpad[:, 32 * j: 32 * j + 1],
                                           in_=qn[:, pc:pc + 1])
                                vt = k.mark("vector", "V")
                                k.wait("tensor", vt)
                                k.wait("tensor", pb1_free)
                                k.emit("tensor", "transpose", out=pb1[:], in_=qpad[:],
                                       identity=identf[:])
                                tt = k.mark("tensor", "T")
                                k.wait("vector", tt)
                                for j, pc in enumerate(pcs):
                                    k.emit("vector", "reduce_max",
                                           out=nres[32 * j: 32 * j + 1, ti:ti + 1],
                                           in_=pb1[32 * j: 32 * j + 1, :], axis=AX.X)
                                    k.emit("vector", "tensor_copy",
                                           out=nsc[32 * j:32 * j + 1, ti * DC + pc: ti * DC + pc + 1],
                                           in_=nres[32 * j: 32 * j + 1, ti:ti + 1])
                                pb1_free = k.mark("vector", "V")
                        k.wait("vector", k.tok("V"))
                        for pc in range(DC):
                            r = 32 * (pc % 3)
                            k.emit("vector", "tensor_mul", out=nsc[r:r + 1, pc:pc + 1],
                                   in0=nsc[r:r + 1, pc:pc + 1],
                                   in1=nsc[r:r + 1, DC + pc:DC + pc + 1])
                        prodt = k.mark("vector", "V")
                        k.wait("scalar", prodt)
                        for pc in range(DC):
                            r = 32 * (pc % 3)
                            k.emit("scalar", "activation", out=nsc[r:r + 1, pc:pc + 1],
                                   in_=nsc[r:r + 1, pc:pc + 1], func=AF.Sqrt)
                        sqt = k.mark("scalar", "S")
                        k.wait("vector", sqt)
                        for pc in range(DC):
                            r = 32 * (pc % 3)
                            k.emit("vector", "tensor_scalar", out=nsc[r:r + 1, pc:pc + 1],
                                   in0=nsc[r:r + 1, pc:pc + 1], scalar1=-0.125,
                                   scalar2=None, op0=ALU.mult)
                        nsct = k.mark("vector", "V")
                        k.wait("tensor", nsct)
                        k.wait("tensor", pb1_free)
                        for pc in range(DC):
                            r = 32 * (pc % 3)
                            k.emit("tensor", "matmul", out=pb1[:, pc:pc + 1],
                                   lhsT=onesf_all[r:r + 1, :],
                                   rhs=nsc[r:r + 1, pc:pc + 1],
                                   start=True, stop=True)
                        bct = k.mark("tensor", "T")
                        k.wait("vector", bct)
                        k.emit("vector", "tensor_scalar", out=cb[:], in0=pb1[:, 0:DC],
                               scalar1=dmaxb[:], scalar2=None, op0=ALU.subtract)
                        cbtok = k.mark("vector", "V")
                        k.wait("tensor", cbtok)
                        k.wait("scalar", cbtok)


                    k.phase_mark(f"p2at{s}")
                    with contextlib.ExitStack() as aa:
                        sc_ = lambda n, s2, d: aa.enter_context(nc.sbuf_tensor(n, s2, d))
                        pc_ = lambda n, s2, d=F32: aa.enter_context(nc.psum_tensor(n, s2, d))
                        eTb2 = [sc_(f"p2_eT{i}_{s}", [128, NCH, 512], FP16) for i in range(2)]
                        rq = sc_(f"p2_rq_{s}", [128, 2], F32)
                        pss = [pc_(f"p2_ps{i}_{s}", [128, 512]) for i in range(2)]
                        pot = [pc_(f"p2_po{i}_{s}", [128, 65]) for i in range(2)]
                        ps_ev = [None, None]
                        po_ev = [None, None]
                        eT_freeT = [None, None]
                        it_idx = 0
                        k.wait("tensor", k.tok("V"))
                        k.wait("tensor", k.tok("S"))

                        def emit_o(p):
                            eTb_p, h_p, qb4_p, par, exptok_p, vtok_p = p
                            k.warm()
                            k.wait("tensor", exptok_p)
                            k.wait("tensor", vtok_p)
                            k.warm()
                            for qsub in range(4):
                                qb = 4 * qb4_p + qsub
                                pbo = pot[qsub % 2]
                                k.wait("tensor", po_ev[qsub % 2])
                                for kb in range(qb + 1):
                                    k.emit("tensor", "matmul", out=pbo[:],
                                           lhsT=eTb_p[:, kb, qsub * 128:(qsub + 1) * 128],
                                           rhs=vaug[:, kb, 65 * h_p:65 * h_p + 65],
                                           start=(kb == 0), stop=(kb == qb))
                                omt = k.mark("tensor", "T")
                                k.wait("vector", omt)
                                k.wait("vector", po_ev[qsub % 2])
                                k.emit("vector", "reciprocal",
                                       out=rq[:, qsub % 2:qsub % 2 + 1],
                                       in_=pbo[:, 64:65])
                                rqt = k.mark("vector", "V")
                                k.wait("scalar", rqt)
                                k.emit("scalar", "activation",
                                       out=o_all[:, qb, DK1 * h_p:DK1 * (h_p + 1)],
                                       in_=pbo[:, 0:64], func=AF.Identity,
                                       scale=rq[:, qsub % 2:qsub % 2 + 1])
                                po_ev[qsub % 2] = k.mark("scalar", "S")
                            eT_freeT[par] = k.tok("T")

                        pend = None
                        for h in range(H1):
                            pcair = h // 2
                            pbase = 64 * (h % 2)
                            for qb4 in range(4):
                                eTb = eTb2[it_idx % 2]
                                k.wait("scalar", eT_freeT[it_idx % 2])
                                k.warm()
                                vt_last = None
                                for kb in range(4 * qb4 + 4):
                                    off = max(0, 128 * (kb - 4 * qb4))
                                    wdt = 512 - off
                                    q0 = qb4 * 512 + off
                                    pb = pss[kb % 2]
                                    k.wait("tensor", ps_ev[kb % 2])
                                    k.emit("tensor", "matmul", out=pb[:, 0:wdt],
                                           lhsT=K16[pbase:pbase + 64, pcair, kb * 128:(kb + 1) * 128],
                                           rhs=Q16[pbase:pbase + 64, pcair, q0:q0 + wdt],
                                           start=True, stop=True)
                                    mtk = k.mark("tensor", "T")
                                    k.wait("scalar", mtk)
                                    k.emit("scalar", "activation",
                                           out=eTb[:, kb, off:512], in_=pb[:, 0:wdt],
                                           func=AF.Exp, scale=0.125,
                                           bias=cb[:, pcair:pcair + 1])
                                    expm = k.mark("scalar", "S")
                                    ps_ev[kb % 2] = expm
                                    for dmas, kk in ((D0, kb), (D1, kb + 1)):
                                        if 128 * kk >= qb4 * 512 and 128 * kk < qb4 * 512 + 512:
                                            doff = 128 * kk - q0 + off
                                            k.wait("vector", expm)
                                            k.emit("vector", "tensor_mul",
                                                   out=eTb[:, kb, doff:doff + 128],
                                                   in0=eTb[:, kb, doff:doff + 128],
                                                   in1=dmas[:, h, :])
                                            vt_last = k.mark("vector", "V")
                                cur = (eTb, h, qb4, it_idx % 2, k.tok("S"), vt_last)
                                if pend is not None:
                                    emit_o(pend)
                                pend = cur
                                it_idx += 1
                        emit_o(pend)

                k.phase_mark(f"p2wo{s}")
                with contextlib.ExitStack() as wo:
                    sw = lambda n, s2, d: wo.enter_context(nc.sbuf_tensor(n, s2, d))
                    pw_ = lambda n, s2, d=F32: wo.enter_context(nc.psum_tensor(n, s2, d))
                    Wo16 = sw(f"p2_wo16_{s}", [128, DC, 1024], FP16)
                    ofmP = sw(f"p2_ofmp_{s}", [128, 8, 128], FP16)
                    z_all = sw(f"p2_zall_{s}", [128, NCH, D], F32)
                    xa2 = [sw(f"p2_xe{i}_{s}", [128, D], F32) for i in range(2)]
                    ob = [sw(f"p2_ob{i}_{s}", [128, D], F32) for i in range(2)]
                    lnz = LnState(k, f"p2_lnz_{s}", NCH, alloc=sw)
                    ptp = pw_(f"p2_ptp_{s}", [128, 1024], FP16)
                    pwo = [pw_(f"p2_pwo{i}_{s}", [128, 1024]) for i in range(2)]
                    k.stage_join()
                    for kc in range(DC):
                        k.dma("gpsimd", out=Wo16[:, kc, :],
                              in_=W["Wo"][kc * 128:(kc + 1) * 128, :], buf=f"p2wo_{s}")
                    wotok = k.tok(f"d_p2wo_{s}")
                    k.wait("tensor", wotok)
                    tr_ev = None
                    pwo_ev = [None, None]
                    for qb in range(NCH):
                        k.wait("tensor", tr_ev)
                        for pr in range(8):
                            k.emit("tensor", "transpose",
                                   out=ptp[:, pr * 128:(pr + 1) * 128],
                                   in_=o_all[:, qb, 128 * pr:128 * (pr + 1)],
                                   identity=identh[:])
                        ttk = k.mark("tensor", "T")
                        k.wait("vector", ttk)
                        k.emit("vector", "tensor_copy",
                               out=ofmP[:, :, :],
                               in_=ptp[:, :].rearrange("p (c t) -> p c t", t=128))
                        tr_ev = k.mark("vector", "V")
                        k.wait("tensor", tr_ev)
                        pb = pwo[qb % 2]
                        k.wait("tensor", pwo_ev[qb % 2])
                        for hh in range(2):
                            for pr in range(8):
                                k.emit("tensor", "matmul",
                                       out=pb[:, hh * 512:(hh + 1) * 512],
                                       lhsT=ofmP[:, pr, :],
                                       rhs=Wo16[:, pr, hh * 512:(hh + 1) * 512],
                                       start=(pr == 0), stop=False)
                            k.emit("tensor", "matmul", out=pb[:, hh * 512:(hh + 1) * 512],
                                   lhsT=onesrow_h[:],
                                   rhs=borow[0:1, hh * 512:(hh + 1) * 512],
                                   start=False, stop=True)
                        wmt = k.mark("tensor", "T")
                        k.wait("vector", wmt)
                        k.emit("vector", "tensor_copy", out=z_all[:, qb, :], in_=pb[:])
                        zv = k.mark("vector", "V")
                        pwo_ev[qb % 2] = zv
                        k.wait("scalar", zv)
                        ts_, tv_ = lnz.stats_tile(z_all[:, qb, :], qb)
                    finz = lnz.finalize()
                    k.wait("scalar", finz)
                    xfree2 = {0: None, 1: None}
                    stfree = {0: None, 1: None}
                    for qb in range(NCH):
                        b = qb % 2
                        k.warm()
                        atok = lnz.apply(z_all[:, qb, :], z_all[:, qb, :], qb)
                        k.wait("sync", stfree[b])
                        xl = k.dma("sync", out=ob[b][:],
                                   in_=src[tok0 + qb * 128: tok0 + (qb + 1) * 128, :],
                                   buf=f"p2xl{b}")
                        k.wait("vector", atok)
                        k.wait("vector", xl)
                        k.emit("vector", "tensor_add", out=ob[b][:], in0=ob[b][:],
                               in1=z_all[:, qb, :])
                        ot = k.mark("vector", "V")
                        k.wait("sync", ot)
                        stfree[b] = k.dma("sync",
                                          out=dst[tok0 + qb * 128: tok0 + (qb + 1) * 128, :],
                                          in_=ob[b][:], buf=f"p2o{b}")
        k.set_warm(None)
        k.wait("sync", k.tok("d_p2o0"))
        k.wait("sync", k.tok("d_p2o1"))


def build_program(phases=("p0", "p1", "p2", "p3")):
    k = K()
    nc = k.nc
    x0 = nc.dram_tensor("x0", [NTOK, D], F32, kind="ExternalInput").ap()
    out = nc.dram_tensor("out", [NTOK, D], F32, kind="ExternalOutput").ap()

    def din(name, shape):
        return nc.dram_tensor(name, shape, F32, kind="ExternalInput").ap()

    with k.es:
        ident_bf16(k)
        ones_col128(k)
        k.const_sb("ident1_f32", np.ones((1, 1), dtype=np.float32))
        k.const_sb("ones_f32_row", np.ones((1, 128), dtype=np.float32))
        k.const_sb("identity128_f32", np.eye(128, dtype=np.float32))
        k.const_sb("identity128_f16", np.eye(128, dtype=np.float16))
        k.const_sb("ones_col128_f32", np.ones((128, 1), dtype=np.float32))
        k.const_sb("ones_col128_f16", np.ones((128, 1), dtype=np.float16))
        k.const_sb("ones_row128_f16", np.ones((1, 128), dtype=np.float16))
        if phases == ("p0", "p1", "p2", "p3"):
            W0 = {"attn_norm_w": din("L0_attn_attn_norm_w", [D]),
                  "Wq": din("L0_attn_Wq", [D, D]), "Wk": din("L0_attn_Wk", [D, D]),
                  "Wv": din("L0_attn_Wv", [D, D]), "Wo": din("L0_attn_Wo", [D, D]),
                  "out_norm_w": din("L0_attn_attn_out_norm_w", [128]),
                  "mlp_norm_w": din("L0_attn_mlp_norm_w", [D]),
                  "Wgate": din("L0_attn_Wgate", [D, INTER]),
                  "Wup": din("L0_attn_Wup", [D, INTER]),
                  "Wdown": din("L0_attn_Wdown", [INTER, D])}
            W1 = {"w1": din("L0_ffn_w1", [INTER, D]),
                  "w2": din("L0_ffn_w2", [D, INTER]),
                  "conv_w": din("L0_ffn_conv_w", [INTER, 3]),
                  "conv_b": din("L0_ffn_conv_b", [INTER])}
            W2 = {"Wq": din("L1_attn_Wq", [D, D]), "bq": din("L1_attn_bq", [D]),
                  "Wk": din("L1_attn_Wk", [D, D]), "bk": din("L1_attn_bk", [D]),
                  "Wv": din("L1_attn_Wv", [D, D]), "bv": din("L1_attn_bv", [D]),
                  "Wo": din("L1_attn_Wo", [D, D]), "bo": din("L1_attn_bo", [D]),
                  "G0": din("L1_attn_G0", [128, 16, 128]),
                  "G1": din("L1_attn_G1", [128, 16, 128]),
                  "rbf": din("L1_attn_rbf", [128, 16])}
            W3 = {"weight": din("L1_ffn_weight", [D, D]),
                  "aux_weight1": din("L1_ffn_aux_weight1", [D, D]),
                  "aux_weight2": din("L1_ffn_aux_weight2", [D, D]),
                  "bias": din("L1_ffn_bias", [D])}
            ybuf = nc.dram_tensor("ybuf", [NTOK, D], F32, kind="Internal").ap()
            xA = nc.dram_tensor("xA", [NTOK, D], F32, kind="Internal").ap()
            xB = nc.dram_tensor("xB", [NTOK, D], F32, kind="Internal").ap()
            xC = nc.dram_tensor("xC", [NTOK, D], F32, kind="Internal").ap()
            uqs = nc.dram_tensor("uq_scratch", [NTOK, D], FP16, kind="Internal").ap()
            k.phase_mark("p0")
            phase_p0(k, x0, ybuf, xA, W0)
            k.stage_join()
            k.phase_mark("p1")
            phase_p1(k, xA, xB, W1, uqs)
            k.stage_join()
            k.phase_mark("p2")
            phase_p2(k, xB, xC, W2)
            k.stage_join()
            k.phase_mark("p3")
            phase_p3(k, xC, out, W3)
            k.phase_mark("end")
        elif phases == ("p0",):
            W = {"attn_norm_w": din("L0a_anw", [D]),
                 "Wq": din("L0a_Wq", [D, D]), "Wk": din("L0a_Wk", [D, D]),
                 "Wv": din("L0a_Wv", [D, D]), "Wo": din("L0a_Wo", [D, D]),
                 "out_norm_w": din("L0a_onw", [128]),
                 "mlp_norm_w": din("L0a_mnw", [D]),
                 "Wgate": din("L0a_Wg", [D, INTER]), "Wup": din("L0a_Wu", [D, INTER]),
                 "Wdown": din("L0a_Wd", [INTER, D])}
            ybuf = nc.dram_tensor("ybuf", [NTOK, D], F32, kind="Internal").ap()
            phase_p0(k, x0, ybuf, out, W)
        elif phases == ("p1",):
            W = {"w1": din("L0f_w1", [INTER, D]), "w2": din("L0f_w2", [D, INTER]),
                 "conv_w": din("L0f_cw", [INTER, 3]), "conv_b": din("L0f_cb", [INTER])}
            uqs = nc.dram_tensor("uq_scratch", [NTOK, D], FP16, kind="Internal").ap()
            phase_p1(k, x0, out, W, uqs)
        elif phases == ("p2",):
            W = {"Wq": din("L1a_Wq", [D, D]), "bq": din("L1a_bq", [D]),
                 "Wk": din("L1a_Wk", [D, D]), "bk": din("L1a_bk", [D]),
                 "Wv": din("L1a_Wv", [D, D]), "bv": din("L1a_bv", [D]),
                 "Wo": din("L1a_Wo", [D, D]), "bo": din("L1a_bo", [D]),
                 "G0": din("L1a_G0", [128, 16, 128]),
                 "G1": din("L1a_G1", [128, 16, 128]),
                 "rbf": din("L1a_rbf", [128, 16])}
            phase_p2(k, x0, out, W)
        elif phases == ("p3",):
            W = {"weight": din("L1f_weight", [D, D]),
                 "aux_weight1": din("L1f_aux1", [D, D]),
                 "aux_weight2": din("L1f_aux2", [D, D]),
                 "bias": din("L1f_bias", [D])}
            phase_p3(k, x0, out, W)
        else:
            raise NotImplementedError(phases)
        k.build_block()
    return k


def _make_masters(rel_bias):
    q = np.arange(128)[None, :]
    kk = np.arange(128)[:, None]
    i0 = q - kk + 128
    i1 = np.minimum(128 + q - kk, 128) + 128
    G0 = np.ascontiguousarray(rel_bias[i0].transpose(0, 2, 1))
    G1 = np.ascontiguousarray(rel_bias[i1].transpose(0, 2, 1))
    rbf = np.ascontiguousarray(np.broadcast_to(rel_bias[256], (128, 16)))
    return G0, G1, rbf


def kernel(**inputs):
    ins = {n: np.ascontiguousarray(np.asarray(v)) for n, v in inputs.items()}
    x = ins.pop("L0_attn_hidden_states")
    shards = [np.ascontiguousarray(x[2 * i:2 * i + 2].reshape(NTOK, D))
              for i in range(8)]
    G0, G1, rbf = _make_masters(ins["L1_attn_rel_bias"])
    common = dict(ins)
    del common["L1_attn_rel_bias"]
    common["L0_ffn_conv_w"] = np.ascontiguousarray(
        ins["L0_ffn_conv_w"].reshape(INTER, 3))
    common["L1_attn_G0"] = G0
    common["L1_attn_G1"] = G1
    common["L1_attn_rbf"] = rbf
    k = build_program()
    in_maps = [dict(common, x0=s) for s in shards]
    res = run_bass_kernel_spmd(k.nc, in_maps, core_ids=list(range(8)))
    outs = [res.results[i]["out"] for i in range(8)]
    return np.concatenate(outs, axis=0).reshape(16, 2048, D)


# version 18 (final)
# speedup vs baseline: 1.0107x; 1.0107x over previous
import contextlib
import numpy as np
import ml_dtypes
import concourse.bass as bass
import concourse.mybir as mybir
from concourse.bass_utils import run_bass_kernel_spmd

F32 = mybir.dt.float32
BF16 = mybir.dt.bfloat16
FP16 = mybir.dt.float16
AF = mybir.ActivationFunctionType
ALU = mybir.AluOpType
AX = mybir.AxisListType

ENGINES = ("sync", "tensor", "vector", "scalar", "gpsimd")

NSEQ = 2
T = 2048
NTOK = NSEQ * T
D = 1024
DC = D // 128
INTER = 4096
EPS = 1e-5
MAGIC = float(np.float32(1.5 * 2 ** 23))


class K:
    def __init__(self):
        self.nc = bass.Bass("TRN2", target_bir_lowering=False)
        self.prog = {e: [] for e in ENGINES}
        self.cnt = {}
        self.last_wait = {}
        self.sem_names = ["T", "V", "S", "G"]
        self.dma_sems = set()
        self._dirty = {e: [] for e in ENGINES}
        self._dirty_r = {e: [] for e in ENGINES}
        self.phase_bounds = []
        self.es = contextlib.ExitStack()
        self._consts = {}

    def phase_mark(self, name):
        self.phase_bounds.append((name, {e: sum(1 for it in self.prog[e] if it[0] == "op")
                                         for e in ENGINES}))

    _WRITE_KEYS = ("out", "ap", "accum_out", "out_max", "out_indices")

    @staticmethod
    def _ap_name(v):
        t = getattr(v, "tensor", None)
        return getattr(t, "name", None) if t is not None else None

    NODRAIN = ("junk", "qjunk", "ojnk")

    def _is_scratch(self, nm):
        return any(s in nm for s in self.NODRAIN)

    @staticmethod
    def _fpr(v):
        t = getattr(v, "tensor", None)
        nm = getattr(t, "name", None) if t is not None else None
        if nm is None:
            return None
        try:
            dims = v.ap
            bp = v.base_partition()
            np_ = dims[0][1] if dims else 1
            pstride = dims[0][0] if dims else 0
            lo = v.offset - bp * pstride
            hi = lo + 1
            for st, sz in dims[1:]:
                d = (sz - 1) * st
                if d >= 0:
                    hi += d
                else:
                    lo += d
            return (nm, bp, bp + np_, lo, hi)
        except Exception:
            return (nm, 0, 1 << 20, 0, 1 << 40)

    @staticmethod
    def _ovl(a, b):
        return (a[0] == b[0] and a[1] < b[2] and b[1] < a[2]
                and a[3] < b[4] and b[3] < a[4])

    def emit(self, eng, meth, **kw):
        if meth == "drain":
            self._dirty[eng] = []
            self._dirty_r[eng] = []
            self.prog[eng].append(("op", meth, kw))
            return
        if meth != "dma_start" and eng in ("vector", "scalar", "gpsimd"):
            writes = []
            reads = []
            for key, v in kw.items():
                fp = self._fpr(v)
                if fp is None or (key in self._WRITE_KEYS and self._is_scratch(fp[0])):
                    continue
                (writes if key in self._WRITE_KEYS else reads).append(fp)
            pw, pr = self._dirty[eng], self._dirty_r[eng]
            hazard = any(self._ovl(x, p) for x in writes + reads for p in pw) or \
                     any(self._ovl(w, p) for w in writes for p in pr)
            if hazard:
                self.prog[eng].append(("op", "drain", {}))
                self._dirty[eng] = pw = []
                self._dirty_r[eng] = pr = []
            pw.extend(writes)
            pr.extend(reads)
        self.prog[eng].append(("op", meth, kw))

    def wait(self, eng, tok):
        if tok is None:
            return
        sem, val = tok
        if val <= 0:
            return
        key = (eng, sem)
        if self.last_wait.get(key, -1) >= val:
            return
        self.last_wait[key] = val
        self.prog[eng].append(("wait", sem, val))

    def mark(self, eng, sem, inc=1):
        assert self.prog[eng] and self.prog[eng][-1][0] == "op", f"mark needs op on {eng}"
        self.cnt[sem] = self.cnt.get(sem, 0) + inc
        self.prog[eng].append(("inc", sem, inc))
        return (sem, self.cnt[sem])

    def tok(self, sem):
        return (sem, self.cnt.get(sem, 0))

    def set_warm(self, psum_ap, opnd_ap=None):
        self._warm = None if psum_ap is None else (psum_ap, opnd_ap)

    def warm(self):
        w = getattr(self, "_warm", None)
        if w is None:
            return
        self.emit("tensor", "matmul", out=w[0], lhsT=w[1], rhs=w[1],
                  start=True, stop=True)

    def stage_join(self):
        for e in ("sync", "gpsimd"):
            for sem in ("T", "V", "S"):
                self.wait(e, self.tok(sem))
        for e, own in (("vector", "V"), ("scalar", "S"), ("tensor", "T")):
            for sem in ("T", "V", "S"):
                if sem != own:
                    self.wait(e, self.tok(sem))
        for e in ("sync", "gpsimd", "vector", "scalar", "tensor"):
            for s in sorted(self.dma_sems):
                self.wait(e, self.tok(s))

    def dma(self, eng, out, in_, buf, **kw):
        sem = "d_" + buf
        self.dma_sems.add(sem)
        self.emit(eng, "dma_start", out=out, in_=in_, **kw)
        return self.mark(eng, sem, 16)

    def sb(self, name, shape, dtype):
        return self.es.enter_context(self.nc.sbuf_tensor(name, shape, dtype))

    def ps(self, name, shape, dtype=F32):
        return self.es.enter_context(self.nc.psum_tensor(name, shape, dtype))

    def const(self, name, arr):
        if name not in self._consts:
            self._consts[name] = self.nc.inline_tensor(np.ascontiguousarray(arr), name)
        return self._consts[name]

    def const_sb(self, name, arr):
        key = "sb_" + name
        if key not in self._consts:
            dram = self.nc.inline_tensor(np.ascontiguousarray(arr), name)
            dtm = {np.dtype(ml_dtypes.bfloat16): BF16, np.dtype(np.float32): F32,
                   np.dtype(np.float16): FP16}
            tile = self.sb(key, list(arr.shape), dtm[np.dtype(arr.dtype)])
            tok = self.dma("sync", out=tile[:], in_=dram.ap(), buf="c_" + name)
            self._consts[key] = (tile, tok)
        return self._consts[key]

    def build_block(self):
        for s in sorted(self.dma_sems):
            self.wait("sync", (s, self.cnt.get(s, 0)))
        names = self.sem_names + sorted(self.dma_sems)
        sems = {n: self.es.enter_context(self.nc.semaphore(n)) for n in names}

        def runner(E, stream):
            last = None
            for item in stream:
                kind = item[0]
                if kind == "op":
                    last = getattr(E, item[1])(**item[2])
                elif kind == "wait":
                    E.wait_ge(sems[item[1]], item[2])
                else:
                    last.then_inc(sems[item[1]], item[2])

        block = self.es.enter_context(self.nc.Block())
        for e in ENGINES:
            getattr(block, e)(lambda E, s=tuple(self.prog[e]): runner(E, s))


def ident_bf16(k):
    return k.const_sb("identity128", np.eye(128, dtype=ml_dtypes.bfloat16))


def ones_col128(k):
    return k.const_sb("ones_col128", np.ones((1, 128), dtype=ml_dtypes.bfloat16))


class LnState:
    def __init__(self, k, name, G, alloc=None, junk_from=None):
        self.k = k
        self.G = G
        alloc = alloc or k.sb
        self.ssq = alloc(name + "_ssq", [128, G], F32)
        self.sm = alloc(name + "_sm", [128, G], F32)
        self.rs = alloc(name + "_rs", [128, G], F32)
        self.nmr = alloc(name + "_nmr", [128, G], F32)
        self.tmp = alloc(name + "_tmp", [128, G], F32)
        self.junk = junk_from.junk if junk_from is not None else alloc(
            name + "_junk", [128, 1024], F32)
        if junk_from is not None:
            self.last_statS = getattr(junk_from, "last_statS", None)
        self.fin_tok = None

    def stats_tile(self, x_ap, g, n=D):
        k = self.k
        k.wait("scalar", self.fin_tok)
        k.wait("scalar", getattr(self, "last_statS", None))
        k.emit("scalar", "activation", out=self.junk[:, 0:n], in_=x_ap,
               func=AF.Square, accum_out=self.ssq[:, g:g + 1])
        tokS = k.mark("scalar", "S")
        self.last_statS = tokS
        k.emit("vector", "reduce_sum", out=self.sm[:, g:g + 1], in_=x_ap, axis=AX.X)
        tokV = k.mark("vector", "V")
        return tokS, tokV

    def finalize(self, n=D, eps=EPS, mean_zero=False):
        k = self.k
        c = 1.0 / n

        def dr():
            k.emit("vector", "drain")

        k.wait("vector", getattr(self, "last_statS", None))
        dr()
        if mean_zero:
            k.emit("vector", "tensor_scalar", out=self.nmr[:], in0=self.ssq[:],
                   scalar1=c, scalar2=eps, op0=ALU.mult, op1=ALU.add)
        else:
            k.emit("vector", "tensor_scalar", out=self.tmp[:], in0=self.sm[:],
                   scalar1=c, scalar2=None, op0=ALU.mult)
            dr()
            k.emit("vector", "tensor_mul", out=self.rs[:], in0=self.tmp[:], in1=self.tmp[:])
            dr()
            k.emit("vector", "scalar_tensor_tensor", out=self.rs[:], in0=self.ssq[:],
                   scalar=c, in1=self.rs[:], op0=ALU.mult, op1=ALU.subtract)
            dr()
            k.emit("vector", "tensor_scalar", out=self.nmr[:], in0=self.rs[:],
                   scalar1=eps, scalar2=None, op0=ALU.add)
        tokV = k.mark("vector", "V")
        k.wait("scalar", tokV)
        k.emit("scalar", "activation", out=self.rs[:], in_=self.nmr[:], func=AF.Sqrt)
        tokS = k.mark("scalar", "S")
        k.wait("vector", tokS)
        k.emit("vector", "reciprocal", out=self.rs[:], in_=self.rs[:])
        dr()
        k.emit("vector", "tensor_mul", out=self.ssq[:], in0=self.rs[:], in1=self.rs[:])
        dr()
        k.emit("vector", "tensor_mul", out=self.ssq[:], in0=self.ssq[:], in1=self.nmr[:])
        dr()
        k.emit("vector", "tensor_scalar", out=self.ssq[:], in0=self.ssq[:],
               scalar1=-0.5, scalar2=1.5, op0=ALU.mult, op1=ALU.add)
        dr()
        k.emit("vector", "tensor_mul", out=self.rs[:], in0=self.rs[:], in1=self.ssq[:])
        dr()
        if mean_zero:
            k.emit("vector", "memset", ap=self.nmr[:], constant=0.0)
        else:
            k.emit("vector", "scalar_tensor_tensor", out=self.nmr[:], in0=self.tmp[:],
                   scalar=-1.0, in1=self.rs[:], op0=ALU.mult, op1=ALU.mult)
        dr()
        self.fin_tok = k.mark("vector", "V")
        return self.fin_tok

    def apply(self, out_ap, x_ap, g, eng="scalar"):
        k = self.k
        if eng == "scalar":
            k.emit("scalar", "activation", out=out_ap, in_=x_ap, func=AF.Identity,
                   scale=self.rs[:, g:g + 1], bias=self.nmr[:, g:g + 1])
            return k.mark("scalar", "S")
        k.emit("vector", "tensor_scalar", out=out_ap, in0=x_ap,
               scalar1=self.rs[:, g:g + 1], scalar2=self.nmr[:, g:g + 1],
               op0=ALU.mult, op1=ALU.add)
        return k.mark("vector", "V")


def tm_to_fm(k, dst_ap, src_ap_fn, nchunks, psum, tok_src, evac_eng="vector", dt16=None):
    if dt16 is FP16:
        ident, itok = k.const_sb("identity128_f16", np.eye(128, dtype=np.float16))
    else:
        ident, itok = ident_bf16(k)
    k.wait("tensor", itok)
    k.wait("tensor", tok_src)
    for c in range(nchunks):
        k.emit("tensor", "transpose", out=psum[:, c * 128:(c + 1) * 128],
               in_=src_ap_fn(c), identity=ident[:])
    tokT = k.mark("tensor", "T")
    k.wait(evac_eng, tokT)
    k.emit(evac_eng, "tensor_copy", out=dst_ap,
           in_=psum[:, 0:nchunks * 128].rearrange("p (c t) -> p c t", t=128))
    return k.mark(evac_eng, "V" if evac_eng == "vector" else "S")


def phase_p3(k, src, dst, W):
    nc = k.nc
    TT = 512
    NT = NTOK // TT
    with contextlib.ExitStack() as ph:
        sb = lambda n, s, d: ph.enter_context(nc.sbuf_tensor(n, s, d))
        ps = lambda n, s, d=F32: ph.enter_context(nc.psum_tensor(n, s, d))
        Wsb = [sb(f"p3_w{i}", [128, DC, D], BF16) for i in range(3)]
        bias_sb = sb("p3_bias", [1, D], BF16)
        xa = [sb(f"p3_x{i}", [128, 4, D], F32) for i in range(2)]
        ubf = sb("p3_ubf", [128, 4, D], BF16)
        uT = sb("p3_uT", [128, DC, TT], BF16)
        t1 = sb("p3_t1", [128, DC, TT], BF16)
        t2 = sb("p3_t2", [128, DC, TT], BF16)
        z2b = [sb(f"p3_z2{i}", [128, 4, D], F32) for i in range(2)]
        outb = [sb(f"p3_out{i}", [128, 4, D], F32) for i in range(2)]
        ptr = ps("p3_ptr", [128, 1024], BF16)
        pmm = [ps(f"p3_pm{i}", [128, 512], F32) for i in range(2)]
        pt3 = [ps(f"p3_pt3_{i}", [128, 1024], F32) for i in range(2)]
        pwm3 = ph.enter_context(nc.psum_tensor("p3_pwm", [1, 1], F32))
        och3, ochtok3 = k.const_sb("ones_col128_f16", np.ones((128, 1), dtype=np.float16))
        k.wait("tensor", ochtok3)
        k.set_warm(pwm3[:], och3[:])
        ln_u = LnState(k, "p3lnu", 4, alloc=sb)
        ln_z = LnState(k, "p3lnz", 4, alloc=sb)

        for i, nm in enumerate(("weight", "aux_weight1", "aux_weight2")):
            for kc in range(DC):
                k.dma("gpsimd", out=Wsb[i][:, kc, :], in_=W[nm][kc * 128:(kc + 1) * 128, :], buf="p3w")
        k.dma("gpsimd", out=bias_sb[:], in_=W["bias"].rearrange("(o d) -> o d", o=1), buf="p3w")
        tok_w = k.tok("d_p3w")

        store_tok = {0: None, 1: None}
        z2_free = {0: None, 1: None}
        HS = {"ubf_free": None, "apm": None, "x": None, "atoks": None}
        pmm_ev = [None, None]
        pt3_ev = [None, None]

        def emit_head(t):
            buf = t % 2
            x = xa[buf]
            k.wait("sync", store_tok[buf])
            ld = k.dma("sync", out=x[:, :, :],
                       in_=src[t * TT:(t + 1) * TT, :].rearrange("(a p) d -> p a d", p=128),
                       buf=f"p3x{buf}")
            k.wait("scalar", ld)
            k.wait("vector", ld)
            k.warm()
            for st in range(4):
                ln_u.stats_tile(x[:, st, :], st)
            k.wait("vector", HS["apm"])
            fin = ln_u.finalize()
            k.wait("scalar", fin)
            k.wait("scalar", HS["ubf_free"])
            atoks = [ln_u.apply(ubf[:, st, :], x[:, st, :], st) for st in range(4)]
            HS["apm"] = atoks[-1]
            HS["x"] = x
            HS["atoks"] = atoks

        emit_head(0)
        for t in range(NT):
            buf = t % 2
            x = HS["x"]
            atoks = HS["atoks"]
            ob = outb[buf]
            z2 = z2b[buf]
            tr_ev = None
            for st in range(4):
                tr_ev = tm_to_fm(
                    k, uT[:, :, st * 128:(st + 1) * 128],
                    lambda c, st=st: ubf[:, st, c * 128:(c + 1) * 128],
                    DC, ptr, atoks[st], evac_eng="vector")
                k.wait("tensor", tr_ev)
            HS["ubf_free"] = tr_ev
            k.wait("tensor", tok_w)
            src_fm = uT
            prev_ev = tr_ev
            for li, dst_fm in ((0, t1), (1, t2)):
                k.wait("tensor", prev_ev)
                for nch in range(DC):
                    pb = pmm[nch % 2]
                    k.wait("tensor", pmm_ev[nch % 2])
                    for kc in range(DC):
                        k.emit("tensor", "matmul", out=pb[:],
                               lhsT=Wsb[li][:, kc, nch * 128:(nch + 1) * 128],
                               rhs=src_fm[:, kc, :],
                               start=(kc == 0), stop=(kc == DC - 1))
                    tokm = k.mark("tensor", "T")
                    k.wait("vector", tokm)
                    k.emit("vector", "tensor_copy", out=dst_fm[:, nch, :], in_=pb[:])
                    pmm_ev[nch % 2] = k.mark("vector", "V")
                prev_ev = pmm_ev[(DC - 1) % 2]
                src_fm = t1
            if t + 1 < NT:
                emit_head(t + 1)
            for st in range(4):
                pb = pt3[st % 2]
                k.wait("tensor", prev_ev)
                k.wait("tensor", pt3_ev[st % 2])
                for kc in range(DC):
                    for hh in range(2):
                        k.emit("tensor", "matmul", out=pb[:, hh * 512:(hh + 1) * 512],
                               lhsT=t2[:, kc, st * 128:(st + 1) * 128],
                               rhs=Wsb[2][:, kc, hh * 512:(hh + 1) * 512],
                               start=(kc == 0), stop=False)
                onescol, octok = ones_col128(k)
                k.wait("tensor", octok)
                for hh in range(2):
                    k.emit("tensor", "matmul", out=pb[:, hh * 512:(hh + 1) * 512],
                           lhsT=onescol[:],
                           rhs=bias_sb[0:1, hh * 512:(hh + 1) * 512],
                           start=False, stop=True)
                tokm = k.mark("tensor", "T")
                k.wait("scalar", tokm)
                k.wait("scalar", z2_free[buf])
                k.emit("scalar", "activation", out=z2[:, st, :], in_=pb[:], func=AF.Tanh)
                tokz = k.mark("scalar", "S")
                pt3_ev[st % 2] = tokz
                k.wait("vector", tokz)
                ln_z.stats_tile(z2[:, st, :], st)
            fin = ln_z.finalize()
            k.wait("scalar", fin)
            k.wait("vector", store_tok[buf])
            for st in range(4):
                ap_tok = ln_z.apply(z2[:, st, :], z2[:, st, :], st)
                k.wait("vector", ap_tok)
                k.emit("vector", "tensor_add", out=ob[:, st, :], in0=z2[:, st, :], in1=x[:, st, :])
            z2_free[buf] = k.mark("vector", "V")
            k.wait("sync", z2_free[buf])
            store_tok[buf] = k.dma(
                "sync", out=dst[t * TT:(t + 1) * TT, :].rearrange("(a p) d -> p a d", p=128),
                in_=ob[:, :, :], buf=f"p3o{buf}")
        k.set_warm(None)
        k.wait("sync", k.tok("d_p3o0"))
        k.wait("sync", k.tok("d_p3o1"))


def bcast_scalar(k, dst_sb, src_11, psum1, ident1_f32):
    onesf, otok = k.const_sb("ones_f32_row", np.ones((1, 128), dtype=np.float32))
    k.wait("tensor", otok)
    k.emit("tensor", "matmul", out=psum1[:], lhsT=onesf[:], rhs=src_11, start=True, stop=True)
    tk = k.mark("tensor", "T")
    k.wait("vector", tk)
    k.emit("vector", "tensor_copy", out=dst_sb, in_=psum1[:])
    return k.mark("vector", "V")


def part_sum_col(k, col_ap, out11_psum, guard=None):
    onesf, otok = k.const_sb("ones_col128_f32", np.ones((128, 1), dtype=np.float32))
    k.wait("tensor", otok)
    k.wait("tensor", guard)
    k.emit("tensor", "matmul", out=out11_psum, lhsT=onesf[:], rhs=col_ap,
           start=True, stop=True)
    return k.mark("tensor", "T")


def phase_p1(k, src, dst, W, scratch):
    nc = k.nc
    TT = 512
    NT = NTOK // TT
    T2 = 256
    N2 = NTOK // T2
    HC = INTER // 128
    B256 = [b for b in range(1, 16) if b % 8 != 0]
    NE = 2 * len(B256)
    with contextlib.ExitStack() as ph:
        sb = lambda n, s, d: ph.enter_context(nc.sbuf_tensor(n, s, d))
        ps = lambda n, s, d=F32: ph.enter_context(nc.psum_tensor(n, s, d))
        w1qT = sb("p1_w1qT", [128, DC, INTER], FP16)
        w2qT = sb("p1_w2qT", [128, HC, D], FP16)
        cw = sb("p1_cw", [128, HC, 3], F32)
        cb = sb("p1_cb", [128, HC], F32)
        sc = sb("p1_sc", [128, 8], F32)
        tiny = sb("p1_tiny", [1, 8], F32)
        uTedge = sb("p1_uTedge", [128, DC, NE], FP16)
        h_edge = sb("p1_hedge", [128, HC, NE], FP16)
        ptr = ps("p1_ptr", [128, 1024], FP16)
        pwm = ps("p1_pwm", [1, 1])
        ident1f, id1tok = k.const_sb("ident1_f32", np.ones((1, 1), dtype=np.float32))
        och, ochtok = k.const_sb("ones_col128_f16", np.ones((128, 1), dtype=np.float16))
        k.wait("tensor", ochtok)
        k.set_warm(pwm[:], och[:])
        bc_guard = [None]

        def bcast(dst_sb, src_11):
            onesf, otok = k.const_sb("ones_f32_row", np.ones((1, 128), dtype=np.float32))
            k.wait("tensor", otok)
            k.wait("tensor", bc_guard[0])
            k.emit("tensor", "matmul", out=ps1[:, 0:1], lhsT=onesf[:], rhs=src_11,
                   start=True, stop=True)
            tk = k.mark("tensor", "T")
            k.wait("vector", tk)
            k.emit("vector", "tensor_copy", out=dst_sb, in_=ps1[:, 0:1])
            bc_guard[0] = k.mark("vector", "V")
            return bc_guard[0]

        with contextlib.ExitStack() as wp:
            sbw = lambda n, s, d: wp.enter_context(nc.sbuf_tensor(n, s, d))
            psw = lambda n, s, d=F32: wp.enter_context(nc.psum_tensor(n, s, d))
            ps1 = psw("p1_ps1", [128, 4])
            pss = psw("p1_pss", [1, 4])
            wbuf = [sbw(f"p1_wbuf{i}", [128, 4, 1024], F32) for i in range(2)]
            qsc = wp.enter_context(contextlib.ExitStack())
            wq = [qsc.enter_context(nc.sbuf_tensor(f"p1_wq{i}", [128, 4096], FP16))
                  for i in range(2)]
            wqf = qsc.enter_context(nc.sbuf_tensor("p1_wqf", [128, 4096], F32))
            acc8 = qsc.enter_context(nc.sbuf_tensor("p1_acc8", [128, 9], F32))

            def wstats(wdram, nrows, ncols, col_out, gate, gate2=None):
                rpg = 4096 * 128 // ncols
                ngrp = nrows // rpg
                toks = [gate, gate]
                k.wait("scalar", gate2)
                prevS = None
                for g in range(ngrp):
                    b = g % 2
                    k.wait("sync", toks[b])
                    seg = wdram[g * rpg:(g + 1) * rpg, :]
                    if ncols == 1024:
                        seg = seg.rearrange("(a p) d -> p a d", p=128)
                    else:
                        seg = seg.rearrange("p (a d) -> p a d", d=1024)
                    ld = k.dma("sync", out=wbuf[b][:, :, :], in_=seg, buf=f"p1wb{b}")
                    k.wait("vector", ld)
                    k.warm()
                    k.emit("vector", "reduce_sum", out=acc8[:, g:g + 1],
                           in_=wbuf[b][:, :, :], axis=AX.XY, apply_absolute_value=True)
                    toks[b] = k.mark("vector", "V")
                    prevS = toks[b]
                k.emit("vector", "drain")
                k.emit("vector", "reduce_sum", out=acc8[:, 8:9], in_=acc8[:, 0:ngrp], axis=AX.X)
                vt = k.mark("vector", "V")
                ttk = part_sum_col(k, acc8[:, 8:9], pss[0:1, 0:1], guard=vt)
                k.wait("vector", ttk)
                k.emit("vector", "tensor_copy", out=tiny[0:1, col_out:col_out + 1],
                       in_=pss[0:1, 0:1])
                return k.mark("vector", "V")

            def wscales(nelem, col_in, cols, gate):
                k.wait("vector", gate)
                k.emit("vector", "tensor_scalar", out=tiny[0:1, col_in:col_in + 1],
                       in0=tiny[0:1, col_in:col_in + 1], scalar1=1.0 / nelem,
                       scalar2=1e-5, op0=ALU.mult, op1=ALU.max)
                k.emit("vector", "drain")
                k.emit("vector", "reciprocal", out=tiny[0:1, 7:8], in_=tiny[0:1, col_in:col_in + 1])
                k.emit("vector", "drain")
                vt = k.mark("vector", "V")
                k.wait("tensor", vt)
                bcast(sc[:, cols[0]:cols[0] + 1], tiny[0:1, col_in:col_in + 1])
                return bcast(sc[:, cols[1]:cols[1] + 1], tiny[0:1, 7:8])

            def wquant(wdram, nrows, ncols, sw_col, tr_groups, gate, sc_tok):
                nchunk = nrows // 128
                toks = [gate, gate]
                k.wait("scalar", sc_tok)
                evtok = None
                vtoks = {}
                tts = {}
                for r in range(nchunk):
                    b = r % 2
                    k.wait("sync", toks[b])
                    ld = k.dma("sync", out=wbuf[b][:, 0:ncols // 1024, :],
                               in_=wdram[r * 128:(r + 1) * 128, :].rearrange(
                                   "p (a d) -> p a d", d=1024),
                               buf=f"p1wb{b}")
                    k.wait("scalar", ld)
                    k.wait("scalar", vtoks.get(r - 1))
                    k.emit("scalar", "activation", out=wqf[:, 0:ncols],
                           in_=wbuf[b][:, 0:ncols // 1024, :],
                           func=AF.Identity, scale=sc[:, sw_col:sw_col + 1], bias=np.float32(MAGIC))
                    stok = k.mark("scalar", "S")
                    toks[b] = stok
                    k.wait("vector", stok)
                    k.wait("vector", tts.get(r - 2))
                    k.emit("vector", "tensor_scalar", out=wqf[:, 0:ncols], in0=wqf[:, 0:ncols],
                           scalar1=-MAGIC, scalar2=1.0, op0=ALU.add, op1=ALU.min)
                    k.emit("vector", "tensor_scalar", out=wq[b][:, 0:ncols], in0=wqf[:, 0:ncols],
                           scalar1=-1.0, scalar2=None, op0=ALU.max)
                    vtoks[r] = k.mark("vector", "V")
                    for g, dst_ap in tr_groups(r):
                        evtok = tm_to_fm(k, dst_ap,
                                         lambda c, g=g, b=b: wq[b][:, (g * 8 + c) * 128:(g * 8 + c + 1) * 128],
                                         8, ptr, vtoks[r], evac_eng="vector", dt16=FP16)
                        k.wait("tensor", evtok)
                    tts[r] = k.tok("T")
                return evtok, toks, vtoks[nchunk - 1]

            k.phase_mark("p1_wq")
            s1 = wstats(W["w1"], INTER, 1024, 0, None)
            sct1 = wscales(INTER * D, 0, (0, 1), s1)

            def w1_groups(r):
                yield 0, w1qT[:, :, r * 128:(r + 1) * 128]
            w1ev, w1casts, w1clamp = wquant(W["w1"], INTER, 1024, 1, w1_groups, s1, sct1)

            s2 = wstats(W["w2"], D, 4096, 1,
                        max((t for t in w1casts if t), key=lambda t: t[1]),
                        gate2=w1clamp)
            sct2 = wscales(INTER * D, 1, (2, 3), s2)

            def w2_groups(r):
                for g in range(4):
                    yield g, w2qT[:, g * 8:(g + 1) * 8, r * 128:(r + 1) * 128]
            w2ev, _, _ = wquant(W["w2"], D, 4096, 3, w2_groups, s2, sct2)

            qsc.close()
            k.dma("sync", out=cw[:, :, :], in_=W["conv_w"].rearrange("(c p) t -> p c t", p=128), buf="p1cw")
            k.dma("sync", out=cb[:, :], in_=W["conv_b"].rearrange("(c p) -> p c", p=128), buf="p1cw", allow_slow_non_contiguous=True)
            tok_cw = k.tok("d_p1cw")
            k.wait("scalar", tok_cw)
            k.wait("scalar", sct1)
            k.emit("scalar", "activation",
                   out=cw[:, :, 1:2].rearrange("p c o -> p (c o)"),
                   in_=cw[:, :, 1:2].rearrange("p c o -> p (c o)"),
                   func=AF.Identity, scale=sc[:, 0:1])
            tok_cw = k.mark("scalar", "S")

            k.phase_mark("p1_l1")
            qjunk = sbw("p1_qjunk", [128, D], F32)
            uqd = sbw("p1_uqd", [128, 4, D], FP16)
            mx = sbw("p1_mx", [128, 4], F32)
            mn = sbw("p1_mn", [128, 4], F32)
            rsu = sbw("p1_rsu", [128, 4], F32)
            c1 = sbw("p1_c1", [128, 4], F32)
            c0 = sbw("p1_c0", [128, 4], F32)
            ln_u = LnState(k, "p1lnu", 4, alloc=sbw)

            x_free = {0: w2ev, 1: w2ev}
            uqd_store = None
            for t in range(NT):
                b = t % 2
                x = wbuf[b]
                k.wait("sync", x_free[b])
                ld = k.dma("sync", out=x[:, :, :],
                           in_=src[t * TT:(t + 1) * TT, :].rearrange("(a p) d -> p a d", p=128),
                           buf=f"p1wb{b}")
                k.wait("scalar", ld)
                k.wait("vector", ld)
                k.warm()
                for st in range(4):
                    ln_u.stats_tile(x[:, st, :], st)
                    k.emit("vector", "reduce_max", out=mx[:, st:st + 1], in_=x[:, st, :],
                           axis=AX.X)
                    k.emit("vector", "tensor_reduce", out=mn[:, st:st + 1], in_=x[:, st, :],
                           axis=AX.X, op=ALU.min)
                    k.mark("vector", "V")
                fin = ln_u.finalize()
                k.emit("vector", "drain")
                k.emit("vector", "scalar_tensor_tensor", out=mx[:], in0=mx[:],
                       scalar=1.0, in1=ln_u.rs[:], op0=ALU.mult, op1=ALU.mult)
                k.emit("vector", "scalar_tensor_tensor", out=mn[:], in0=mn[:],
                       scalar=1.0, in1=ln_u.rs[:], op0=ALU.mult, op1=ALU.mult)
                k.emit("vector", "drain")
                k.emit("vector", "tensor_add", out=mx[:], in0=mx[:], in1=ln_u.nmr[:])
                k.emit("vector", "tensor_add", out=mn[:], in0=mn[:], in1=ln_u.nmr[:])
                k.emit("vector", "drain")
                k.emit("vector", "scalar_tensor_tensor", out=mx[:], in0=mn[:],
                       scalar=-1.0, in1=mx[:], op0=ALU.mult, op1=ALU.max)
                k.emit("vector", "drain")
                k.emit("vector", "tensor_scalar", out=rsu[:], in0=mx[:],
                       scalar1=1e-5, scalar2=1.0 / 127.0, op0=ALU.max, op1=ALU.mult)
                k.emit("vector", "drain")
                k.emit("vector", "reciprocal", out=mn[:], in_=rsu[:])
                k.emit("vector", "drain")
                k.emit("vector", "tensor_mul", out=c1[:], in0=ln_u.rs[:], in1=mn[:])
                k.emit("vector", "tensor_mul", out=c0[:], in0=ln_u.nmr[:], in1=mn[:])
                k.emit("vector", "drain")
                k.emit("vector", "tensor_scalar", out=c0[:], in0=c0[:],
                       scalar1=MAGIC, scalar2=None, op0=ALU.add)
                k.emit("vector", "drain")
                qtok = k.mark("vector", "V")
                k.wait("scalar", qtok)
                k.wait("vector", uqd_store)
                k.wait("vector", k.tok("d_p1ue"))
                k.warm()
                for st in range(4):
                    k.emit("scalar", "activation", out=qjunk[:], in_=x[:, st, :],
                           func=AF.Identity, scale=c1[:, st:st + 1], bias=c0[:, st:st + 1])
                    stk = k.mark("scalar", "S")
                    k.wait("vector", stk)
                    k.emit("vector", "tensor_scalar", out=uqd[:, st, :], in0=qjunk[:],
                           scalar1=-MAGIC, scalar2=rsu[:, st:st + 1], op0=ALU.add, op1=ALU.mult)
                    vst = k.mark("vector", "V")
                    k.wait("scalar", vst)
                vtok = k.tok("V")
                k.wait("sync", vtok)
                uqd_store = k.dma("sync",
                                  out=scratch[t * TT:(t + 1) * TT, :].rearrange("(a p) d -> p a d", p=128),
                                  in_=uqd[:, :, :], buf="p1uq")
                x_free[b] = uqd_store
                for i, bnd in enumerate(B256):
                    for tk_, slot in ((256 * bnd - 1, 2 * i), (256 * bnd, 2 * i + 1)):
                        if t * TT <= tk_ < (t + 1) * TT:
                            k.wait("gpsimd", uqd_store)
                            k.dma("gpsimd",
                                  out=uTedge[:, :, slot:slot + 1].rearrange("p c o -> p (c o)"),
                                  in_=scratch[tk_:tk_ + 1, :].rearrange("o (c p) -> (o p) c", p=128),
                                  buf="p1ue", allow_slow_non_contiguous=True)
            tok_uqd_all = k.tok("d_p1uq")
            tok_edge = k.tok("d_p1ue")

            pse = psw("p1_pse", [128, NE])
            k.wait("tensor", tok_edge)
            k.wait("tensor", w1ev)
            pe_ev = None
            for ic in range(HC):
                k.wait("tensor", pe_ev)
                for dc in range(DC):
                    k.emit("tensor", "matmul", out=pse[:],
                           lhsT=w1qT[:, dc, ic * 128:(ic + 1) * 128],
                           rhs=uTedge[:, dc, :],
                           start=(dc == 0), stop=(dc == DC - 1))
                mtk = k.mark("tensor", "T")
                k.wait("scalar", mtk)
                k.wait("scalar", sct1)
                k.emit("scalar", "activation", out=h_edge[:, ic, :], in_=pse[:],
                       func=AF.Identity, scale=sc[:, 0:1])
                pe_ev = k.mark("scalar", "S")
            h_edge_tok = pe_ev
        k.stage_join()

        k.phase_mark("p1_l2")
        uqdl = [sb(f"p1_uqdl{i}", [128, 2, D], FP16) for i in range(2)]
        x2 = sb("p1_x2", [128, 2, D], F32)
        uqdT = sb("p1_uqdT", [128, DC, T2], FP16)
        ht4 = sb("p1_ht4", [128, 4, T2], FP16)
        cvt4 = sb("p1_cvt4", [128, 4, T2], FP16)
        h2 = [sb(f"p1_h2{i}", [128, HC, T2], FP16) for i in range(2)]
        am4 = [sb(f"p1_am4{i}", [128, 4, T2], FP16) for i in range(2)]
        am2b = sb("p1_am2b", [128, T2], FP16)
        am2 = sb("p1_am2", [128, T2], F32)
        su2tm = sb("p1_su2tm", [128, 2], F32)
        su2bf = sb("p1_su2bf", [1, T2], F32)
        su2b = sb("p1_su2b", [128, T2], F32)
        uq2m3 = sb("p1_uq2m3", [128, 3, T2], F32)
        rsu2T = sb("p1_rsu2T", [128, 2], F32)
        rsud = sb("p1_rsud", [128, 2], F32)
        z = sb("p1_z", [128, 2, D], BF16)
        ln_z = LnState(k, "p1lnz", 2, alloc=sb)
        psh = [ps(f"p1_psh{i}", [128, T2]) for i in range(3)]
        psd = ps("p1_psd", [128, 1024])
        pst = ps("p1_pst", [128, T2])
        onesf32, onesf32tok = k.const_sb("ones_f32_row", np.ones((1, 128), dtype=np.float32))

        identf, iftok = k.const_sb("identity128_f32", np.eye(128, dtype=np.float32))
        k.wait("tensor", iftok)
        k.wait("vector", h_edge_tok)
        k.wait("sync", tok_uqd_all)
        ld_free = {0: None, 1: None}
        G = {"out_free": None, "x2_free": None, "pst_free": None, "z_free": None,
             "psd_ev": None, "psb_ev": None, "last_rne": None, "gen": None}
        h2_free_p = {0: None, 1: None}
        am_free = {0: None, 1: None}
        Am = {}
        B2m = {}
        Cm = {}
        mtkd = {}
        trtoks = {}

        def back_half(t):
            b = t % 2
            h2t = h2[b]
            am4t = am4[b]
            k.emit("vector", "scalar_tensor_tensor", out=am2b[:], in0=am4t[:, 0, :],
                   scalar=1.0, in1=am4t[:, 1, :], op0=ALU.mult, op1=ALU.max)
            k.emit("vector", "scalar_tensor_tensor", out=am2[:], in0=am4t[:, 2, :],
                   scalar=1.0, in1=am4t[:, 3, :], op0=ALU.mult, op1=ALU.max)
            k.emit("vector", "scalar_tensor_tensor", out=am2[:], in0=am2b[:],
                   scalar=1.0, in1=am2[:], op0=ALU.mult, op1=ALU.max)
            am2tok = k.mark("vector", "V")
            yield
            k.wait("tensor", am2tok)
            k.wait("tensor", iftok)
            k.wait("tensor", G["pst_free"])
            for j in range(2):
                k.emit("tensor", "transpose", out=pst[:, j * 128:(j + 1) * 128],
                       in_=am2[:, j * 128:(j + 1) * 128], identity=identf[:])
            ttk = k.mark("tensor", "T")
            am_free[b] = ttk
            yield
            k.wait("vector", ttk)
            for j in range(2):
                k.emit("vector", "reduce_max", out=rsu2T[:, j:j + 1],
                       in_=pst[:, j * 128:(j + 1) * 128], axis=AX.X)
            k.emit("vector", "tensor_scalar", out=rsu2T[:], in0=rsu2T[:],
                   scalar1=1e-5, scalar2=1.0 / 127.0, op0=ALU.max, op1=ALU.mult)
            k.emit("vector", "tensor_scalar", out=rsud[:], in0=rsu2T[:],
                   scalar1=sc[:, 2:3], scalar2=None, op0=ALU.mult)
            k.emit("vector", "reciprocal", out=su2tm[:], in_=rsu2T[:])
            psd_ev0 = k.mark("vector", "V")
            yield
            k.wait("tensor", psd_ev0)
            for j in range(2):
                k.emit("tensor", "transpose", out=pst[0:1, j * 128:(j + 1) * 128],
                       in_=su2tm[:, j:j + 1], identity=identf[:])
            ttk2 = k.mark("tensor", "T")
            k.wait("vector", ttk2)
            for j in range(2):
                k.emit("vector", "tensor_copy", out=su2bf[0:1, j * 128:(j + 1) * 128],
                       in_=pst[0:1, j * 128:(j + 1) * 128])
            sutok = k.mark("vector", "V")
            G["pst_free"] = sutok
            yield
            k.wait("tensor", sutok)
            k.wait("tensor", onesf32tok)
            k.wait("tensor", G["psb_ev"])
            k.wait("tensor", G["psd_ev"])
            k.emit("tensor", "matmul", out=psd[:, 0:T2], lhsT=onesf32[:],
                   rhs=su2bf[:], start=True, stop=True)
            btk = k.mark("tensor", "T")
            k.wait("vector", btk)
            k.emit("vector", "tensor_copy", out=su2b[:], in_=psd[:, 0:T2])
            G["psb_ev"] = k.mark("vector", "V")
            yield
            k.wait("vector", G["last_rne"])
            stk1 = {}
            stk2 = {}
            mlt = {}
            for jj in range(HC + 2):
                if jj < HC:
                    k.wait("vector", G["psb_ev"])
                    k.wait("vector", stk2.get(jj - 3))
                    k.emit("vector", "tensor_mul", out=uq2m3[:, jj % 3, :],
                           in0=h2t[:, jj, :], in1=su2b[:])
                    mlt[jj] = k.mark("vector", "V")
                j1 = jj - 1
                if 0 <= j1 < HC:
                    k.wait("scalar", mlt[j1])
                    k.emit("scalar", "activation", out=uq2m3[:, j1 % 3, :],
                           in_=uq2m3[:, j1 % 3, :], func=AF.Identity,
                           bias=np.float32(MAGIC))
                    stk1[j1] = k.mark("scalar", "S")
                j2 = jj - 2
                if 0 <= j2 < HC:
                    k.emit("scalar", "activation", out=h2t[:, j2, :],
                           in_=uq2m3[:, j2 % 3, :], func=AF.Identity,
                           bias=np.float32(-MAGIC))
                    stk2[j2] = k.mark("scalar", "S")
                yield
            h2_rne = stk2[HC - 1]
            G["last_rne"] = k.tok("S")
            k.wait("sync", G["x2_free"])
            xl = k.dma("sync", out=x2[:, :, :],
                       in_=src[t * T2:(t + 1) * T2, :].rearrange("(a p) d -> p a d", p=128),
                       buf="p1y")
            k.wait("tensor", h2_rne)
            dmtk = None
            for s4 in range(2):
                k.wait("tensor", G["psd_ev"])
                for hcg in range(0, HC, 8):
                    if hcg:
                        yield
                    for hc in range(hcg, hcg + 8):
                        for hh in range(2):
                            k.emit("tensor", "matmul",
                                   out=psd[:, hh * 512:(hh + 1) * 512],
                                   lhsT=h2t[:, hc, s4 * 128:(s4 + 1) * 128],
                                   rhs=w2qT[:, hc, hh * 512:(hh + 1) * 512],
                                   start=(hc == 0), stop=(hc == HC - 1))
                dmtk = k.mark("tensor", "T")
                k.wait("scalar", dmtk)
                k.wait("scalar", G["z_free"])
                k.emit("scalar", "activation", out=z[:, s4, :], in_=psd[:],
                       func=AF.Identity, scale=rsud[:, s4:s4 + 1])
                ztk = k.mark("scalar", "S")
                G["psd_ev"] = ztk
                k.wait("vector", ztk)
                ln_z.stats_tile(z[:, s4, :], s4)
                yield
            h2_free_p[b] = dmtk
            finz = ln_z.finalize()
            k.wait("scalar", finz)
            k.wait("scalar", xl)
            k.wait("vector", xl)
            k.wait("vector", G["out_free"])
            yield
            for s4 in range(2):
                atk = ln_z.apply(z[:, s4, :], z[:, s4, :], s4)
                k.wait("vector", atk)
                k.emit("vector", "tensor_add", out=x2[:, s4, :], in0=z[:, s4, :],
                       in1=x2[:, s4, :])
            ot = k.mark("vector", "V")
            G["z_free"] = ot
            k.wait("sync", ot)
            G["out_free"] = k.dma("sync",
                                  out=dst[t * T2:(t + 1) * T2, :].rearrange("(a p) d -> p a d", p=128),
                                  in_=x2[:, :, :], buf="p1o")
            G["x2_free"] = G["out_free"]

        def pump(n):
            g = G["gen"]
            if g is None:
                return
            for _ in range(n):
                try:
                    next(g)
                except StopIteration:
                    G["gen"] = None
                    return

        for t in range(N2):
            b = t % 2
            uq_t = uqdl[b]
            k.wait("sync", ld_free[b])
            ld = k.dma("sync", out=uq_t[:, :, :],
                       in_=scratch[t * T2:(t + 1) * T2, :].rearrange("(a p) d -> p a d", p=128),
                       buf=f"p1ul{b}")
            trtok = None
            for st in range(2):
                trtok = tm_to_fm(k, uqdT[:, :, st * 128:(st + 1) * 128],
                                 lambda c, st=st: uq_t[:, st, c * 128:(c + 1) * 128],
                                 DC, ptr, ld, evac_eng="vector", dt16=FP16)
                k.wait("tensor", trtok)
            ld_free[b] = trtok
            seqpos = t % 8
            g0 = t * HC
            k.wait("vector", h2_free_p[b])
            k.wait("scalar", h2_free_p[b])
            k.wait("scalar", tok_cw)
            k.wait("vector", am_free[b])
            k.emit("vector", "memset",
                   ap=am4[b][:, :, :].rearrange("p a t -> p (a t)"), constant=0.0)
            k.mark("vector", "V")
            if t >= 1:
                G["gen"] = back_half(t - 1)
            for ii in range(HC + 4):
                ic = ii
                gic = g0 + ic
                if ic < HC:
                    k.wait("tensor", trtok)
                    k.wait("tensor", Am.get(gic - 3))
                    pb = psh[gic % 3]
                    for dc in range(DC):
                        k.emit("tensor", "matmul", out=pb[:],
                               lhsT=w1qT[:, dc, ic * 128:(ic + 1) * 128],
                               rhs=uqdT[:, dc, :],
                               start=(dc == 0), stop=(dc == DC - 1))
                    mtkd[gic] = k.mark("tensor", "T")
                    k.wait("scalar", mtkd[gic])
                    k.wait("scalar", B2m.get(gic - 4))
                    k.emit("scalar", "activation", out=ht4[:, gic % 4, :],
                           in_=pb[:], func=AF.Identity, scale=sc[:, 0:1])
                    k.wait("scalar", Cm.get(gic - 4))
                    k.emit("scalar", "activation", out=cvt4[:, gic % 4, :],
                           in_=pb[:], func=AF.Identity, scale=cw[:, ic, 1:2],
                           bias=cb[:, ic:ic + 1])
                    Am[gic] = k.mark("scalar", "S")
                jb1 = ii - 2
                gb1 = g0 + jb1
                if 0 <= jb1 < HC:
                    k.wait("vector", Am[gb1])
                    k.emit("vector", "scalar_tensor_tensor",
                           out=cvt4[:, gb1 % 4, 1:T2],
                           in0=ht4[:, gb1 % 4, 0:T2 - 1], scalar=cw[:, jb1, 0:1],
                           in1=cvt4[:, gb1 % 4, 1:T2], op0=ALU.mult, op1=ALU.add)
                    if seqpos > 0:
                        e0 = 2 * B256.index(t)
                        k.emit("vector", "scalar_tensor_tensor",
                               out=cvt4[:, gb1 % 4, 0:1],
                               in0=h_edge[:, jb1, e0:e0 + 1], scalar=cw[:, jb1, 0:1],
                               in1=cvt4[:, gb1 % 4, 0:1], op0=ALU.mult, op1=ALU.add)
                    k.mark("vector", "V")
                jb2 = ii - 3
                gb2 = g0 + jb2
                if 0 <= jb2 < HC:
                    k.emit("vector", "scalar_tensor_tensor",
                           out=cvt4[:, gb2 % 4, 0:T2 - 1],
                           in0=ht4[:, gb2 % 4, 1:T2], scalar=cw[:, jb2, 2:3],
                           in1=cvt4[:, gb2 % 4, 0:T2 - 1], op0=ALU.mult, op1=ALU.add)
                    if seqpos < 7:
                        e1 = 2 * B256.index(t + 1) + 1
                        k.emit("vector", "scalar_tensor_tensor",
                               out=cvt4[:, gb2 % 4, T2 - 1:T2],
                               in0=h_edge[:, jb2, e1:e1 + 1], scalar=cw[:, jb2, 2:3],
                               in1=cvt4[:, gb2 % 4, T2 - 1:T2], op0=ALU.mult, op1=ALU.add)
                    B2m[gb2] = k.mark("vector", "V")
                    k.wait("scalar", B2m[gb2])
                    k.emit("scalar", "activation", out=h2[b][:, jb2, :],
                           in_=cvt4[:, gb2 % 4, :], func=AF.Silu)
                    Cm[gb2] = k.mark("scalar", "S")
                jd = ii - 4
                gd = g0 + jd
                if 0 <= jd < HC:
                    k.wait("vector", Cm[gd])
                    k.emit("vector", "scalar_tensor_tensor", out=am4[b][:, jd % 2, :],
                           in0=h2[b][:, jd, :], scalar=1.0, in1=am4[b][:, jd % 2, :],
                           op0=ALU.mult, op1=ALU.max)
                    k.emit("vector", "scalar_tensor_tensor", out=am4[b][:, 2 + jd % 2, :],
                           in0=h2[b][:, jd, :], scalar=-1.0, in1=am4[b][:, 2 + jd % 2, :],
                           op0=ALU.mult, op1=ALU.max)
                    k.mark("vector", "V")
                pump(2)
            pump(1 << 20)
        for _ in back_half(N2 - 1):
            pass
        k.set_warm(None)
        k.wait("sync", k.tok("d_p1o"))


def phase_p0(k, src, ydram, dst, W):
    nc = k.nc
    NCH = T // 128
    H0 = 8
    identf, iftok = k.const_sb("identity128_f32", np.eye(128, dtype=np.float32))
    identh, ihtok = k.const_sb("identity128_f16", np.eye(128, dtype=np.float16))
    mT = np.triu(np.ones((128, 128), dtype=np.float16))
    maskT, mttok = k.const_sb("p0_maskT", mT)

    with contextlib.ExitStack() as ph:
        sb = lambda n, s, d: ph.enter_context(nc.sbuf_tensor(n, s, d))
        ps = lambda n, s, d=F32: ph.enter_context(nc.psum_tensor(n, s, d))
        och0, ochtok0 = k.const_sb("ones_col128_f16", np.ones((128, 1), dtype=np.float16))
        k.wait("tensor", ochtok0)
        anw = sb("p0_anw", [128, DC], F32)
        onw = sb("p0_onw", [128, 1], F32)
        mlpw = sb("p0_mlpw", [128, DC], F32)
        k.dma("sync", out=anw[:], in_=W["attn_norm_w"].rearrange("(c p) -> p c", p=128),
              buf="p0nw", allow_slow_non_contiguous=True)
        k.dma("sync", out=onw[:], in_=W["out_norm_w"].rearrange("(o p) -> p o", p=128),
              buf="p0nw")
        k.dma("sync", out=mlpw[:], in_=W["mlp_norm_w"].rearrange("(c p) -> p c", p=128),
              buf="p0nw", allow_slow_non_contiguous=True)
        tok_nw = k.tok("d_p0nw")

        for s in range(NSEQ):
            k.stage_join()
            tok0 = s * T
            with contextlib.ExitStack() as sq:
                abc = sq.enter_context(contextlib.ExitStack())
                sbs = lambda n, s2, d: abc.enter_context(nc.sbuf_tensor(n, s2, d))
                lnu = LnState(k, f"p0lnu{s}", NCH, alloc=sbs)
                ch1 = sbs(f"p0_ch1{s}", [128, NCH], F32)
                ch0 = sbs(f"p0_ch0{s}", [128, NCH], F32)
                hT_scope = contextlib.ExitStack()
                hT = hT_scope.enter_context(
                    nc.sbuf_tensor(f"p0_hT{s}", [128, DC, T], FP16, side="right"))
                q16 = sbs(f"p0_q{s}", [128, DC, T], FP16)
                k16 = sbs(f"p0_k{s}", [128, DC, T], FP16)
                v16 = sbs(f"p0_v{s}", [128, NCH, D], FP16)

                k.phase_mark(f"p0A{s}")
                with contextlib.ExitStack() as sa:
                    sba = lambda n, s2, d: sa.enter_context(nc.sbuf_tensor(n, s2, d))
                    xa = [sba(f"p0a_x{i}_{s}", [128, D], F32) for i in range(2)]
                    htm = sba(f"p0a_htm_{s}", [128, D], FP16)
                    ptr = sa.enter_context(nc.psum_tensor(f"p0a_ptr{s}", [128, 1024], FP16))
                    pwma = sa.enter_context(nc.psum_tensor(f"p0a_pwm{s}", [1, 1], F32))
                    k.set_warm(pwma[:], och0[:])
                    xfree = {0: None, 1: None}
                    for ct in range(NCH):
                        b = ct % 2
                        k.wait("sync", xfree[b])
                        ld = k.dma("sync", out=xa[b][:],
                                   in_=src[tok0 + ct * 128: tok0 + (ct + 1) * 128, :],
                                   buf=f"p0ax{b}")
                        k.wait("scalar", ld)
                        k.wait("vector", ld)
                        k.warm()
                        ts_, tv_ = lnu.stats_tile(xa[b][:], ct)
                        k.wait("sync", ts_)
                        xfree[b] = tv_
                    fin = lnu.finalize()
                    k.emit("vector", "tensor_mul", out=ch1[:], in0=lnu.rs[:], in1=lnu.rs[:])
                    k.emit("vector", "tensor_scalar", out=ch0[:], in0=ch1[:],
                           scalar1=-EPS, scalar2=1.0 + EPS, op0=ALU.mult, op1=ALU.add)
                    tv = k.mark("vector", "V")
                    k.wait("scalar", tv)
                    k.emit("scalar", "activation", out=ch1[:], in_=ch0[:], func=AF.Sqrt)
                    ts_ = k.mark("scalar", "S")
                    k.wait("vector", ts_)
                    k.emit("vector", "reciprocal", out=ch1[:], in_=ch1[:])
                    k.emit("vector", "tensor_mul", out=lnu.tmp[:], in0=ch1[:], in1=ch1[:])
                    k.emit("vector", "tensor_mul", out=lnu.tmp[:], in0=lnu.tmp[:], in1=ch0[:])
                    k.emit("vector", "tensor_scalar", out=lnu.tmp[:], in0=lnu.tmp[:],
                           scalar1=-0.5, scalar2=1.5, op0=ALU.mult, op1=ALU.add)
                    k.emit("vector", "tensor_mul", out=ch1[:], in0=ch1[:], in1=lnu.tmp[:])
                    k.emit("vector", "tensor_mul", out=ch0[:], in0=lnu.nmr[:], in1=ch1[:])
                    k.emit("vector", "tensor_mul", out=ch1[:], in0=lnu.rs[:], in1=ch1[:])
                    chtok = k.mark("vector", "V")
                    k.wait("scalar", chtok)
                    xfree = {0: chtok, 1: chtok}
                    trtok = None
                    for ct in range(NCH):
                        b = ct % 2
                        k.wait("sync", xfree[b])
                        ld = k.dma("sync", out=xa[b][:],
                                   in_=src[tok0 + ct * 128: tok0 + (ct + 1) * 128, :],
                                   buf=f"p0ax{b}")
                        k.wait("scalar", ld)
                        k.wait("scalar", trtok)
                        k.emit("scalar", "activation", out=htm[:], in_=xa[b][:],
                               func=AF.Identity, scale=ch1[:, ct:ct + 1], bias=ch0[:, ct:ct + 1])
                        atok = k.mark("scalar", "S")
                        xfree[b] = atok
                        trtok = tm_to_fm(k, hT[:, :, ct * 128:(ct + 1) * 128],
                                         lambda c: htm[:, c * 128:(c + 1) * 128],
                                         DC, ptr, atok, evac_eng="vector", dt16=FP16)
                        k.wait("tensor", trtok)

                k.phase_mark(f"p0B{s}")
                k.stage_join()
                k.set_warm(None)
                with contextlib.ExitStack() as sb_:
                    sbb = lambda n, s2, d: sb_.enter_context(nc.sbuf_tensor(n, s2, d))
                    psb_ = lambda n, s2, d=F32: sb_.enter_context(nc.psum_tensor(n, s2, d))
                    wldb = [sbb(f"p0b_wld{i}_{s}", [128, 4, 1024], F32) for i in range(2)]
                    wf = sbb(f"p0b_wf_{s}", [128, DC, 1024], FP16)
                    pqk = [psb_(f"p0b_pqk{i}_{s}", [128, 1024]) for i in range(2)]
                    wld_free = {0: None, 1: None}

                    def load_weight_f16(wdram, scale_col_fn):
                        k.wait("scalar", k.tok("T"))
                        for g in range(2):
                            b = g % 2
                            k.wait("sync", wld_free[b])
                            ld = k.dma("sync", out=wldb[b][:, :, :],
                                       in_=wdram[g * 512:(g + 1) * 512, :].rearrange("(a p) d -> p a d", p=128),
                                       buf=f"p0w{b}")
                            k.wait("scalar", ld)
                            k.wait("scalar", tok_nw)
                            for a in range(4):
                                kc = g * 4 + a
                                k.emit("scalar", "activation", out=wf[:, kc, :],
                                       in_=wldb[b][:, a, :], func=AF.Identity,
                                       scale=scale_col_fn(kc))
                            wld_free[b] = k.mark("scalar", "S")
                        return k.tok("S")

                    pev = [None, None]
                    for dst16, wnm in ((q16, "Wq"), (k16, "Wk")):
                        ctok = load_weight_f16(W[wnm], lambda kc: anw[:, kc:kc + 1])
                        k.wait("tensor", ctok)
                        for nch in range(DC):
                            for th in range(2):
                                pb = pqk[th]
                                k.wait("tensor", pev[th])
                                for ts4 in range(2):
                                    for kc in range(DC):
                                        k.emit("tensor", "matmul",
                                               out=pb[:, ts4 * 512:(ts4 + 1) * 512],
                                               lhsT=wf[:, kc, nch * 128:(nch + 1) * 128],
                                               rhs=hT[:, kc, th * 1024 + ts4 * 512: th * 1024 + (ts4 + 1) * 512],
                                               start=(kc == 0), stop=(kc == DC - 1))
                                mtk = k.mark("tensor", "T")
                                k.wait("vector", mtk)
                                k.emit("vector", "tensor_copy",
                                       out=dst16[:, nch, th * 1024:(th + 1) * 1024], in_=pb[:])
                                pev[th] = k.mark("vector", "V")
                    ctok = load_weight_f16(W["Wv"], lambda kc: anw[:, kc:kc + 1])
                    k.wait("tensor", ctok)
                    for tc in range(NCH):
                        pb = pqk[tc % 2]
                        k.wait("tensor", pev[tc % 2])
                        for kc in range(DC):
                            for hh in range(2):
                                k.emit("tensor", "matmul",
                                       out=pb[:, hh * 512:(hh + 1) * 512],
                                       lhsT=hT[:, kc, tc * 128:(tc + 1) * 128],
                                       rhs=wf[:, kc, hh * 512:(hh + 1) * 512],
                                       start=(kc == 0), stop=(kc == DC - 1))
                        mtk = k.mark("tensor", "T")
                        k.wait("vector", mtk)
                        k.emit("vector", "tensor_copy", out=v16[:, tc, :], in_=pb[:])
                        pev[tc % 2] = k.mark("vector", "V")

                hT_scope.close()
                k.stage_join()
                k.phase_mark(f"p0C{s}")
                with contextlib.ExitStack() as sc_:
                    sbc = lambda n, s2, d: sc_.enter_context(nc.sbuf_tensor(n, s2, d))
                    psc = lambda n, s2, d=F32: sc_.enter_context(nc.psum_tensor(n, s2, d))
                    Wo16 = sbc(f"p0c_wo_{s}", [128, DC, 1024], FP16)
                    S16 = sbc(f"p0c_s16_{s}", [128, H0, 128], FP16)
                    Sfp = sbc(f"p0c_sfp_{s}", [128, H0, 128], F32)
                    at16 = sbc(f"p0c_at16_{s}", [128, H0, 128], FP16)
                    kt16 = sbc(f"p0c_kt16_{s}", [128, H0, 128], FP16)
                    o_n = sbc(f"p0c_on_{s}", [128, H0, 128], FP16)
                    onT = sbc(f"p0c_onT_{s}", [128, H0, 128], FP16)
                    xa = [sbc(f"p0c_x{i}_{s}", [128, D], F32) for i in range(2)]
                    y0t = sbc(f"p0c_y0_{s}", [128, D], F32)
                    y1t = sbc(f"p0c_y1_{s}", [128, D], F32)
                    sso = sbc(f"p0c_sso_{s}", [128, H0], F32)
                    rso = sbc(f"p0c_rso_{s}", [128, H0], F32)
                    ojnk = sbc(f"p0c_ojnk_{s}", [128, H0, 128], F32)
                    pAT = psc(f"p0c_pat_{s}", [128, 512])
                    pKT = psc(f"p0c_pkt_{s}", [128, 1024], FP16)
                    k.set_warm(pAT[0:1, 0:1], och0[:])
                    pO = psc(f"p0c_po_{s}", [128, 1024])
                    pS = psc(f"p0c_ps_{s}", [128, 1024])
                    pW = psc(f"p0c_pw_{s}", [128, 1024])

                    with contextlib.ExitStack() as wo_sc:
                        wldb = [wo_sc.enter_context(nc.sbuf_tensor(
                            f"p0c_wld{i}_{s}", [128, 4, 1024], F32, side="right")) for i in range(2)]
                        for g in range(2):
                            b = g % 2
                            ld = k.dma("sync", out=wldb[b][:, :, :],
                                       in_=W["Wo"][g * 512:(g + 1) * 512, :].rearrange("(a p) d -> p a d", p=128),
                                       buf=f"p0wo{b}")
                            k.wait("scalar", ld)
                            k.wait("scalar", tok_nw)
                            for a in range(4):
                                kc = g * 4 + a
                                k.emit("scalar", "activation", out=Wo16[:, kc, :],
                                       in_=wldb[b][:, a, :], func=AF.Identity, scale=onw[:, 0:1])
                            k.mark("scalar", "S")
                    wo_tok = k.tok("S")
                    k.wait("tensor", wo_tok)
                    k.wait("tensor", mttok)

                    xfree = {0: None, 1: None}
                    y1_store = None
                    sev = None
                    at_ev = None
                    kt_ev = None
                    on_ev = None
                    pW_ev = None
                    pO_ev = None
                    for c in range(NCH):
                        b = c % 2
                        k.wait("sync", xfree[b])
                        ld = k.dma("sync", out=xa[b][:],
                                   in_=src[tok0 + c * 128: tok0 + (c + 1) * 128, :],
                                   buf=f"p0cx{b}")
                        k.wait("scalar", ld)
                        k.wait("scalar", k.tok("V"))
                        k.emit("scalar", "activation", out=y0t[:], in_=xa[b][:],
                               func=AF.Identity, scale=lnu.rs[:, c:c + 1], bias=lnu.nmr[:, c:c + 1])
                        y0tok = k.mark("scalar", "S")
                        xfree[b] = y0tok
                        k.wait("tensor", at_ev)
                        k.wait("tensor", kt_ev)
                        for h in range(H0):
                            k.emit("tensor", "matmul", out=pAT[:, (h % 4) * 128:(h % 4 + 1) * 128],
                                   lhsT=k16[:, h, c * 128:(c + 1) * 128],
                                   rhs=q16[:, h, c * 128:(c + 1) * 128],
                                   start=True, stop=True)
                            if h % 4 == 3:
                                mtk = k.mark("tensor", "T")
                                k.wait("vector", mtk)
                                for hh in range(h - 3, h + 1):
                                    k.emit("vector", "tensor_mul", out=at16[:, hh, :],
                                           in0=pAT[:, (hh % 4) * 128:(hh % 4 + 1) * 128],
                                           in1=maskT[:])
                                at_ev = k.mark("vector", "V")
                                if h == 3:
                                    k.wait("tensor", at_ev)
                        for h in range(H0):
                            k.emit("tensor", "transpose", out=pKT[:, h * 128:(h + 1) * 128],
                                   in_=k16[:, h, c * 128:(c + 1) * 128], identity=identh[:])
                        ktk = k.mark("tensor", "T")
                        k.wait("vector", ktk)
                        k.emit("vector", "tensor_copy", out=kt16[:, :, :],
                               in_=pKT[:, :].rearrange("p (h t) -> p h t", t=128))
                        kt_ev = k.mark("vector", "V")
                        k.wait("tensor", at_ev)
                        k.wait("tensor", pO_ev)
                        k.wait("tensor", sev)
                        k.wait("tensor", kt_ev)
                        for h in range(H0):
                            k.emit("tensor", "matmul", out=pO[:, h * 128:(h + 1) * 128],
                                   lhsT=at16[:, h, :], rhs=v16[:, c, h * 128:(h + 1) * 128],
                                   start=True, stop=(c == 0))
                            if c > 0:
                                k.emit("tensor", "matmul", out=pO[:, h * 128:(h + 1) * 128],
                                       lhsT=q16[:, h, c * 128:(c + 1) * 128], rhs=S16[:, h, :],
                                       start=False, stop=True)
                        for h in range(H0):
                            k.emit("tensor", "matmul", out=pS[:, h * 128:(h + 1) * 128],
                                   lhsT=kt16[:, h, :], rhs=v16[:, c, h * 128:(h + 1) * 128],
                                   start=True, stop=True)
                        smt = k.mark("tensor", "T")
                        if c < NCH - 1:
                            k.wait("vector", smt)
                            if c == 0:
                                k.emit("vector", "tensor_copy", out=Sfp[:, :, :],
                                       in_=pS[:, :].rearrange("p (h t) -> p h t", t=128))
                            else:
                                k.emit("vector", "tensor_add", out=Sfp[:, :, :],
                                       in0=Sfp[:, :, :],
                                       in1=pS[:, :].rearrange("p (h t) -> p h t", t=128))
                            k.emit("vector", "tensor_copy", out=S16[:, :, :], in_=Sfp[:, :, :])
                            sev = k.mark("vector", "V")
                        k.wait("scalar", smt)
                        k.wait("scalar", on_ev)
                        for h in range(H0):
                            k.emit("scalar", "activation", out=ojnk[:, h, :],
                                   in_=pO[:, h * 128:(h + 1) * 128], func=AF.Square,
                                   accum_out=sso[:, h:h + 1])
                        sqt = k.mark("scalar", "S")
                        k.wait("vector", sqt)
                        k.emit("vector", "tensor_scalar", out=rso[:], in0=sso[:],
                               scalar1=1.0 / 128.0, scalar2=EPS, op0=ALU.mult, op1=ALU.add)
                        vt = k.mark("vector", "V")
                        k.wait("scalar", vt)
                        k.emit("scalar", "activation", out=sso[:], in_=rso[:], func=AF.Sqrt)
                        st_ = k.mark("scalar", "S")
                        k.wait("vector", st_)
                        k.emit("vector", "reciprocal", out=sso[:], in_=sso[:])
                        k.emit("vector", "tensor_mul", out=rso[:], in0=rso[:], in1=sso[:])
                        k.emit("vector", "tensor_mul", out=rso[:], in0=rso[:], in1=sso[:])
                        k.emit("vector", "tensor_scalar", out=rso[:], in0=rso[:],
                               scalar1=-0.5, scalar2=1.5, op0=ALU.mult, op1=ALU.add)
                        k.emit("vector", "tensor_mul", out=sso[:], in0=sso[:], in1=rso[:])
                        rsot = k.mark("vector", "V")
                        k.wait("scalar", rsot)
                        for h in range(H0):
                            k.emit("scalar", "activation", out=o_n[:, h, :],
                                   in_=pO[:, h * 128:(h + 1) * 128], func=AF.Identity,
                                   scale=sso[:, h:h + 1])
                        on_ev = k.mark("scalar", "S")
                        pO_ev = on_ev
                        k.wait("tensor", kt_ev)
                        k.warm()
                        k.wait("tensor", sqt)
                        k.warm()
                        k.wait("tensor", rsot)
                        k.warm()
                        trt = tm_to_fm(k, onT[:, :, :],
                                       lambda cc: o_n[:, cc, :],
                                       H0, pKT, on_ev, evac_eng="vector", dt16=FP16)
                        k.wait("tensor", trt)
                        kt_ev = trt
                        k.wait("tensor", pW_ev)
                        for hh in range(2):
                            for oc in range(DC):
                                k.emit("tensor", "matmul", out=pW[:, hh * 512:(hh + 1) * 512],
                                       lhsT=onT[:, oc, :],
                                       rhs=Wo16[:, oc, hh * 512:(hh + 1) * 512],
                                       start=(oc == 0), stop=(oc == DC - 1))
                        wmt = k.mark("tensor", "T")
                        k.wait("vector", wmt)
                        k.wait("vector", y0tok)
                        k.wait("vector", y1_store)
                        k.emit("vector", "tensor_add", out=y1t[:], in0=pW[:], in1=y0t[:])
                        yv = k.mark("vector", "V")
                        pW_ev = yv
                        k.wait("sync", yv)
                        y1_store = k.dma("sync",
                                         out=ydram[tok0 + c * 128: tok0 + (c + 1) * 128, :],
                                         in_=y1t[:], buf="p0y1")

                abc.close()
                k.stage_join()
                k.phase_mark(f"p0D{s}")
                with contextlib.ExitStack() as sd:
                    sbd = lambda n, s2, d: sd.enter_context(nc.sbuf_tensor(n, s2, d))
                    psd_ = lambda n, s2, d=F32: sd.enter_context(nc.psum_tensor(n, s2, d))
                    h2T = sbd(f"p0d_h2T_{s}", [128, DC, T], FP16)
                    bscope = sd.enter_context(contextlib.ExitStack())
                    ptr2 = [bscope.enter_context(nc.psum_tensor(f"p0d_ptr{i}_{s}", [128, 1024], FP16))
                            for i in range(2)]
                    pwmd = bscope.enter_context(nc.psum_tensor(f"p0d_pwm{s}", [1, 1], F32))
                    k.set_warm(pwmd[:], och0[:])
                    lnr = LnState(k, f"p0d_lnr{s}", NCH, alloc=sbd)
                    ya = [sbd(f"p0d_y{i}_{s}", [128, D], F32) for i in range(3)]
                    htm = [sbd(f"p0d_htm{i}_{s}", [128, D], FP16) for i in range(2)]
                    yfree = {i: None for i in range(3)}
                    sfree = {i: None for i in range(3)}
                    ldk = {}
                    k.wait("sync", k.tok("d_p0y1"))
                    for ii in range(NCH + 1):
                        if ii < NCH:
                            b = ii % 3
                            k.wait("sync", yfree[b])
                            k.wait("sync", sfree[b])
                            ldk[ii] = k.dma("sync", out=ya[b][:],
                                            in_=ydram[tok0 + ii * 128: tok0 + (ii + 1) * 128, :],
                                            buf=f"p0dy{b}")
                        j1 = ii - 1
                        if 0 <= j1 < NCH:
                            k.wait("scalar", ldk[j1])
                            k.wait("vector", ldk[j1])
                            k.warm()
                            ts_, tv_ = lnr.stats_tile(ya[j1 % 3][:], j1)
                            yfree[j1 % 3] = tv_
                            sfree[j1 % 3] = ts_
                    finr = lnr.finalize(mean_zero=True)
                    k.wait("scalar", finr)
                    yfree = {i: finr for i in range(3)}
                    trtoks = {}
                    atoks = {}
                    for ii in range(NCH + 1):
                        if ii < NCH:
                            b = ii % 3
                            k.wait("sync", yfree[b])
                            ldk[ii] = k.dma("sync", out=ya[b][:],
                                            in_=ydram[tok0 + ii * 128: tok0 + (ii + 1) * 128, :],
                                            buf=f"p0dy{b}")
                        j1 = ii - 1
                        if 0 <= j1 < NCH:
                            k.wait("scalar", ldk[j1])
                            k.wait("scalar", trtoks.get(j1 - 2))
                            k.emit("scalar", "activation", out=htm[j1 % 2][:],
                                   in_=ya[j1 % 3][:],
                                   func=AF.Identity, scale=lnr.rs[:, j1:j1 + 1])
                            atoks[j1] = k.mark("scalar", "S")
                            yfree[j1 % 3] = atoks[j1]
                            k.wait("tensor", trtoks.get(j1 - 2))
                            trtoks[j1] = tm_to_fm(
                                k, h2T[:, :, j1 * 128:(j1 + 1) * 128],
                                lambda c, j1=j1: htm[j1 % 2][:, c * 128:(c + 1) * 128],
                                DC, ptr2[j1 % 2], atoks[j1], evac_eng="vector", dt16=FP16)
                    k.wait("tensor", trtoks[NCH - 1])

                    bscope.close()
                    k.set_warm(None)
                    m16 = sbd(f"p0d_m16_{s}", [128, 8, T], FP16)
                    z_sb = sbd(f"p0d_z_{s}", [128, NCH, D], F32)
                    wgld = [sbd(f"p0d_wg{i}_{s}", [128, DC, 128], F32) for i in range(2)]
                    wuld = [sbd(f"p0d_wu{i}_{s}", [128, DC, 128], F32) for i in range(2)]
                    wg16 = [sbd(f"p0d_wg16{i}_{s}", [128, DC, 128], FP16) for i in range(2)]
                    wu16 = [sbd(f"p0d_wu16{i}_{s}", [128, DC, 128], FP16) for i in range(2)]
                    wd16 = sbd(f"p0d_wd16_{s}", [128, 8, 1024], FP16)
                    gs = sbd(f"p0d_gs_{s}", [128, 2, 512], FP16)
                    us = sbd(f"p0d_us_{s}", [128, 2, 512], FP16)
                    pg = [psd_(f"p0d_pg{i}_{s}", [128, 512]) for i in range(2)]
                    pu = [psd_(f"p0d_pu{i}_{s}", [128, 512]) for i in range(2)]
                    pz = [psd_(f"p0d_pz{i}_{s}", [128, 1024]) for i in range(2)]

                    wg_free = {0: None, 1: None}
                    wu_free = {0: None, 1: None}
                    mm_tok = {0: None, 1: None}
                    pg_ev = [None, None]
                    pu_ev = [None, None]
                    gs_free = [None, None]
                    pz_ev = [None, None]
                    m_free = None
                    z_done = None
                    def issue_wload(ic):
                        b = ic % 2
                        k.wait("sync", wg_free[b])
                        lg = k.dma("sync", out=wgld[b][:, :, :],
                                   in_=W["Wgate"][:, ic * 128:(ic + 1) * 128].rearrange("(c p) n -> p c n", p=128),
                                   buf=f"p0dg{b}")
                        k.wait("sync", wu_free[b])
                        lu = k.dma("sync", out=wuld[b][:, :, :],
                                   in_=W["Wup"][:, ic * 128:(ic + 1) * 128].rearrange("(c p) n -> p c n", p=128),
                                   buf=f"p0du{b}")
                        return lg, lu

                    def emit_casts(ic, lg, lu):
                        b = ic % 2
                        k.wait("scalar", lg)
                        k.wait("scalar", lu)
                        k.wait("scalar", mm_tok[b])
                        for kc in range(DC):
                            k.emit("scalar", "activation", out=wg16[b][:, kc, :],
                                   in_=wgld[b][:, kc, :], func=AF.Identity,
                                   scale=mlpw[:, kc:kc + 1])
                        wg_free[b] = k.mark("scalar", "S")
                        for kc in range(DC):
                            k.emit("scalar", "activation", out=wu16[b][:, kc, :],
                                   in_=wuld[b][:, kc, :], func=AF.Identity,
                                   scale=mlpw[:, kc:kc + 1])
                        wu_free[b] = k.mark("scalar", "S")

                    lds = {0: issue_wload(0), 1: issue_wload(1)}
                    emit_casts(0, *lds[0])
                    for icb in range(4):
                        k.wait("gpsimd", m_free)
                        for ici in range(8):
                            ic = icb * 8 + ici
                            k.dma("gpsimd", out=wd16[:, ici, :],
                                  in_=W["Wdown"][ic * 128:(ic + 1) * 128, :],
                                  buf="p0dd")
                        for ici in range(8):
                            ic = icb * 8 + ici
                            b = ic % 2
                            if ic + 2 < 32:
                                lds[(ic + 2) % 2] = issue_wload(ic + 2)
                            k.wait("tensor", wg_free[b])
                            k.wait("tensor", wu_free[b])
                            k.wait("vector", m_free)
                            gmts = {}
                            umts = {}

                            def emit_mm(ts4):
                                pp = ts4 % 2
                                k.wait("tensor", pg_ev[pp])
                                for kc in range(DC):
                                    k.emit("tensor", "matmul", out=pg[pp][:],
                                           lhsT=wg16[b][:, kc, :],
                                           rhs=h2T[:, kc, ts4 * 512:(ts4 + 1) * 512],
                                           start=(kc == 0), stop=(kc == DC - 1))
                                gmts[ts4] = k.mark("tensor", "T")
                                k.wait("tensor", pu_ev[pp])
                                for kc in range(DC):
                                    k.emit("tensor", "matmul", out=pu[pp][:],
                                           lhsT=wu16[b][:, kc, :],
                                           rhs=h2T[:, kc, ts4 * 512:(ts4 + 1) * 512],
                                           start=(kc == 0), stop=(kc == DC - 1))
                                umts[ts4] = k.mark("tensor", "T")

                            def emit_act(ts4):
                                pp = ts4 % 2
                                k.wait("scalar", gmts[ts4])
                                k.wait("scalar", gs_free[pp])
                                k.emit("scalar", "activation", out=gs[:, pp, :],
                                       in_=pg[pp][:], func=AF.Silu)
                                pg_ev[pp] = k.mark("scalar", "S")
                                k.wait("vector", umts[ts4])
                                k.emit("vector", "tensor_copy", out=us[:, pp, :],
                                       in_=pu[pp][:])
                                pu_ev[pp] = k.mark("vector", "V")
                                k.wait("vector", pg_ev[pp])
                                k.emit("vector", "tensor_mul",
                                       out=m16[:, ici, ts4 * 512:(ts4 + 1) * 512],
                                       in0=gs[:, pp, :], in1=us[:, pp, :])
                                gs_free[pp] = k.mark("vector", "V")

                            emit_mm(0)
                            emit_mm(1)
                            if ic + 1 < 32:
                                emit_casts(ic + 1, *lds[(ic + 1) % 2])
                            emit_act(0)
                            emit_mm(2)
                            emit_act(1)
                            emit_mm(3)
                            mm_tok[b] = umts[3]
                            emit_act(2)
                            emit_act(3)
                        mtok = k.tok("V")
                        k.wait("tensor", mtok)
                        k.wait("tensor", k.tok("d_p0dd"))
                        for sub in range(NCH):
                            pb = pz[sub % 2]
                            k.wait("tensor", pz_ev[sub % 2])
                            for ici in range(8):
                                for hh in range(2):
                                    k.emit("tensor", "matmul", out=pb[:, hh * 512:(hh + 1) * 512],
                                           lhsT=m16[:, ici, sub * 128:(sub + 1) * 128],
                                           rhs=wd16[:, ici, hh * 512:(hh + 1) * 512],
                                           start=(ici == 0), stop=(ici == 7))
                            dmt = k.mark("tensor", "T")
                            k.wait("vector", dmt)
                            if icb == 0:
                                k.emit("vector", "tensor_copy", out=z_sb[:, sub, :], in_=pb[:])
                            else:
                                k.emit("vector", "tensor_add", out=z_sb[:, sub, :],
                                       in0=z_sb[:, sub, :], in1=pb[:])
                            pz_ev[sub % 2] = k.mark("vector", "V")
                        m_free = k.tok("T")
                        z_done = k.tok("V")

                    lne = LnState(k, f"p0d_lne{s}", NCH, alloc=sbd, junk_from=lnr)
                    k.set_warm(pz[0][0:1, 0:1], och0[:])
                    k.wait("tensor", z_done)
                    k.wait("vector", z_done)
                    yfree = {i: None for i in range(3)}
                    ldk = {}
                    addm = {}
                    for ii in range(NCH + 2):
                        if ii < NCH:
                            b = ii % 3
                            k.wait("sync", yfree[b])
                            ldk[ii] = k.dma("sync", out=ya[b][:],
                                            in_=ydram[tok0 + ii * 128: tok0 + (ii + 1) * 128, :],
                                            buf=f"p0dy{b}")
                        j1 = ii - 1
                        if 0 <= j1 < NCH:
                            k.wait("vector", ldk[j1])
                            k.warm()
                            k.emit("vector", "tensor_add", out=z_sb[:, j1, :],
                                   in0=z_sb[:, j1, :], in1=ya[j1 % 3][:])
                            addm[j1] = k.mark("vector", "V")
                            yfree[j1 % 3] = addm[j1]
                        j2 = ii - 2
                        if 0 <= j2 < NCH:
                            k.wait("scalar", addm[j2])
                            lne.stats_tile(z_sb[:, j2, :], j2)
                    fine = lne.finalize()
                    k.wait("scalar", fine)
                    xfree = {i: fine for i in range(3)}
                    st_free = {0: None, 1: None}
                    ob = [sbd(f"p0d_ob{i}_{s}", [128, D], F32) for i in range(2)]
                    ld2 = {}
                    apm = {}
                    for ii in range(NCH + 2):
                        if ii < NCH:
                            b = ii % 3
                            k.wait("sync", xfree[b])
                            ld2[ii] = k.dma("sync", out=ya[b][:],
                                            in_=src[tok0 + ii * 128: tok0 + (ii + 1) * 128, :],
                                            buf=f"p0dy{b}")
                        j1 = ii - 1
                        if 0 <= j1 < NCH:
                            k.warm()
                            apm[j1] = lne.apply(z_sb[:, j1, :], z_sb[:, j1, :], j1)
                        j2 = ii - 2
                        if 0 <= j2 < NCH:
                            bo = j2 % 2
                            k.wait("vector", apm[j2])
                            k.wait("vector", ld2[j2])
                            k.wait("vector", st_free[bo])
                            k.emit("vector", "tensor_add", out=ob[bo][:],
                                   in0=z_sb[:, j2, :], in1=ya[j2 % 3][:])
                            ot = k.mark("vector", "V")
                            xfree[j2 % 3] = ot
                            k.wait("sync", ot)
                            st_free[bo] = k.dma("sync",
                                                out=dst[tok0 + j2 * 128: tok0 + (j2 + 1) * 128, :],
                                                in_=ob[bo][:], buf=f"p0o{bo}")
        k.set_warm(None)
        k.wait("sync", k.tok("d_p0o0"))
        k.wait("sync", k.tok("d_p0o1"))


def phase_p2(k, src, dst, W):
    nc = k.nc
    H1 = 16
    DK1 = 64
    NCH = T // 128
    identf, iftok = k.const_sb("identity128_f32", np.eye(128, dtype=np.float32))
    identh, ihtok = k.const_sb("identity128_f16", np.eye(128, dtype=np.float16))
    onesf_all, oatok = k.const_sb("ones128x128_f32", np.ones((128, 128), dtype=np.float32))
    onesrow_h, orhtok = k.const_sb("ones_row128_f16", np.ones((1, 128), dtype=np.float16))
    onescol_f, ocf32tok = k.const_sb("ones_col128_f32", np.ones((128, 1), dtype=np.float32))
    _ki = np.arange(128)[:, None] // 64
    _qi = np.arange(128)[None, :] // 64
    maskc, mctok = k.const_sb("p2_maskneg",
                              np.where(_ki > _qi, -50000.0, 0.0).astype(np.float16))

    with contextlib.ExitStack() as ph:
        sb = lambda n, s, d: ph.enter_context(nc.sbuf_tensor(n, s, d))
        pwm2 = ph.enter_context(nc.psum_tensor("p2_pwm", [1, 1], F32))
        och2, ochtok2 = k.const_sb("ones_col128_f16", np.ones((128, 1), dtype=np.float16))
        k.wait("tensor", ochtok2)
        k.set_warm(pwm2[:], och2[:])
        bqs = sb("p2_bq", [128, DC], F32)
        bks = sb("p2_bk", [128, DC], F32)
        bvrow = sb("p2_bv", [1, D], FP16)
        borow = sb("p2_bo", [1, D], FP16)
        D0 = sb("p2_d0", [128, H1, 128], FP16)
        D1 = sb("p2_d1", [128, H1, 128], FP16)
        rbf = sb("p2_rbf", [128, H1], F32)
        dmaxb = sb("p2_dmaxb", [128, 1], F32)
        cb = sb("p2_cb", [128, DC], F32)
        scr1 = sb("p2_scr1", [128, 4], F32)

        k.dma("sync", out=bqs[:], in_=W["bq"].rearrange("(c p) -> p c", p=128),
              buf="p2b", allow_slow_non_contiguous=True)
        k.dma("sync", out=bks[:], in_=W["bk"].rearrange("(c p) -> p c", p=128),
              buf="p2b", allow_slow_non_contiguous=True)
        k.dma("gpsimd", out=bvrow[:], in_=W["bv"].rearrange("(o d) -> o d", o=1), buf="p2b2")
        k.dma("gpsimd", out=borow[:], in_=W["bo"].rearrange("(o d) -> o d", o=1), buf="p2b2")
        k.dma("sync", out=rbf[:], in_=W["rbf"], buf="p2b")
        tok_b = k.tok("d_p2b")
        tok_b2 = k.tok("d_p2b2")

        with contextlib.ExitStack() as ld0:
            gld = [ld0.enter_context(nc.sbuf_tensor(f"p2_gld{i}", [128, H1, 128], F32,
                                                    side="right")) for i in range(2)]
            pdm = ld0.enter_context(nc.psum_tensor("p2_pdm", [128, 128], F32))
            l0 = k.dma("sync", out=gld[0][:, :, :], in_=W["G0"], buf="p2g0")
            l1 = k.dma("sync", out=gld[1][:, :, :], in_=W["G1"], buf="p2g1")
            k.wait("vector", l0)
            k.wait("vector", l1)
            k.wait("vector", tok_b)
            for h in range(H1):
                k.emit("vector", "tensor_scalar", out=D0[:, h, :], in0=gld[0][:, h, :],
                       scalar1=rbf[:, h:h + 1], scalar2=8.0, op0=ALU.subtract, op1=ALU.mult)
                k.emit("vector", "tensor_scalar", out=D1[:, h, :], in0=gld[1][:, h, :],
                       scalar1=rbf[:, h:h + 1], scalar2=8.0, op0=ALU.subtract, op1=ALU.mult)
            k.mark("vector", "V")
            k.emit("vector", "reduce_max", out=scr1[:, 0:1], in_=D0[:, :, :], axis=AX.XY,
                   apply_absolute_value=True)
            k.emit("vector", "reduce_max", out=scr1[:, 1:2], in_=D1[:, :, :], axis=AX.XY,
                   apply_absolute_value=True)
            k.emit("vector", "tensor_max", out=scr1[:, 0:1], in0=scr1[:, 0:1],
                   in1=scr1[:, 1:2])
            vt = k.mark("vector", "V")
            k.wait("tensor", vt)
            k.wait("tensor", iftok)
            k.emit("tensor", "transpose", out=pdm[0:1, :], in_=scr1[:, 0:1],
                   identity=identf[:])
            tt = k.mark("tensor", "T")
            k.wait("vector", tt)
            k.emit("vector", "reduce_max", out=scr1[0:1, 2:3], in_=pdm[0:1, :], axis=AX.X)
            k.emit("vector", "tensor_scalar", out=scr1[0:1, 3:4], in0=scr1[0:1, 2:3],
                   scalar1=0.125, scalar2=None, op0=ALU.mult)
            vt2 = k.mark("vector", "V")
            k.wait("tensor", vt2)
            k.wait("tensor", oatok)
            k.emit("tensor", "matmul", out=pdm[:, 0:1], lhsT=onesf_all[0:1, :],
                   rhs=scr1[0:1, 3:4], start=True, stop=True)
            tt2 = k.mark("tensor", "T")
            k.wait("vector", tt2)
            k.emit("vector", "tensor_copy", out=dmaxb[:], in_=pdm[:, 0:1])
            k.mark("vector", "V")
            k.wait("vector", mctok)
            for h in range(H1):
                k.emit("vector", "tensor_add", out=D0[:, h, :], in0=D0[:, h, :],
                       in1=maskc[:])
            mvt = k.mark("vector", "V")
            k.wait("scalar", mvt)
            k.emit("scalar", "activation", out=D0[:, :, :], in_=D0[:, :, :],
                   func=AF.Exp, scale=np.float32(0.125))
            k.emit("scalar", "activation", out=D1[:, :, :], in_=D1[:, :, :],
                   func=AF.Exp, scale=np.float32(0.125))
            k.mark("scalar", "S")
            k.stage_join()

        for s in range(NSEQ):
            k.stage_join()
            tok0 = s * T
            with contextlib.ExitStack() as sq:
                sbs = lambda n, s2, d: sq.enter_context(nc.sbuf_tensor(n, s2, d))
                o_all = sbs(f"p2_oall_{s}", [128, NCH, H1 * DK1], FP16)
                lne = LnState(k, f"p2_lne_{s}", NCH, alloc=sbs)
                with contextlib.ExitStack() as at_sc:
                    sba = lambda n, s2, d: at_sc.enter_context(nc.sbuf_tensor(n, s2, d))
                    psa = lambda n, s2, d=F32: at_sc.enter_context(nc.psum_tensor(n, s2, d))
                    Q16 = sba(f"p2_q_{s}", [128, DC, T], FP16)
                    K16 = sba(f"p2_k_{s}", [128, DC, T], FP16)
                    vaug = sba(f"p2_va_{s}", [128, NCH, 1040], FP16)

                    k.phase_mark(f"p2bu{s}")
                    with contextlib.ExitStack() as bu:
                        sbb = lambda n, s2, d: bu.enter_context(nc.sbuf_tensor(n, s2, d))
                        psb_ = lambda n, s2, d=F32: bu.enter_context(nc.psum_tensor(n, s2, d))
                        uT = sbb(f"p2_uT_{s}", [128, DC, T], FP16)
                        xa = [sbb(f"p2_x{i}_{s}", [128, D], F32) for i in range(2)]
                        utm = sbb(f"p2_utm_{s}", [128, D], FP16)
                        wf = sbb(f"p2_wf_{s}", [128, DC, 1024], FP16)
                        ptr = psb_(f"p2_ptr_{s}", [128, 1024], FP16)
                        pqk = [psb_(f"p2_pqk{i}_{s}", [128, 1024]) for i in range(2)]

                        xfree = {0: None, 1: None}
                        trtok = None
                        for ct in range(NCH):
                            b = ct % 2
                            k.wait("sync", xfree[b])
                            ld = k.dma("sync", out=xa[b][:],
                                       in_=src[tok0 + ct * 128: tok0 + (ct + 1) * 128, :],
                                       buf=f"p2x{b}")
                            k.wait("scalar", ld)
                            k.wait("vector", ld)
                            k.warm()
                            ts_, tv_ = lne.stats_tile(xa[b][:], ct)
                            k.wait("sync", ts_)
                            xfree[b] = tv_
                        fin = lne.finalize()
                        k.wait("scalar", fin)
                        xfree = {0: fin, 1: fin}
                        for ct in range(NCH):
                            b = ct % 2
                            k.wait("sync", xfree[b])
                            ld = k.dma("sync", out=xa[b][:],
                                       in_=src[tok0 + ct * 128: tok0 + (ct + 1) * 128, :],
                                       buf=f"p2x{b}")
                            k.wait("scalar", ld)
                            k.wait("scalar", trtok)
                            k.warm()
                            atok = lne.apply(utm[:], xa[b][:], ct)
                            xfree[b] = atok
                            trtok = tm_to_fm(k, uT[:, :, ct * 128:(ct + 1) * 128],
                                             lambda c: utm[:, c * 128:(c + 1) * 128],
                                             DC, ptr, atok, evac_eng="vector", dt16=FP16)
                            k.wait("tensor", trtok)

                        def ldw(wnm):
                            k.wait("gpsimd", k.tok("T"))
                            for kc in range(DC):
                                k.dma("gpsimd", out=wf[:, kc, :],
                                      in_=W[wnm][kc * 128:(kc + 1) * 128, :], buf="p2wf")
                            return k.tok("d_p2wf")

                        pev = [None, None]
                        for dst16, wnm, brow in ((Q16, "Wq", bqs), (K16, "Wk", bks)):
                            ctok = ldw(wnm)
                            k.wait("tensor", ctok)
                            k.wait("gpsimd", k.tok("S"))
                            for nch in range(DC):
                                for th in range(2):
                                    pb = pqk[th]
                                    k.wait("tensor", pev[th])
                                    for ts4 in range(2):
                                        for kc in range(DC):
                                            k.emit("tensor", "matmul",
                                                   out=pb[:, ts4 * 512:(ts4 + 1) * 512],
                                                   lhsT=wf[:, kc, nch * 128:(nch + 1) * 128],
                                                   rhs=uT[:, kc, th * 1024 + ts4 * 512: th * 1024 + (ts4 + 1) * 512],
                                                   start=(kc == 0), stop=(kc == DC - 1))
                                    mtk = k.mark("tensor", "T")
                                    if th == 0:
                                        k.wait("scalar", mtk)
                                        k.wait("scalar", tok_b)
                                        k.emit("scalar", "activation",
                                               out=dst16[:, nch, th * 1024:(th + 1) * 1024],
                                               in_=pb[:], func=AF.Identity,
                                               bias=brow[:, nch:nch + 1])
                                        pev[th] = k.mark("scalar", "S")
                                    else:
                                        k.wait("vector", mtk)
                                        k.wait("vector", tok_b)
                                        k.emit("vector", "tensor_scalar",
                                               out=dst16[:, nch, th * 1024:(th + 1) * 1024],
                                               in0=pb[:], scalar1=brow[:, nch:nch + 1],
                                               scalar2=None, op0=ALU.add)
                                        pev[th] = k.mark("vector", "V")
                        ctok = ldw("Wv")
                        k.wait("tensor", ctok)
                        k.wait("tensor", orhtok)
                        k.wait("tensor", tok_b2)
                        k.emit("vector", "memset",
                               ap=vaug[:, :, :].rearrange("p c (h e) -> p c h e", e=65)[:, :, :, 64:65],
                               constant=1.0)
                        onestok = k.mark("vector", "V")
                        for tc in range(NCH):
                            pb = pqk[tc % 2]
                            k.wait("tensor", pev[tc % 2])
                            for kc in range(DC):
                                for hh in range(2):
                                    k.emit("tensor", "matmul",
                                           out=pb[:, hh * 512:(hh + 1) * 512],
                                           lhsT=uT[:, kc, tc * 128:(tc + 1) * 128],
                                           rhs=wf[:, kc, hh * 512:(hh + 1) * 512],
                                           start=(kc == 0), stop=False)
                            for hh in range(2):
                                k.emit("tensor", "matmul",
                                       out=pb[:, hh * 512:(hh + 1) * 512],
                                       lhsT=onesrow_h[:],
                                       rhs=bvrow[0:1, hh * 512:(hh + 1) * 512],
                                       start=False, stop=True)
                            mtk = k.mark("tensor", "T")
                            k.wait("vector", mtk)
                            k.wait("vector", onestok)
                            k.emit("vector", "tensor_copy",
                                   out=vaug[:, tc, :].rearrange("p (h e) -> p h e", e=65)[:, :, 0:64],
                                   in_=pb[:].rearrange("p (h e) -> p h e", e=64))
                            pev[tc % 2] = k.mark("vector", "V")

                    with contextlib.ExitStack() as sh:
                        sbb = lambda n, s2, d: sh.enter_context(nc.sbuf_tensor(n, s2, d))
                        psb_ = lambda n, s2, d=F32: sh.enter_context(nc.psum_tensor(n, s2, d))
                        sqv = sbb(f"p2_sq_{s}", [128, T], F32)
                        qn = sbb(f"p2_qn_{s}", [128, DC], F32)
                        qpad = sbb(f"p2_qpad_{s}", [128, 128], F32)
                        nres = sbb(f"p2_nres_{s}", [128, 2], F32)
                        nsc = sbb(f"p2_nsc_{s}", [128, 16], F32)
                        pn = psb_(f"p2_pn_{s}", [128, 16])
                        pb1 = psb_(f"p2_pb1_{s}", [128, 128])
                        k.stage_join()
                        k.wait("tensor", ocf32tok)
                        pn_free = None
                        pb1_free = None
                        for ti, t16 in ((0, Q16), (1, K16)):
                            k.wait("vector", k.tok("S"))
                            for pc in range(DC):
                                k.warm()
                                k.emit("vector", "tensor_mul", out=sqv[:],
                                       in0=t16[:, pc, :], in1=t16[:, pc, :])
                                sqk = k.mark("vector", "V")
                                k.wait("tensor", sqk)
                                k.wait("tensor", pn_free)
                                for ts16 in range(NCH):
                                    k.emit("tensor", "matmul", out=pn[:, ts16:ts16 + 1],
                                           lhsT=sqv[:, ts16 * 128:(ts16 + 1) * 128],
                                           rhs=onescol_f[:], start=True, stop=True)
                                ntk = k.mark("tensor", "T")
                                k.wait("vector", ntk)
                                k.emit("vector", "reduce_max", out=qn[:, pc:pc + 1],
                                       in_=pn[:, 0:NCH], axis=AX.X)
                                pn_free = k.mark("vector", "V")
                            for g3 in range(3):
                                pcs = [p for p in range(g3 * 3, min(g3 * 3 + 3, DC))]
                                k.emit("vector", "memset", ap=qpad[:], constant=0.0)
                                for j, pc in enumerate(pcs):
                                    k.emit("vector", "tensor_copy",
                                           out=qpad[:, 32 * j: 32 * j + 1],
                                           in_=qn[:, pc:pc + 1])
                                vt = k.mark("vector", "V")
                                k.wait("tensor", vt)
                                k.wait("tensor", pb1_free)
                                k.emit("tensor", "transpose", out=pb1[:], in_=qpad[:],
                                       identity=identf[:])
                                tt = k.mark("tensor", "T")
                                k.wait("vector", tt)
                                for j, pc in enumerate(pcs):
                                    k.emit("vector", "reduce_max",
                                           out=nres[32 * j: 32 * j + 1, ti:ti + 1],
                                           in_=pb1[32 * j: 32 * j + 1, :], axis=AX.X)
                                    k.emit("vector", "tensor_copy",
                                           out=nsc[32 * j:32 * j + 1, ti * DC + pc: ti * DC + pc + 1],
                                           in_=nres[32 * j: 32 * j + 1, ti:ti + 1])
                                pb1_free = k.mark("vector", "V")
                        k.wait("vector", k.tok("V"))
                        for pc in range(DC):
                            r = 32 * (pc % 3)
                            k.emit("vector", "tensor_mul", out=nsc[r:r + 1, pc:pc + 1],
                                   in0=nsc[r:r + 1, pc:pc + 1],
                                   in1=nsc[r:r + 1, DC + pc:DC + pc + 1])
                        prodt = k.mark("vector", "V")
                        k.wait("scalar", prodt)
                        for pc in range(DC):
                            r = 32 * (pc % 3)
                            k.emit("scalar", "activation", out=nsc[r:r + 1, pc:pc + 1],
                                   in_=nsc[r:r + 1, pc:pc + 1], func=AF.Sqrt)
                        sqt = k.mark("scalar", "S")
                        k.wait("vector", sqt)
                        for pc in range(DC):
                            r = 32 * (pc % 3)
                            k.emit("vector", "tensor_scalar", out=nsc[r:r + 1, pc:pc + 1],
                                   in0=nsc[r:r + 1, pc:pc + 1], scalar1=-0.125,
                                   scalar2=None, op0=ALU.mult)
                        nsct = k.mark("vector", "V")
                        k.wait("tensor", nsct)
                        k.wait("tensor", pb1_free)
                        for pc in range(DC):
                            r = 32 * (pc % 3)
                            k.emit("tensor", "matmul", out=pb1[:, pc:pc + 1],
                                   lhsT=onesf_all[r:r + 1, :],
                                   rhs=nsc[r:r + 1, pc:pc + 1],
                                   start=True, stop=True)
                        bct = k.mark("tensor", "T")
                        k.wait("vector", bct)
                        k.emit("vector", "tensor_scalar", out=cb[:], in0=pb1[:, 0:DC],
                               scalar1=dmaxb[:], scalar2=None, op0=ALU.subtract)
                        cbtok = k.mark("vector", "V")
                        k.wait("tensor", cbtok)
                        k.wait("scalar", cbtok)


                    k.phase_mark(f"p2at{s}")
                    with contextlib.ExitStack() as aa:
                        sc_ = lambda n, s2, d: aa.enter_context(nc.sbuf_tensor(n, s2, d))
                        pc_ = lambda n, s2, d=F32: aa.enter_context(nc.psum_tensor(n, s2, d))
                        eTb2 = [sc_(f"p2_eT{i}_{s}", [128, NCH, 512], FP16) for i in range(2)]
                        rq = sc_(f"p2_rq_{s}", [128, 2], F32)
                        pss = [pc_(f"p2_ps{i}_{s}", [128, 512]) for i in range(2)]
                        pot = [pc_(f"p2_po{i}_{s}", [128, 65]) for i in range(2)]
                        ps_ev = [None, None]
                        po_ev = [None, None]
                        eT_freeT = [None, None]
                        it_idx = 0
                        k.wait("tensor", k.tok("V"))
                        k.wait("tensor", k.tok("S"))

                        def emit_o(p):
                            eTb_p, h_p, qb4_p, par, exptok_p, vtok_p = p
                            k.warm()
                            k.wait("tensor", exptok_p)
                            k.wait("tensor", vtok_p)
                            k.warm()
                            for qsub in range(4):
                                qb = 4 * qb4_p + qsub
                                pbo = pot[qsub % 2]
                                k.wait("tensor", po_ev[qsub % 2])
                                for kb in range(qb + 1):
                                    k.emit("tensor", "matmul", out=pbo[:],
                                           lhsT=eTb_p[:, kb, qsub * 128:(qsub + 1) * 128],
                                           rhs=vaug[:, kb, 65 * h_p:65 * h_p + 65],
                                           start=(kb == 0), stop=(kb == qb))
                                omt = k.mark("tensor", "T")
                                k.wait("vector", omt)
                                k.wait("vector", po_ev[qsub % 2])
                                k.emit("vector", "reciprocal",
                                       out=rq[:, qsub % 2:qsub % 2 + 1],
                                       in_=pbo[:, 64:65])
                                rqt = k.mark("vector", "V")
                                k.wait("scalar", rqt)
                                k.emit("scalar", "activation",
                                       out=o_all[:, qb, DK1 * h_p:DK1 * (h_p + 1)],
                                       in_=pbo[:, 0:64], func=AF.Identity,
                                       scale=rq[:, qsub % 2:qsub % 2 + 1])
                                po_ev[qsub % 2] = k.mark("scalar", "S")
                            eT_freeT[par] = k.tok("T")

                        pend = None
                        for h in range(H1):
                            pcair = h // 2
                            pbase = 64 * (h % 2)
                            for qb4 in range(4):
                                eTb = eTb2[it_idx % 2]
                                k.wait("scalar", eT_freeT[it_idx % 2])
                                k.warm()
                                vt_last = None
                                for kb in range(4 * qb4 + 4):
                                    off = max(0, 128 * (kb - 4 * qb4))
                                    wdt = 512 - off
                                    q0 = qb4 * 512 + off
                                    pb = pss[kb % 2]
                                    k.wait("tensor", ps_ev[kb % 2])
                                    k.emit("tensor", "matmul", out=pb[:, 0:wdt],
                                           lhsT=K16[pbase:pbase + 64, pcair, kb * 128:(kb + 1) * 128],
                                           rhs=Q16[pbase:pbase + 64, pcair, q0:q0 + wdt],
                                           start=True, stop=True)
                                    mtk = k.mark("tensor", "T")
                                    k.wait("scalar", mtk)
                                    k.emit("scalar", "activation",
                                           out=eTb[:, kb, off:512], in_=pb[:, 0:wdt],
                                           func=AF.Exp, scale=0.125,
                                           bias=cb[:, pcair:pcair + 1])
                                    expm = k.mark("scalar", "S")
                                    ps_ev[kb % 2] = expm
                                    for dmas, kk in ((D0, kb), (D1, kb + 1)):
                                        if 128 * kk >= qb4 * 512 and 128 * kk < qb4 * 512 + 512:
                                            doff = 128 * kk - q0 + off
                                            k.wait("vector", expm)
                                            k.emit("vector", "tensor_mul",
                                                   out=eTb[:, kb, doff:doff + 128],
                                                   in0=eTb[:, kb, doff:doff + 128],
                                                   in1=dmas[:, h, :])
                                            vt_last = k.mark("vector", "V")
                                cur = (eTb, h, qb4, it_idx % 2, k.tok("S"), vt_last)
                                if pend is not None:
                                    emit_o(pend)
                                pend = cur
                                it_idx += 1
                        emit_o(pend)

                k.phase_mark(f"p2wo{s}")
                with contextlib.ExitStack() as wo:
                    sw = lambda n, s2, d: wo.enter_context(nc.sbuf_tensor(n, s2, d))
                    pw_ = lambda n, s2, d=F32: wo.enter_context(nc.psum_tensor(n, s2, d))
                    Wo16 = sw(f"p2_wo16_{s}", [128, DC, 1024], FP16)
                    ofmP = sw(f"p2_ofmp_{s}", [128, 8, 128], FP16)
                    z_all = sw(f"p2_zall_{s}", [128, NCH, D], F32)
                    xa2 = [sw(f"p2_xe{i}_{s}", [128, D], F32) for i in range(2)]
                    ob = [sw(f"p2_ob{i}_{s}", [128, D], F32) for i in range(2)]
                    lnz = LnState(k, f"p2_lnz_{s}", NCH, alloc=sw)
                    ptp = pw_(f"p2_ptp_{s}", [128, 1024], FP16)
                    pwo = [pw_(f"p2_pwo{i}_{s}", [128, 1024]) for i in range(2)]
                    k.stage_join()
                    for kc in range(DC):
                        k.dma("gpsimd", out=Wo16[:, kc, :],
                              in_=W["Wo"][kc * 128:(kc + 1) * 128, :], buf=f"p2wo_{s}")
                    wotok = k.tok(f"d_p2wo_{s}")
                    k.wait("tensor", wotok)
                    tr_ev = None
                    pwo_ev = [None, None]
                    for qb in range(NCH):
                        k.wait("tensor", tr_ev)
                        for pr in range(8):
                            k.emit("tensor", "transpose",
                                   out=ptp[:, pr * 128:(pr + 1) * 128],
                                   in_=o_all[:, qb, 128 * pr:128 * (pr + 1)],
                                   identity=identh[:])
                        ttk = k.mark("tensor", "T")
                        k.wait("vector", ttk)
                        k.emit("vector", "tensor_copy",
                               out=ofmP[:, :, :],
                               in_=ptp[:, :].rearrange("p (c t) -> p c t", t=128))
                        tr_ev = k.mark("vector", "V")
                        k.wait("tensor", tr_ev)
                        pb = pwo[qb % 2]
                        k.wait("tensor", pwo_ev[qb % 2])
                        for hh in range(2):
                            for pr in range(8):
                                k.emit("tensor", "matmul",
                                       out=pb[:, hh * 512:(hh + 1) * 512],
                                       lhsT=ofmP[:, pr, :],
                                       rhs=Wo16[:, pr, hh * 512:(hh + 1) * 512],
                                       start=(pr == 0), stop=False)
                            k.emit("tensor", "matmul", out=pb[:, hh * 512:(hh + 1) * 512],
                                   lhsT=onesrow_h[:],
                                   rhs=borow[0:1, hh * 512:(hh + 1) * 512],
                                   start=False, stop=True)
                        wmt = k.mark("tensor", "T")
                        k.wait("vector", wmt)
                        k.emit("vector", "tensor_copy", out=z_all[:, qb, :], in_=pb[:])
                        zv = k.mark("vector", "V")
                        pwo_ev[qb % 2] = zv
                        k.wait("scalar", zv)
                        ts_, tv_ = lnz.stats_tile(z_all[:, qb, :], qb)
                    finz = lnz.finalize()
                    k.wait("scalar", finz)
                    xfree2 = {0: None, 1: None}
                    stfree = {0: None, 1: None}
                    for qb in range(NCH):
                        b = qb % 2
                        k.warm()
                        atok = lnz.apply(z_all[:, qb, :], z_all[:, qb, :], qb)
                        k.wait("sync", stfree[b])
                        xl = k.dma("sync", out=ob[b][:],
                                   in_=src[tok0 + qb * 128: tok0 + (qb + 1) * 128, :],
                                   buf=f"p2xl{b}")
                        k.wait("vector", atok)
                        k.wait("vector", xl)
                        k.emit("vector", "tensor_add", out=ob[b][:], in0=ob[b][:],
                               in1=z_all[:, qb, :])
                        ot = k.mark("vector", "V")
                        k.wait("sync", ot)
                        stfree[b] = k.dma("sync",
                                          out=dst[tok0 + qb * 128: tok0 + (qb + 1) * 128, :],
                                          in_=ob[b][:], buf=f"p2o{b}")
        k.set_warm(None)
        k.wait("sync", k.tok("d_p2o0"))
        k.wait("sync", k.tok("d_p2o1"))


def build_program(phases=("p0", "p1", "p2", "p3")):
    k = K()
    nc = k.nc
    x0 = nc.dram_tensor("x0", [NTOK, D], F32, kind="ExternalInput").ap()
    out = nc.dram_tensor("out", [NTOK, D], F32, kind="ExternalOutput").ap()

    def din(name, shape):
        return nc.dram_tensor(name, shape, F32, kind="ExternalInput").ap()

    with k.es:
        ident_bf16(k)
        ones_col128(k)
        k.const_sb("ident1_f32", np.ones((1, 1), dtype=np.float32))
        k.const_sb("ones_f32_row", np.ones((1, 128), dtype=np.float32))
        k.const_sb("identity128_f32", np.eye(128, dtype=np.float32))
        k.const_sb("identity128_f16", np.eye(128, dtype=np.float16))
        k.const_sb("ones_col128_f32", np.ones((128, 1), dtype=np.float32))
        k.const_sb("ones_col128_f16", np.ones((128, 1), dtype=np.float16))
        k.const_sb("ones_row128_f16", np.ones((1, 128), dtype=np.float16))
        if phases == ("p0", "p1", "p2", "p3"):
            W0 = {"attn_norm_w": din("L0_attn_attn_norm_w", [D]),
                  "Wq": din("L0_attn_Wq", [D, D]), "Wk": din("L0_attn_Wk", [D, D]),
                  "Wv": din("L0_attn_Wv", [D, D]), "Wo": din("L0_attn_Wo", [D, D]),
                  "out_norm_w": din("L0_attn_attn_out_norm_w", [128]),
                  "mlp_norm_w": din("L0_attn_mlp_norm_w", [D]),
                  "Wgate": din("L0_attn_Wgate", [D, INTER]),
                  "Wup": din("L0_attn_Wup", [D, INTER]),
                  "Wdown": din("L0_attn_Wdown", [INTER, D])}
            W1 = {"w1": din("L0_ffn_w1", [INTER, D]),
                  "w2": din("L0_ffn_w2", [D, INTER]),
                  "conv_w": din("L0_ffn_conv_w", [INTER, 3]),
                  "conv_b": din("L0_ffn_conv_b", [INTER])}
            W2 = {"Wq": din("L1_attn_Wq", [D, D]), "bq": din("L1_attn_bq", [D]),
                  "Wk": din("L1_attn_Wk", [D, D]), "bk": din("L1_attn_bk", [D]),
                  "Wv": din("L1_attn_Wv", [D, D]), "bv": din("L1_attn_bv", [D]),
                  "Wo": din("L1_attn_Wo", [D, D]), "bo": din("L1_attn_bo", [D]),
                  "G0": din("L1_attn_G0", [128, 16, 128]),
                  "G1": din("L1_attn_G1", [128, 16, 128]),
                  "rbf": din("L1_attn_rbf", [128, 16])}
            W3 = {"weight": din("L1_ffn_weight", [D, D]),
                  "aux_weight1": din("L1_ffn_aux_weight1", [D, D]),
                  "aux_weight2": din("L1_ffn_aux_weight2", [D, D]),
                  "bias": din("L1_ffn_bias", [D])}
            ybuf = nc.dram_tensor("ybuf", [NTOK, D], F32, kind="Internal").ap()
            xA = nc.dram_tensor("xA", [NTOK, D], F32, kind="Internal").ap()
            xB = nc.dram_tensor("xB", [NTOK, D], F32, kind="Internal").ap()
            xC = nc.dram_tensor("xC", [NTOK, D], F32, kind="Internal").ap()
            uqs = nc.dram_tensor("uq_scratch", [NTOK, D], FP16, kind="Internal").ap()
            k.phase_mark("p0")
            phase_p0(k, x0, ybuf, xA, W0)
            k.stage_join()
            k.phase_mark("p1")
            phase_p1(k, xA, xB, W1, uqs)
            k.stage_join()
            k.phase_mark("p2")
            phase_p2(k, xB, xC, W2)
            k.stage_join()
            k.phase_mark("p3")
            phase_p3(k, xC, out, W3)
            k.phase_mark("end")
        elif phases == ("p0",):
            W = {"attn_norm_w": din("L0a_anw", [D]),
                 "Wq": din("L0a_Wq", [D, D]), "Wk": din("L0a_Wk", [D, D]),
                 "Wv": din("L0a_Wv", [D, D]), "Wo": din("L0a_Wo", [D, D]),
                 "out_norm_w": din("L0a_onw", [128]),
                 "mlp_norm_w": din("L0a_mnw", [D]),
                 "Wgate": din("L0a_Wg", [D, INTER]), "Wup": din("L0a_Wu", [D, INTER]),
                 "Wdown": din("L0a_Wd", [INTER, D])}
            ybuf = nc.dram_tensor("ybuf", [NTOK, D], F32, kind="Internal").ap()
            phase_p0(k, x0, ybuf, out, W)
        elif phases == ("p1",):
            W = {"w1": din("L0f_w1", [INTER, D]), "w2": din("L0f_w2", [D, INTER]),
                 "conv_w": din("L0f_cw", [INTER, 3]), "conv_b": din("L0f_cb", [INTER])}
            uqs = nc.dram_tensor("uq_scratch", [NTOK, D], FP16, kind="Internal").ap()
            phase_p1(k, x0, out, W, uqs)
        elif phases == ("p2",):
            W = {"Wq": din("L1a_Wq", [D, D]), "bq": din("L1a_bq", [D]),
                 "Wk": din("L1a_Wk", [D, D]), "bk": din("L1a_bk", [D]),
                 "Wv": din("L1a_Wv", [D, D]), "bv": din("L1a_bv", [D]),
                 "Wo": din("L1a_Wo", [D, D]), "bo": din("L1a_bo", [D]),
                 "G0": din("L1a_G0", [128, 16, 128]),
                 "G1": din("L1a_G1", [128, 16, 128]),
                 "rbf": din("L1a_rbf", [128, 16])}
            phase_p2(k, x0, out, W)
        elif phases == ("p3",):
            W = {"weight": din("L1f_weight", [D, D]),
                 "aux_weight1": din("L1f_aux1", [D, D]),
                 "aux_weight2": din("L1f_aux2", [D, D]),
                 "bias": din("L1f_bias", [D])}
            phase_p3(k, x0, out, W)
        else:
            raise NotImplementedError(phases)
        k.build_block()
    return k


def _make_masters(rel_bias):
    q = np.arange(128)[None, :]
    kk = np.arange(128)[:, None]
    i0 = q - kk + 128
    i1 = np.minimum(128 + q - kk, 128) + 128
    G0 = np.ascontiguousarray(rel_bias[i0].transpose(0, 2, 1))
    G1 = np.ascontiguousarray(rel_bias[i1].transpose(0, 2, 1))
    rbf = np.ascontiguousarray(np.broadcast_to(rel_bias[256], (128, 16)))
    return G0, G1, rbf


def kernel(**inputs):
    ins = {n: np.ascontiguousarray(np.asarray(v)) for n, v in inputs.items()}
    x = ins.pop("L0_attn_hidden_states")
    shards = [np.ascontiguousarray(x[2 * i:2 * i + 2].reshape(NTOK, D))
              for i in range(8)]
    G0, G1, rbf = _make_masters(ins["L1_attn_rel_bias"])
    common = dict(ins)
    del common["L1_attn_rel_bias"]
    common["L0_ffn_conv_w"] = np.ascontiguousarray(
        ins["L0_ffn_conv_w"].reshape(INTER, 3))
    common["L1_attn_G0"] = G0
    common["L1_attn_G1"] = G1
    common["L1_attn_rbf"] = rbf
    k = build_program()
    in_maps = [dict(common, x0=s) for s in shards]
    res = run_bass_kernel_spmd(k.nc, in_maps, core_ids=list(range(8)))
    outs = [res.results[i]["out"] for i in range(8)]
    return np.concatenate(outs, axis=0).reshape(16, 2048, D)
